# version 04
# baseline (speedup 1.0000x reference)
import math
from contextlib import ExitStack

import numpy as np
import ml_dtypes

import concourse.bass as bass
import concourse.mybir as mybir
from concourse.bass_utils import run_bass_kernel_spmd

F32 = mybir.dt.float32
BF16 = mybir.dt.bfloat16
I32 = mybir.dt.int32
AF = mybir.ActivationFunctionType
ALU = mybir.AluOpType
AX = mybir.AxisListType

D = 4096
SEQ = 2048
NT = 16
EPS = 1e-5
NWORDS = 53200
P1_TILES = NT

ENGINES = ("sync", "scalar", "vector", "tensor", "gpsimd")



_CONST_CACHE = {}


def _host_consts():
    if _CONST_CACHE:
        return _CONST_CACHE
    L = D
    lv = np.arange(L, dtype=np.float64)
    om = np.exp(-2j * np.pi * (lv / L))
    g0 = (1.0 - om) / (1.0 + om)
    cc = 2.0 / (1.0 + om)
    G0b_re = np.broadcast_to(g0.real.astype(np.float32), (64, L)).copy()
    G0b_im = np.broadcast_to(g0.imag.astype(np.float32), (64, L)).copy()
    ccons = np.concatenate([
        cc.real.astype(np.float32).reshape(32, 128).T,
        cc.imag.astype(np.float32).reshape(32, 128).T], axis=1).copy()
    cidx = np.arange(32)
    F1 = np.exp(2j * np.pi * np.outer(cidx, cidx) / 32.0) / L
    fcons = np.ascontiguousarray(
        np.concatenate([F1.real, F1.imag, -F1.imag], axis=1).astype(np.float32))
    p_ = np.arange(128)
    TW = np.exp(2j * np.pi * np.outer(p_, cidx) / L)
    twcons = np.ascontiguousarray(
        np.concatenate([TW.real, TW.imag], axis=1).astype(np.float32))
    G2 = np.exp(2j * np.pi * np.outer(p_, p_) / 128.0)
    g2cons = np.ascontiguousarray(
        np.concatenate([G2.real, -G2.imag], axis=1).astype(np.float32))

    identb = np.eye(128, dtype=ml_dtypes.bfloat16)
    identf = np.eye(128, dtype=np.float32)
    onesrow = np.ones((1, 128), dtype=np.float32)
    e8 = np.zeros((128, 16), dtype=np.float32)
    for p in range(128):
        e8[p, p // 32] = 1.0 / math.sqrt(32.0)
        e8[p, 8 + 4 + p // 32] = 1.0 / math.sqrt(32.0)
    e8 = e8.astype(ml_dtypes.bfloat16)
    e16f = np.zeros((8, 128), dtype=np.float32)
    for er in range(128):
        e16f[er // 16, er] = 1.0

    _CONST_CACHE.update(dict(
        g0bre=G0b_re, g0bim=G0b_im, ccons=ccons, fcons=fcons, twcons=twcons,
        g2cons=g2cons, identb=identb, identf=identf, onesrow=onesrow,
        e8=e8, e16f=e16f,
    ))
    return _CONST_CACHE



class Sem:
    def __init__(self, ctx, nc, name):
        self.h = ctx.enter_context(nc.semaphore(name))
        self.v = 0


class Prog:

    def __init__(self, nc, ctx):
        self.nc = nc
        self.ctx = ctx
        self.q = {k: [] for k in ENGINES}
        self.last = {k: None for k in ENGINES}
        self.auto = {}
        self.waited = {k: {} for k in ENGINES}

    def sem(self, name):
        return Sem(self.ctx, self.nc, name)

    def wait(self, eng, sem, val):
        if val <= 0:
            return
        w = self.waited[eng]
        if w.get(id(sem), 0) >= val:
            return
        w[id(sem)] = val

        def run(e, s=sem.h, v=val):
            e.wait_ge(s, v)
        self.q[eng].append(run)
        self.last[eng] = None

    def op(self, eng, meth, *args, inc=None, incn=1, **kw):
        aser = self.auto.get(eng)
        if aser is not None:
            self.wait(eng, aser, aser.v)
            if meth not in ("dma_start", "sem_inc"):
                if inc is None:
                    inc = aser
                assert inc is aser, \
                    f"op({eng}) inc must be the engine self-sem"
        cell = {"t": None}
        if inc is not None:
            inc.v += incn
            cell["t"] = (inc.h, incn)

        def run(e, m=meth, a=args, k=kw, c=cell):
            ins = getattr(e, m)(*a, **k)
            if c["t"] is not None:
                ins.then_inc(*c["t"])
        self.q[eng].append(run)
        self.last[eng] = (cell, inc)

    def dma(self, eng, out, in_, inc, incn=16, **kw):
        self.op(eng, "dma_start", out=out, in_=in_, inc=inc, incn=incn, **kw)

    def chain(self, eng, sem):
        last = self.last[eng]
        assert last is not None, f"chain({eng}) with no preceding op"
        cell, inc = last
        if cell["t"] is None:
            sem.v += 1
            cell["t"] = (sem.h, 1)
        else:
            assert inc is sem, f"chain({eng}) after op with different inc"
        self.wait(eng, sem, sem.v)


class Arena:

    def __init__(self, ap, nwords):
        self.ap = ap
        self.n = nwords
        self.ptr = 0
        self.marks = []
        self.hi = 0

    def f32(self, parts, words):
        off = self.ptr
        self.ptr += words
        self.hi = max(self.hi, self.ptr)
        assert self.ptr <= self.n, f"arena overflow {self.ptr} > {self.n}"
        return self.ap[0:parts, off:off + words]

    def bf16(self, parts, n):
        w = (n + 1) // 2
        return self.f32(128, w).bitcast(BF16)[0:parts, 0:n]

    def i32(self, parts, n):
        return self.f32(128, n).bitcast(I32)[0:parts, 0:n]

    def mark(self):
        self.marks.append(self.ptr)

    def release(self):
        self.ptr = self.marks.pop()



def build(upto=5, dev=False):
    nc = bass.Bass("TRN2", target_bir_lowering=False, debug=False)

    def din(name, shape, dtype=F32):
        return nc.dram_tensor(name, list(shape), dtype,
                              kind="ExternalInput").ap()

    x0_d = din("x0", (SEQ, D))
    lam_d = din("lam", (64, 2))
    pri_d = din("pri", (64, 2))
    qri_d = din("qri", (64, 2))
    ctri_d = din("ctri", (64, 2))
    bp_d = din("bp", (64, 1))
    dv_d = din("dv", (1,))
    ls_d = din("ls", (1,))
    vcre_d = din("vcre", (64, 64))
    vcim_d = din("vcim", (64, 64))
    wbig_d = din("wbig0T", (32, 128, 4736))
    b0fm_d = din("b0fm", (128, 32))
    bmflat_d = din("bmflat", (128, D))
    wqkv_d = din("wqkvT", (32, 128, 1536))
    bqkv_d = din("bqkv", (128, 12))
    wo_d = din("wo1T", (4, 128, D))
    wob_d = din("wob", (1, D))
    wq1_d = din("wq1", (32, 128, D), I32)
    mask1_d = din("mask1", (32, 128, D), I32)
    scales_d = din("scales1", (32, 128, 32))
    zeros_d = din("zeros1", (32, 128, 32))
    scale2_d = din("scale2", (32, 128, 1))
    b1fm_d = din("b1fm", (128, 32))
    g0bre_d = din("g0bre", (64, D))
    g0bim_d = din("g0bim", (64, D))
    ccons_d = din("ccons", (128, 64))
    fcons_d = din("fcons", (32, 96))
    twcons_d = din("twcons", (128, 64))
    g2cons_d = din("g2cons", (128, 256))
    identb_d = din("identb", (128, 128), BF16)
    identf_d = din("identf", (128, 128))
    ones_d = din("onesrow", (1, 128))
    e8_d = din("e8", (128, 16), BF16)
    e16_d = din("e16f", (8, 128))

    out_d = nc.dram_tensor("out", [SEQ, D], F32, kind="ExternalOutput").ap()

    skind = "ExternalOutput" if dev else "Internal"
    x1s_d = nc.dram_tensor("x1s", [SEQ, D], BF16, kind=skind).ap()
    x2s_d = nc.dram_tensor("x2s", [SEQ, D], BF16, kind=skind).ap()
    x3s_d = nc.dram_tensor("x3s", [SEQ, D], BF16, kind=skind).ap()
    kpad_d = nc.dram_tensor("kpad", [4992], BF16, kind=skind).ap()
    wdeqT_d = nc.dram_tensor("wdeqT", [32, 128, 32, 128], BF16,
                             kind=skind).ap()

    ctx = ExitStack()
    with ctx:
        e_ = ctx.enter_context
        P = Prog(nc, ctx)

        arena_t = e_(nc.sbuf_tensor("arena", [128, NWORDS], F32))
        A = Arena(arena_t[:, :], NWORDS)

        pp = [e_(nc.psum_tensor("pp%d" % i, [128, 1024], F32))
              for i in range(4)]

        def ppf(i):
            return pp[i][:, :]

        def ppb(i):
            return pp[i][:].bitcast(BF16)

        sx = [P.sem("sx0"), P.sem("sx1")]
        sy = [P.sem("sy0"), P.sem("sy1")]
        sm = P.sem("sm")
        sw = [P.sem("sw0"), P.sem("sw1")]
        s_wb = P.sem("s_wb")
        sst = [P.sem("sst0"), P.sem("sst1")]
        s_t = P.sem("s_t")
        s_v = P.sem("s_v")
        s_s = P.sem("s_s")
        s_g = P.sem("s_g")
        sb = P.sem("sb")
        P.auto = {"vector": s_v, "scalar": s_s}

        identb = A.bf16(128, 128)
        identf = A.f32(128, 128)
        onesrow = A.f32(1, 128)
        e8 = A.bf16(128, 16)
        e16f = A.f32(8, 128)
        stats = A.f32(128, 16)
        bnbuf = A.f32(128, 48)
        zcol = A.f32(128, 1)
        dcol = A.f32(128, 1)
        b0fm = A.f32(128, 32)
        b1fm = A.f32(128, 32)
        bqkv = A.f32(128, 12)

        P.dma("sync", identb[:, :], identb_d[:, :], sm)
        P.dma("sync", identf[:, :], identf_d[:, :], sm)
        P.dma("sync", onesrow[:, :], ones_d[:, :], sm)
        P.dma("sync", e8[:, :], e8_d[:, :], sm)
        P.dma("sync", e16f[:, :], e16_d[:, :], sm)
        P.dma("sync", b0fm[:, :], b0fm_d[:, :], sm)
        P.dma("sync", b1fm[:, :], b1fm_d[:, :], sm)
        P.dma("sync", bqkv[:, :], bqkv_d[:, :], sm)
        P.op("vector", "memset", zcol[:, :], 0.0, inc=s_v)

        MEAN, VAR, VAREP, SQ, RSTD, NEGMX, SSUM, RSUM = range(8)

        def emit_ln_stats(x_ap):
            for c in range(8):
                P.op("vector", "bn_stats", bnbuf[:, 6 * c:6 * c + 6],
                     x_ap[:, c * 512:(c + 1) * 512])
            P.chain("vector", s_v)
            P.op("vector", "bn_aggr", stats[:, MEAN:MEAN + 2], bnbuf[:, 0:48])
            P.chain("vector", s_v)
            P.op("vector", "tensor_scalar_add", stats[:, VAREP:VAREP + 1],
                 stats[:, VAR:VAR + 1], EPS, inc=s_v)
            P.wait("scalar", s_v, s_v.v)
            P.op("scalar", "activation", stats[:, SQ:SQ + 1],
                 stats[:, VAREP:VAREP + 1], AF.Sqrt, bias=zcol[:, 0:1],
                 inc=s_s)
            P.wait("vector", s_s, s_s.v)
            P.op("vector", "reciprocal", stats[:, RSTD:RSTD + 1],
                 stats[:, SQ:SQ + 1])
            P.chain("vector", s_v)

        def emit_barrier():
            P.wait("sync", sx[0], sx[0].v)
            P.wait("sync", sx[1], sx[1].v)
            P.wait("sync", sm, sm.v)
            P.wait("scalar", sst[0], sst[0].v)
            P.wait("scalar", sst[1], sst[1].v)
            P.wait("gpsimd", sw[0], sw[0].v)
            P.wait("gpsimd", sw[1], sw[1].v)
            P.wait("gpsimd", s_wb, s_wb.v)
            P.wait("sync", sy[0], sy[0].v)
            P.wait("sync", sy[1], sy[1].v)
            P.wait("tensor", s_t, s_t.v)
            for eng in ("tensor", "vector", "scalar", "gpsimd"):
                P.op(eng, "sem_inc", sb.h, 1)
                sb.v += 1
            for eng in ENGINES:
                P.wait(eng, sb, sb.v)

        def finish():
            P.wait("scalar", sst[0], sst[0].v)
            P.wait("scalar", sst[1], sst[1].v)
            P.wait("gpsimd", sw[0], sw[0].v)
            P.wait("gpsimd", sw[1], sw[1].v)
            P.wait("gpsimd", s_wb, s_wb.v)
            P.wait("sync", sx[0], sx[0].v)
            P.wait("sync", sx[1], sx[1].v)
            P.wait("sync", sy[0], sy[0].v)
            P.wait("sync", sy[1], sy[1].v)
            P.wait("sync", sm, sm.v)
            _emit_block(nc, P)
            return nc

        A.mark()
        w3 = A.bf16(128, 4480)

        A.mark()
        g0re = A.f32(64, D)
        g0im = A.f32(64, D)
        dre = A.f32(64, D)
        dim = A.f32(64, D)
        m1 = A.f32(64, D)
        m2 = A.f32(64, D)
        c2 = A.f32(64, D)
        rcp = A.f32(64, D)
        rec_re = A.f32(64, D)
        rec_im = A.f32(64, D)
        smallp = A.f32(64, 16)
        vc = A.f32(64, 128)
        vcT = A.f32(64, 128)
        v8 = A.f32(64, 12)
        ccons = A.f32(128, 64)
        fcons = A.f32(32, 96)
        twcons = A.f32(128, 64)
        g2cons = A.f32(128, 256)
        kt = A.f32(128, 256)
        at2 = A.f32(128, 64)
        atT = A.f32(32, 256)
        b2 = A.f32(128, 64)
        ksb = A.f32(32, 128)
        kfinb = A.bf16(128, 32)
        zrow = A.bf16(1, 512)
        s4t = A.f32(64, 8)
        s4u = A.f32(128, 32 * 10)

        def U(j):
            return s4u[:, j * 32:(j + 1) * 32]

        P.dma("sync", g0re[:, :], g0bre_d[:, :], sm)
        P.dma("sync", g0im[:, :], g0bim_d[:, :], sm)
        P.dma("sync", smallp[:, 0:2], lam_d[:, :], sm)
        P.dma("sync", smallp[:, 2:4], pri_d[:, :], sm)
        P.dma("sync", smallp[:, 4:6], qri_d[:, :], sm)
        P.dma("sync", smallp[:, 6:8], ctri_d[:, :], sm)
        P.dma("sync", smallp[:, 8:9], bp_d[:, :], sm)
        P.dma("sync", smallp[0:1, 9:10], ls_d.unsqueeze(0), sm)
        P.dma("sync", smallp[0:1, 10:11], dv_d.unsqueeze(0), sm)
        P.dma("sync", vc[:, 0:64], vcre_d[:, :], sm)
        P.dma("sync", vc[:, 64:128], vcim_d[:, :], sm)
        P.dma("sync", ccons[:, :], ccons_d[:, :], sm)
        P.dma("sync", fcons[:, :], fcons_d[:, :], sm)
        P.dma("sync", twcons[:, :], twcons_d[:, :], sm)
        P.dma("sync", g2cons[:, :], g2cons_d[:, :], sm)
        P.op("vector", "memset", zrow[:, :], 0.0, inc=s_v)
        zmemset_done = s_v.v

        ld0 = sm.v
        P.wait("tensor", sm, ld0)
        P.wait("vector", sm, ld0)
        P.wait("scalar", sm, ld0)

        P.op("tensor", "matmul", ppf(0)[0:64, 0:1], onesrow[0:1, 0:64],
             smallp[0:1, 9:10], start=True, stop=True)
        P.op("tensor", "matmul", ppf(0)[0:128, 512:513], onesrow[0:1, 0:128],
             smallp[0:1, 10:11], start=True, stop=True, inc=s_t)
        P.wait("scalar", s_t, s_t.v)
        P.wait("scalar", s_v, zmemset_done)
        P.op("scalar", "activation", smallp[:, 11:12], ppf(0)[0:64, 0:1],
             AF.Exp, bias=zcol[0:64, 0:1], inc=s_s)
        P.wait("vector", s_s, s_s.v)
        P.wait("vector", s_t, s_t.v)
        P.op("vector", "tensor_copy", dcol[:, 0:1], ppf(0)[0:128, 512:513])
        P.op("vector", "reciprocal", smallp[:, 12:13], smallp[:, 11:12])
        P.chain("vector", s_v)
        P.op("vector", "tensor_scalar_mul", smallp[:, 13:14],
             smallp[:, 12:13], 2.0, inc=s_v)
        tds_done = s_v.v

        P.op("tensor", "transpose", ppf(1)[0:64, 0:64], vc[0:64, 0:64],
             identf[0:64, 0:64])
        P.op("tensor", "transpose", ppf(1)[0:64, 512:576], vc[0:64, 64:128],
             identf[0:64, 0:64], inc=s_t)
        P.wait("vector", s_t, s_t.v)
        P.op("vector", "tensor_copy", vcT[:, 0:64], ppf(1)[0:64, 0:64])
        P.op("vector", "tensor_copy", vcT[:, 64:128], ppf(1)[0:64, 512:576],
             inc=s_v)
        P.wait("tensor", s_v, s_v.v)
        P.op("tensor", "matmul", ppf(2)[0:64, 0:1], vcT[0:64, 0:64],
             smallp[:, 8:9], start=True, stop=True)
        P.op("tensor", "matmul", ppf(2)[0:64, 512:513], vcT[0:64, 64:128],
             smallp[:, 8:9], start=True, stop=True, inc=s_t)
        P.wait("vector", s_t, s_t.v)
        P.op("vector", "tensor_copy", smallp[:, 14:15], ppf(2)[0:64, 0:1])
        P.op("vector", "tensor_copy", smallp[:, 15:16],
             ppf(2)[0:64, 512:513], inc=s_v)

        prs = [(6, 7, 14, 15), (6, 7, 2, 3), (4, 5, 14, 15), (4, 5, 2, 3)]
        P.chain("vector", s_v)
        for i, (ar, ai, br, bi) in enumerate(prs):
            P.op("vector", "tensor_tensor", s4t[:, i:i + 1],
                 smallp[:, ai:ai + 1], smallp[:, bi:bi + 1], ALU.mult)
            P.op("vector", "tensor_tensor", s4t[:, 4 + i:5 + i],
                 smallp[:, ai:ai + 1], smallp[:, br:br + 1], ALU.mult)
        P.chain("vector", s_v)
        for i, (ar, ai, br, bi) in enumerate(prs):
            P.op("vector", "scalar_tensor_tensor", v8[:, i:i + 1],
                 smallp[:, ar:ar + 1], smallp[:, br:br + 1],
                 s4t[:, i:i + 1], ALU.mult, ALU.add)
            P.op("vector", "scalar_tensor_tensor", v8[:, 4 + i:5 + i],
                 smallp[:, ar:ar + 1], smallp[:, bi:bi + 1],
                 s4t[:, 4 + i:5 + i], ALU.mult, ALU.subtract)
        P.chain("vector", s_v)
        P.op("vector", "tensor_scalar_mul", v8[:, 8:12], v8[:, 4:8], -1.0)

        P.wait("vector", s_v, tds_done)
        P.op("vector", "tensor_scalar", dre[:, :], g0re[:, :],
             smallp[:, 13:14], smallp[:, 0:1], ALU.mult, ALU.subtract)
        P.op("vector", "tensor_scalar", dim[:, :], g0im[:, :],
             smallp[:, 13:14], smallp[:, 1:2], ALU.mult, ALU.subtract)
        P.op("vector", "tensor_tensor", m1[:, :], dre[:, :], dre[:, :],
             ALU.mult)
        P.op("vector", "tensor_tensor", m2[:, :], dim[:, :], dim[:, :],
             ALU.mult)
        P.chain("vector", s_v)
        P.op("vector", "tensor_tensor", c2[:, :], m1[:, :], m2[:, :], ALU.add)
        P.chain("vector", s_v)
        P.op("vector", "reciprocal", rcp[:, :], c2[:, :])
        P.chain("vector", s_v)
        P.op("vector", "tensor_tensor", rec_re[:, :], dre[:, :], rcp[:, :],
             ALU.mult)
        P.op("vector", "scalar_tensor_tensor", rec_im[:, :], dim[:, :], -1.0,
             rcp[:, :], ALU.mult, ALU.mult, inc=s_v)
        rec_done = s_v.v

        P.wait("tensor", s_v, rec_done)
        evac_v = {}
        for c in range(32):
            bank = ppf(1 + (c % 2))
            if c >= 2:
                P.wait("tensor", s_v, evac_v[c - 2])
            P.op("tensor", "matmul", bank[0:128, 0:4],
                 rec_re[:, c * 128:(c + 1) * 128], v8[:, 0:4],
                 start=True, stop=False)
            P.op("tensor", "matmul", bank[0:128, 0:4],
                 rec_im[:, c * 128:(c + 1) * 128], v8[:, 8:12],
                 start=False, stop=True)
            P.op("tensor", "matmul", bank[0:128, 4:8],
                 rec_im[:, c * 128:(c + 1) * 128], v8[:, 0:4],
                 start=True, stop=False)
            P.op("tensor", "matmul", bank[0:128, 4:8],
                 rec_re[:, c * 128:(c + 1) * 128], v8[:, 4:8],
                 start=False, stop=True, inc=s_t)
            P.wait("vector", s_t, s_t.v)
            P.op("vector", "tensor_copy", kt[:, c::32],
                 bank[0:128, 0:8], inc=s_v)
            evac_v[c] = s_v.v

        k00re, k01re, k10re, k11re = (kt[:, 0:32], kt[:, 32:64],
                                      kt[:, 64:96], kt[:, 96:128])
        k00im, k01im, k10im, k11im = (kt[:, 128:160], kt[:, 160:192],
                                      kt[:, 192:224], kt[:, 224:256])
        P.chain("vector", s_v)
        P.op("vector", "tensor_scalar_add", U(9), k11re, 1.0)
        P.op("vector", "tensor_tensor", U(0), k01re, k10re, ALU.mult)
        P.op("vector", "tensor_tensor", U(1), k01im, k10im, ALU.mult)
        P.op("vector", "tensor_tensor", U(2), k01re, k10im, ALU.mult)
        P.op("vector", "tensor_tensor", U(3), k01im, k10re, ALU.mult)
        P.chain("vector", s_v)
        P.op("vector", "tensor_tensor", U(4), U(0), U(1), ALU.subtract)
        P.op("vector", "tensor_tensor", U(5), U(2), U(3), ALU.add)
        P.op("vector", "tensor_tensor", U(0), U(9), U(9), ALU.mult)
        P.op("vector", "tensor_tensor", U(1), k11im, k11im, ALU.mult)
        P.chain("vector", s_v)
        P.op("vector", "tensor_tensor", U(2), U(0), U(1), ALU.add)
        P.chain("vector", s_v)
        P.op("vector", "reciprocal", U(3), U(2))
        P.op("vector", "tensor_tensor", U(6), U(4), U(9), ALU.mult)
        P.op("vector", "tensor_tensor", U(7), U(5), k11im, ALU.mult)
        P.op("vector", "tensor_tensor", U(8), U(5), U(9), ALU.mult)
        P.op("vector", "tensor_tensor", U(9), U(4), k11im, ALU.mult)
        P.chain("vector", s_v)
        P.op("vector", "tensor_tensor", U(6), U(6), U(7), ALU.add)
        P.op("vector", "tensor_tensor", U(8), U(8), U(9), ALU.subtract)
        P.chain("vector", s_v)
        P.op("vector", "tensor_tensor", U(6), U(6), U(3), ALU.mult)
        P.op("vector", "tensor_tensor", U(8), U(8), U(3), ALU.mult)
        P.chain("vector", s_v)
        P.op("vector", "tensor_tensor", U(0), k00re, U(6), ALU.subtract)
        P.op("vector", "tensor_tensor", U(1), k00im, U(8), ALU.subtract)
        P.chain("vector", s_v)
        P.op("vector", "tensor_tensor", U(2), ccons[:, 0:32], U(0), ALU.mult)
        P.op("vector", "tensor_tensor", U(3), ccons[:, 32:64], U(1), ALU.mult)
        P.op("vector", "tensor_tensor", U(4), ccons[:, 0:32], U(1), ALU.mult)
        P.op("vector", "tensor_tensor", U(5), ccons[:, 32:64], U(0), ALU.mult)
        P.chain("vector", s_v)
        P.op("vector", "tensor_tensor", at2[:, 0:32], U(2), U(3),
             ALU.subtract)
        P.op("vector", "tensor_tensor", at2[:, 32:64], U(4), U(5), ALU.add,
             inc=s_v)

        P.wait("tensor", s_v, s_v.v)
        P.op("tensor", "transpose", ppf(0)[0:32, 0:128], at2[:, 0:32],
             identf[:, :])
        P.op("tensor", "transpose", ppf(0)[0:32, 512:640], at2[:, 32:64],
             identf[:, :], inc=s_t)
        P.wait("vector", s_t, s_t.v)
        P.op("vector", "tensor_copy", atT[:, 0:128], ppf(0)[0:32, 0:128])
        P.op("vector", "tensor_copy", atT[:, 128:256], ppf(0)[0:32, 512:640],
             inc=s_v)
        P.wait("tensor", s_v, s_v.v)
        P.op("tensor", "matmul", ppf(1)[0:128, 0:32], atT[0:32, 0:128],
             fcons[:, 0:32], start=True, stop=False)
        P.op("tensor", "matmul", ppf(1)[0:128, 0:32], atT[0:32, 128:256],
             fcons[:, 64:96], start=False, stop=True)
        P.op("tensor", "matmul", ppf(1)[0:128, 32:64], atT[0:32, 0:128],
             fcons[:, 32:64], start=True, stop=False)
        P.op("tensor", "matmul", ppf(1)[0:128, 32:64], atT[0:32, 128:256],
             fcons[:, 0:32], start=False, stop=True, inc=s_t)
        P.wait("vector", s_t, s_t.v)
        P.op("vector", "tensor_tensor", U(0), ppf(1)[0:128, 0:32],
             twcons[:, 0:32], ALU.mult)
        P.op("vector", "tensor_tensor", U(1), ppf(1)[0:128, 32:64],
             twcons[:, 32:64], ALU.mult)
        P.op("vector", "tensor_tensor", U(2), ppf(1)[0:128, 0:32],
             twcons[:, 32:64], ALU.mult)
        P.op("vector", "tensor_tensor", U(3), ppf(1)[0:128, 32:64],
             twcons[:, 0:32], ALU.mult)
        P.chain("vector", s_v)
        P.op("vector", "tensor_tensor", b2[:, 0:32], U(0), U(1), ALU.subtract)
        P.op("vector", "tensor_tensor", b2[:, 32:64], U(2), U(3), ALU.add,
             inc=s_v)
        P.wait("tensor", s_v, s_v.v)
        P.op("tensor", "matmul", ppf(2)[0:32, 0:128], b2[:, 0:32],
             g2cons[:, 0:128], start=True, stop=False)
        P.op("tensor", "matmul", ppf(2)[0:32, 0:128], b2[:, 32:64],
             g2cons[:, 128:256], start=False, stop=True, inc=s_t)
        P.wait("vector", s_t, s_t.v)
        P.op("vector", "tensor_copy", ksb[:, :], ppf(2)[0:32, 0:128],
             inc=s_v)
        P.wait("tensor", s_v, s_v.v)
        P.op("tensor", "transpose", ppf(3)[0:128, 0:32], ksb[0:32, :],
             identf[0:32, 0:32], inc=s_t)
        P.wait("vector", s_t, s_t.v)
        P.op("vector", "tensor_copy", kfinb[:, :], ppf(3)[0:128, 0:32],
             inc=s_v)

        if dev:
            dbg_d = nc.dram_tensor("dbg", [128, 3072], F32,
                                   kind="ExternalOutput").ap()
            P.wait("scalar", s_v, s_v.v)
            P.dma("scalar", dbg_d[:, 0:256], kt[:, :], sst[1])
            P.dma("scalar", dbg_d[:, 256:320], at2[:, :], sst[1])
            P.dma("scalar", dbg_d[0:64, 320:332], v8[:, :], sst[1])
            P.dma("scalar", dbg_d[0:64, 332:348], smallp[:, :], sst[1])
            P.dma("scalar", dbg_d[0:64, 348:476], vcT[:, :], sst[1])
            P.dma("scalar", dbg_d[0:32, 512:768], atT[:, :], sst[1])
            P.dma("scalar", dbg_d[0:32, 768:896], ksb[:, :], sst[1])
            P.dma("scalar", dbg_d[:, 896:960], b2[:, :], sst[1])
            P.dma("scalar", dbg_d[0:64, 1024:2048], rec_re[:, 0:1024],
                  sst[1])
            P.dma("scalar", dbg_d[0:64, 2048:3072], rec_im[:, 0:1024],
                  sst[1])

        P.wait("scalar", s_v, s_v.v)
        P.dma("scalar", kpad_d[0:511].unsqueeze(0), zrow[0:1, 0:511], sst[0])
        P.dma("scalar", kpad_d[4607:4992].unsqueeze(0), zrow[0:1, 0:385],
              sst[0])
        P.dma("scalar", kpad_d[511:4607], kfinb[:, :], sst[0])
        P.wait("sync", sst[0], sst[0].v)
        for r in range(128):
            P.dma("sync", w3[r:r + 1, :],
                  kpad_d[511 - r:511 - r + 4480].unsqueeze(0), sm)
        w3_loaded = sm.v

        A.release()
        emit_barrier()
        if upto < 1:
            return finish()

        A.mark()
        xl1 = [A.f32(128, D), A.f32(128, D)]
        xo1 = [A.bf16(128, D), A.bf16(128, D)]
        u_bf = A.bf16(128, D)
        uT = A.bf16(128, D)
        z_bf = A.bf16(128, D)
        tmp1 = A.bf16(128, D)

        P.wait("tensor", sm, w3_loaded)
        xfree = [None, None]
        ofree = [None, None]
        for i in range(P1_TILES):
            par = i % 2
            if xfree[par] is not None:
                P.wait("sync", xfree[par][0], xfree[par][1])
            P.dma("sync", xl1[par][:, :], x0_d[i * 128:(i + 1) * 128, :],
                  sx[par])
            xload = sx[par].v
            P.wait("vector", sx[par], xload)
            emit_ln_stats(xl1[par])
            P.op("vector", "tensor_scalar", u_bf[:, :], xl1[par][:, :],
                 stats[:, MEAN:MEAN + 1], stats[:, RSTD:RSTD + 1],
                 ALU.subtract, ALU.mult, inc=s_v)
            u_done = s_v.v
            P.wait("tensor", s_v, u_done)
            for half in range(2):
                for j in range(16):
                    dd = half * 16 + j
                    P.op("tensor", "transpose",
                         ppb(half)[:, j * 128:(j + 1) * 128],
                         u_bf[:, dd * 128:(dd + 1) * 128], identb[:, :],
                         inc=(s_t if j == 15 else None))
                P.wait("vector", s_t, s_t.v)
                P.op("vector", "tensor_copy",
                     uT[:, half * 2048:(half + 1) * 2048], ppb(half)[:, :],
                     inc=s_v)
            P.wait("tensor", s_v, s_v.v)
            stop_cnt = {}
            for j0 in range(32):
                for c in range(j0 // 4, 8):
                    b = (c + 2) % 8
                    bank = ppf(b // 2)[:, (b % 2) * 512:(b % 2 + 1) * 512]
                    last = (j0 == 4 * c + 3)
                    off = 384 + 512 * c - 128 * j0
                    P.op("tensor", "matmul", bank,
                         uT[:, j0 * 128:(j0 + 1) * 128],
                         w3[:, off:off + 512],
                         start=(j0 == 0), stop=last,
                         inc=(s_t if last else None))
                    if last:
                        stop_cnt[c] = s_t.v
            for c in range(8):
                b = (c + 2) % 8
                bank = ppf(b // 2)[:, (b % 2) * 512:(b % 2 + 1) * 512]
                P.wait("vector", s_t, stop_cnt[c])
                P.op("vector", "scalar_tensor_tensor",
                     z_bf[:, c * 512:(c + 1) * 512],
                     u_bf[:, c * 512:(c + 1) * 512], dcol[:, 0:1], bank,
                     ALU.mult, ALU.add, inc=(s_v if c == 7 else None))
            epi_done = s_v.v
            P.chain("vector", s_v)
            emit_ln_stats(z_bf)
            P.op("vector", "tensor_scalar", tmp1[:, :], z_bf[:, :],
                 stats[:, MEAN:MEAN + 1], stats[:, RSTD:RSTD + 1],
                 ALU.subtract, ALU.mult)
            P.chain("vector", s_v)
            if ofree[par] is not None:
                P.wait("vector", ofree[par][0], ofree[par][1])
            P.op("vector", "tensor_add", xo1[par][:, :], xl1[par][:, :],
                 tmp1[:, :], inc=s_v)
            xfree[par] = (s_v, s_v.v)
            P.wait("scalar", s_v, s_v.v)
            P.dma("scalar", x1s_d[i * 128:(i + 1) * 128, :], xo1[par][:, :],
                  sst[par])
            ofree[par] = (sst[par], sst[par].v)
            P.wait("tensor", s_v, epi_done)

        if dev:
            dbg1_d = nc.dram_tensor("dbg1", [128, 3 * D], BF16,
                                    kind="ExternalOutput").ap()
            P.wait("scalar", s_v, s_v.v)
            P.dma("scalar", dbg1_d[:, 0:D], u_bf[:, :], sst[1])
            P.dma("scalar", dbg1_d[:, D:2 * D], uT[:, :], sst[1])
            P.dma("scalar", dbg1_d[:, 2 * D:3 * D], z_bf[:, :], sst[1])

        A.release()
        A.release()
        emit_barrier()
        if upto < 2:
            return finish()

        def emit_prep_half(xsrc_d, h, xl, tbuf, tT_v, x_free):
            P.wait("vector", s_t, s_t.v)
            for k in range(8):
                i = h * 8 + k
                par = k % 2
                if x_free[par] is not None:
                    P.wait("sync", x_free[par][0], x_free[par][1])
                P.dma("sync", xl[par][:, :],
                      xsrc_d[i * 128:(i + 1) * 128, :], sx[par])
                xload = sx[par].v
                P.wait("vector", sx[par], xload)
                emit_ln_stats(xl[par])
                if k > 0:
                    P.wait("vector", s_t, tp_done)
                P.op("vector", "tensor_scalar", tbuf[:, :], xl[par][:, :],
                     stats[:, MEAN:MEAN + 1], stats[:, RSTD:RSTD + 1],
                     ALU.subtract, ALU.mult, inc=s_v)
                x_free[par] = (s_v, s_v.v)
                P.wait("tensor", s_v, s_v.v)
                for half in range(2):
                    for j in range(16):
                        dd = half * 16 + j
                        P.op("tensor", "transpose",
                             ppb(half)[:, j * 128:(j + 1) * 128],
                             tbuf[:, dd * 128:(dd + 1) * 128], identb[:, :],
                             inc=(s_t if j == 15 else None))
                    tp_done = s_t.v
                    P.wait("vector", s_t, tp_done)
                    P.op("vector", "tensor_copy",
                         tT_v[:, half * 16:(half + 1) * 16,
                              k * 128:(k + 1) * 128],
                         ppb(half)[:, :].rearrange("p (a b) -> p a b", a=16),
                         inc=s_v)

        def emit_fm_to_tm(fv, k, f_tm, src_ready_sem, src_ready_val):
            P.wait("tensor", src_ready_sem, src_ready_val)
            P.wait("tensor", s_v, s_v.v)
            for half in range(2):
                for j in range(16):
                    oc = half * 16 + j
                    P.op("tensor", "transpose",
                         ppb(half)[:, j * 128:(j + 1) * 128],
                         fv[:, oc, k * 128:(k + 1) * 128], identb[:, :],
                         inc=(s_t if j == 15 else None))
                P.wait("vector", s_t, s_t.v)
                P.op("vector", "tensor_copy",
                     f_tm[:, half * 2048:(half + 1) * 2048], ppb(half)[:, :],
                     inc=s_v)

        def emit_boundary_tile(i, f_tm, tmpb, xsrc_d, xdst_d, xl, xo,
                               x_free, o_free):
            par = i % 2
            if x_free[par] is not None:
                P.wait("sync", x_free[par][0], x_free[par][1])
            P.dma("sync", xl[par][:, :], xsrc_d[i * 128:(i + 1) * 128, :],
                  sx[par])
            xload = sx[par].v
            emit_ln_stats(f_tm)
            P.op("vector", "tensor_scalar", tmpb[:, :], f_tm[:, :],
                 stats[:, MEAN:MEAN + 1], stats[:, RSTD:RSTD + 1],
                 ALU.subtract, ALU.mult)
            P.chain("vector", s_v)
            P.wait("vector", sx[par], xload)
            if o_free[par] is not None:
                P.wait("vector", o_free[par][0], o_free[par][1])
            P.op("vector", "tensor_add", xo[par][:, :], xl[par][:, :],
                 tmpb[:, :], inc=s_v)
            x_free[par] = (s_v, s_v.v)
            P.wait("scalar", s_v, s_v.v)
            P.dma("scalar", xdst_d[i * 128:(i + 1) * 128, :], xo[par][:, :],
                  sst[par])
            o_free[par] = (sst[par], sst[par].v)

        A.mark()
        t1T = A.bf16(128, 32 * 1024)
        t1T_v = t1T.rearrange("p (d t) -> p d t", d=32)
        fbuf = A.bf16(128, 32 * 1024)
        fbuf_v = fbuf.rearrange("p (d t) -> p d t", d=32)
        wbuf = [A.bf16(128, 4096), A.bf16(128, 4096)]
        shared = A.bf16(128, 5120)
        hprime = A.bf16(128, 1024)
        qk = A.bf16(128, 1024)
        s8 = A.f32(8, 1024)
        wrow = s8
        stw = A.f32(128, 192)
        bmflat = A.bf16(128, D)
        xl2 = [A.bf16(128, D), A.bf16(128, D)]
        xo2 = [A.bf16(128, D), A.bf16(128, D)]
        qkh = shared

        P.dma("gpsimd", bmflat[:, :], bmflat_d[:, :], s_wb)
        bm_loaded = s_wb.v
        x_free2 = [None, None]
        o_free2 = [None, None]
        wfree = [None, None]

        def p2_scores():
            stm = stw[:, 0:64]
            wtm = stw[:, 64:128]
            wsc = stw[:, 128:192]
            for hq in range(2):
                if hq == 1:
                    P.wait("vector", s_t, s_t.v)
                P.op("vector", "tensor_tensor", qk[:, :],
                     qkh[:, hq * 1024:(hq + 1) * 1024],
                     qkh[:, (2 + hq) * 1024:(3 + hq) * 1024], ALU.mult,
                     inc=s_v)
                P.wait("tensor", s_v, s_v.v)
                for q2 in range(2):
                    P.op("tensor", "matmul",
                         ppf(0)[0:8, q2 * 512:(q2 + 1) * 512],
                         e8[:, hq * 8:(hq + 1) * 8],
                         qk[:, q2 * 512:(q2 + 1) * 512],
                         start=(hq == 0), stop=(hq == 1),
                         inc=(s_t if q2 == 1 else None))
            P.wait("vector", s_t, s_t.v)
            P.op("vector", "tensor_copy", s8[:, :], ppf(0)[0:8, :], inc=s_v)
            P.wait("tensor", s_v, s_v.v)
            for k in range(8):
                P.op("tensor", "transpose", ppf(1)[0:128, k * 8:(k + 1) * 8],
                     s8[0:8, k * 128:(k + 1) * 128], identf[0:8, 0:8],
                     inc=(s_t if k == 7 else None))
            P.wait("vector", s_t, s_t.v)
            P.op("vector", "tensor_copy", stm[:, :], ppf(1)[0:128, 0:64],
                 inc=s_v)
            for k in range(8):
                P.chain("vector", s_v)
                P.op("vector", "tensor_reduce", stats[:, NEGMX:NEGMX + 1],
                     stm[:, k * 8:(k + 1) * 8], AX.X, ALU.max, negate=True,
                     inc=s_v)
                P.wait("scalar", s_v, s_v.v)
                P.op("scalar", "activation", wtm[:, k * 8:(k + 1) * 8],
                     stm[:, k * 8:(k + 1) * 8], AF.Exp,
                     bias=stats[:, NEGMX:NEGMX + 1],
                     accum_out=stats[:, SSUM:SSUM + 1], inc=s_s)
                P.wait("vector", s_s, s_s.v)
                P.op("vector", "reciprocal", stats[:, RSUM:RSUM + 1],
                     stats[:, SSUM:SSUM + 1])
                P.chain("vector", s_v)
                P.op("vector", "tensor_scalar_mul", wsc[:, k * 8:(k + 1) * 8],
                     wtm[:, k * 8:(k + 1) * 8], stats[:, RSUM:RSUM + 1],
                     inc=s_v)
            P.wait("tensor", s_v, s_v.v)
            for k in range(8):
                P.op("tensor", "transpose",
                     ppf(0)[0:8, k * 128:(k + 1) * 128],
                     wsc[:, k * 8:(k + 1) * 8], identf[:, :],
                     inc=(s_t if k == 7 else None))
            P.wait("vector", s_t, s_t.v)
            P.op("vector", "tensor_copy", wrow[:, :], ppf(0)[0:8, :],
                 inc=s_v)
            P.wait("tensor", s_v, s_v.v)
            for q2 in range(2):
                P.op("tensor", "matmul", ppf(1)[:, q2 * 512:(q2 + 1) * 512],
                     e16f[0:8, :], wrow[0:8, q2 * 512:(q2 + 1) * 512],
                     start=True, stop=True, inc=(s_t if q2 == 1 else None))
            P.wait("vector", s_t, s_t.v)
            P.op("vector", "tensor_tensor", hprime[:, :],
                 qkh[:, 4 * 1024:5 * 1024], ppf(1)[:, :], ALU.mult, inc=s_v)
            return s_v.v

        for h in range(2):
            t1b = shared[:, 0:4096]
            emit_prep_half(x1s_d, h, xl2, t1b, t1T_v, x_free2)
            P.wait("tensor", s_v, s_v.v)
            P.wait("tensor", s_s, s_s.v)
            hprime_done = None
            mm_done = {}
            for oc in range(37):
                wpar = oc % 2
                if wfree[wpar] is not None:
                    P.wait("gpsimd", wfree[wpar][0], wfree[wpar][1])
                P.dma("gpsimd", wbuf[wpar][:, :],
                      wbig_d[:, :, oc * 128:(oc + 1) * 128].transpose(
                          [1, 0, 2]), sw[wpar])
                wload = sw[wpar].v
                pr = 2 + oc % 2
                P.wait("tensor", sw[wpar], wload)
                if oc >= 2:
                    P.wait("tensor", mm_done[oc - 2][0], mm_done[oc - 2][1])
                for d in range(32):
                    laststop = (d == 31 and oc < 5)
                    for q2 in range(2):
                        P.op("tensor", "matmul",
                             ppf(pr)[:, q2 * 512:(q2 + 1) * 512],
                             wbuf[wpar][:, d * 128:(d + 1) * 128],
                             t1T_v[:, d, q2 * 512:(q2 + 1) * 512],
                             start=(d == 0), stop=laststop,
                             inc=(s_t if laststop and q2 == 1 else None))
                if oc >= 5:
                    if oc == 5:
                        P.wait("tensor", s_v, hprime_done)
                        P.wait("tensor", s_wb, bm_loaded)
                    for q2 in range(2):
                        P.op("tensor", "matmul",
                             ppf(pr)[:, q2 * 512:(q2 + 1) * 512],
                             bmflat[:, (oc - 5) * 128:(oc - 4) * 128],
                             hprime[:, q2 * 512:(q2 + 1) * 512],
                             start=False, stop=True,
                             inc=(s_t if q2 == 1 else None))
                wfree[wpar] = (s_t, s_t.v)
                mm_t = s_t.v
                if oc < 5:
                    P.wait("vector", s_t, mm_t)
                    P.op("vector", "tensor_copy",
                         qkh[:, oc * 1024:(oc + 1) * 1024], ppf(pr)[:, :],
                         inc=s_v)
                    mm_done[oc] = (s_v, s_v.v)
                else:
                    P.wait("scalar", s_t, mm_t)
                    P.op("scalar", "activation", fbuf_v[:, oc - 5, :],
                         ppf(pr)[:, :], AF.Identity,
                         bias=b0fm[:, oc - 5:oc - 4], scale=1.0, inc=s_s)
                    mm_done[oc] = (s_s, s_s.v)
                if oc == 4:
                    P.wait("vector", mm_done[4][0], mm_done[4][1])
                    hprime_done = p2_scores()
            last_evac = mm_done[36]
            f_tm = shared[:, 0:4096]
            tmpb2 = wbuf[0]
            for k in range(8):
                emit_fm_to_tm(fbuf_v, k, f_tm, last_evac[0], last_evac[1])
                emit_boundary_tile(h * 8 + k, f_tm, tmpb2, x1s_d, x2s_d,
                                   xl2, xo2, x_free2, o_free2)
            wfree[0] = (s_v, s_v.v)

        A.release()
        emit_barrier()
        if upto < 3:
            return finish()

        A.mark()
        qkv = A.bf16(128, 12 * 2048)
        qkv_v = qkv.rearrange("p (c t) -> p c t", c=12)
        attn_cat = A.bf16(128, 4 * 2048)
        attn_v = attn_cat.rearrange("p (c t) -> p c t", c=4)
        vtm = A.bf16(128, 8 * 16 * 65)
        vtm_v = vtm.rearrange("p (h k c) -> p h k c", h=8, k=16)
        xl3 = [A.bf16(128, D), A.bf16(128, D)]
        xo3 = [A.bf16(128, D), A.bf16(128, D)]

        x_free3 = [None, None]
        o_free3 = [None, None]

        A.mark()
        t2T = A.bf16(128, 32 * 1024)
        t2T_v = t2T.rearrange("p (d t) -> p d t", d=32)
        wbuf3 = [A.bf16(128, 4096), A.bf16(128, 4096)]
        t2b = A.bf16(128, 4096)
        qkv_done = None
        for h in range(2):
            emit_prep_half(x2s_d, h, xl3, t2b, t2T_v, x_free3)
            P.wait("tensor", s_v, s_v.v)
            P.wait("tensor", s_s, s_s.v)
            mm_done = {}
            for oc in range(12):
                wpar = oc % 2
                if wfree[wpar] is not None:
                    P.wait("gpsimd", wfree[wpar][0], wfree[wpar][1])
                P.dma("gpsimd", wbuf3[wpar][:, :],
                      wqkv_d[:, :, oc * 128:(oc + 1) * 128].transpose(
                          [1, 0, 2]), sw[wpar])
                wload = sw[wpar].v
                pr = 2 + oc % 2
                P.wait("tensor", sw[wpar], wload)
                if oc >= 2:
                    P.wait("tensor", mm_done[oc - 2][0], mm_done[oc - 2][1])
                for d in range(32):
                    last = (d == 31)
                    for q2 in range(2):
                        P.op("tensor", "matmul",
                             ppf(pr)[:, q2 * 512:(q2 + 1) * 512],
                             wbuf3[wpar][:, d * 128:(d + 1) * 128],
                             t2T_v[:, d, q2 * 512:(q2 + 1) * 512],
                             start=(d == 0), stop=last,
                             inc=(s_t if last and q2 == 1 else None))
                wfree[wpar] = (s_t, s_t.v)
                P.wait("scalar", s_t, s_t.v)
                P.op("scalar", "activation",
                     qkv_v[:, oc, h * 1024:(h + 1) * 1024], ppf(pr)[:, :],
                     AF.Identity, bias=bqkv[:, oc:oc + 1], scale=1.0,
                     inc=s_s)
                mm_done[oc] = (s_s, s_s.v)
        qkv_done = s_s.v
        A.release()

        A.mark()
        PT = A.bf16(128, 16 * 2048)
        kaug = A.bf16(128, 2048)
        qaug = A.bf16(128, 2048)
        rowbuf = A.f32(1, 2048)
        rowbuf2 = A.f32(1, 2048)
        rb_s = A.bf16(64, 2048)
        mq16 = A.f32(128, 16)
        m4 = A.f32(128, 4)
        m16n = A.f32(128, 16)
        mqTb = A.bf16(16, 128)

        P.wait("vector", s_s, qkv_done)
        P.op("vector", "memset", vtm[:, :], 1.0, inc=s_v)
        P.op("vector", "memset", kaug[64:65, :], 1.0, inc=s_v)
        for h8 in range(8):
            vch = 8 + h8 // 2
            rb = (h8 % 2) * 64
            P.wait("tensor", s_v, s_v.v)
            for kt2 in range(16):
                P.op("tensor", "transpose",
                     ppb(0)[:, kt2 * 64:(kt2 + 1) * 64],
                     qkv_v[rb:rb + 64, vch, kt2 * 128:(kt2 + 1) * 128],
                     identb[rb:rb + 64, rb:rb + 64],
                     inc=(s_t if kt2 == 15 else None))
            P.wait("vector", s_t, s_t.v)
            P.op("vector", "tensor_copy", vtm_v[:, h8, :, 0:64],
                 ppb(0)[:, 0:1024].rearrange("p (k c) -> p k c", k=16),
                 inc=s_v)
        P.wait("tensor", s_v, s_v.v)

        for h8 in range(8):
            qch = h8 // 2
            kch = 4 + h8 // 2
            rb = (h8 % 2) * 64
            P.wait("vector", s_t, s_t.v)
            P.op("vector", "tensor_copy", kaug[0:64, :],
                 qkv_v[rb:rb + 64, kch, :])
            P.chain("vector", s_v)
            P.op("vector", "tensor_copy", qaug[0:64, :],
                 qkv_v[rb:rb + 64, qch, :], inc=s_v)
            kq_staged = s_v.v
            P.wait("tensor", s_v, kq_staged)
            qtred = {}
            for qt in range(16):
                pr0 = (qt % 2) * 2
                if qt >= 2:
                    P.wait("tensor", qtred[qt - 2][0], qtred[qt - 2][1])
                for c in range(4):
                    P.op("tensor", "matmul",
                         ppf(pr0 + c // 2)[:, (c % 2) * 512:(c % 2 + 1) * 512],
                         qaug[0:64, qt * 128:(qt + 1) * 128],
                         kaug[0:64, c * 512:(c + 1) * 512],
                         start=True, stop=True,
                         inc=(s_t if c == 3 else None))
                P.wait("vector", s_t, s_t.v)
                for c in range(4):
                    P.op("vector", "tensor_reduce", m4[:, c:c + 1],
                         ppf(pr0 + c // 2)[:, (c % 2) * 512:(c % 2 + 1) * 512],
                         AX.X, ALU.max)
                P.chain("vector", s_v)
                P.op("vector", "tensor_reduce", mq16[:, qt:qt + 1], m4[:, :],
                     AX.X, ALU.max, inc=s_v)
                qtred[qt] = (s_v, s_v.v)
            P.op("vector", "tensor_scalar_mul", m16n[:, :], mq16[:, :], -1.0,
                 inc=s_v)
            P.wait("tensor", s_v, s_v.v)
            P.op("tensor", "transpose", ppf(0)[0:16, 0:128], m16n[:, :],
                 identf[:, :], inc=s_t)
            P.wait("vector", s_t, s_t.v)
            P.op("vector", "tensor_copy", mqTb[:, :], ppf(0)[0:16, 0:128],
                 inc=s_v)
            P.wait("sync", s_v, s_v.v)
            P.dma("sync", qaug[64:65, :], mqTb[0:16, :], sm)
            negm_done = sm.v
            P.wait("tensor", sm, negm_done)
            exp_done = {}
            for kt2 in range(16):
                for reg in range(2):
                    ri = kt2 * 2 + reg
                    pr = ri % 2
                    if ri >= 2:
                        P.wait("tensor", exp_done[ri - 2][0],
                               exp_done[ri - 2][1])
                    for q2 in range(2):
                        P.op("tensor", "matmul",
                             ppf(pr)[:, q2 * 512:(q2 + 1) * 512],
                             kaug[0:65, kt2 * 128:(kt2 + 1) * 128],
                             qaug[0:65, reg * 1024 + q2 * 512:
                                  reg * 1024 + (q2 + 1) * 512],
                             start=True, stop=True,
                             inc=(s_t if q2 == 1 else None))
                    P.wait("scalar", s_t, s_t.v)
                    P.op("scalar", "activation",
                         PT[:, kt2 * 2048 + reg * 1024:
                            kt2 * 2048 + (reg + 1) * 1024],
                         ppf(pr)[:, :], AF.Exp,
                         bias=zcol[:, 0:1], inc=s_s)
                    exp_done[ri] = (s_s, s_s.v)
                P.wait("tensor", exp_done[kt2 * 2 + 1][0],
                       exp_done[kt2 * 2 + 1][1])
                for qs in range(4):
                    P.op("tensor", "matmul",
                         ppf(2 + qs // 2)[0:65, (qs % 2) * 512:
                                          (qs % 2 + 1) * 512],
                         vtm_v[:, h8, kt2, :],
                         PT[:, kt2 * 2048 + qs * 512:
                            kt2 * 2048 + (qs + 1) * 512],
                         start=(kt2 == 0), stop=(kt2 == 15),
                         inc=(s_t if kt2 == 15 and qs == 3 else None))
            P.wait("vector", s_t, s_t.v)
            P.op("vector", "reciprocal", rowbuf[0:1, 0:1024],
                 ppf(2)[64:65, :])
            P.op("vector", "reciprocal", rowbuf[0:1, 1024:2048],
                 ppf(3)[64:65, :])
            P.chain("vector", s_v)
            P.op("vector", "tensor_scalar_mul", rowbuf2[0:1, :],
                 rowbuf[0:1, :], 1.0 / math.sqrt(512.0), inc=s_v)
            P.wait("tensor", s_v, s_v.v)
            for q2 in range(4):
                P.op("tensor", "matmul",
                     ppf(q2 // 2)[0:64, (q2 % 2) * 512:(q2 % 2 + 1) * 512],
                     onesrow[0:1, 0:64],
                     rowbuf2[0:1, q2 * 512:(q2 + 1) * 512],
                     start=True, stop=True,
                     inc=(s_t if q2 == 3 else None))
            P.wait("vector", s_t, s_t.v)
            P.op("vector", "tensor_copy", rb_s[:, 0:1024], ppf(0)[0:64, :])
            P.op("vector", "tensor_copy", rb_s[:, 1024:2048],
                 ppf(1)[0:64, :])
            P.chain("vector", s_v)
            P.op("vector", "tensor_tensor", attn_v[rb:rb + 64, h8 // 2,
                                                   0:1024],
                 ppf(2)[0:64, :], rb_s[0:64, 0:1024], ALU.mult)
            P.op("vector", "tensor_tensor", attn_v[rb:rb + 64, h8 // 2,
                                                   1024:2048],
                 ppf(3)[0:64, :], rb_s[0:64, 1024:2048], ALU.mult, inc=s_v)
            P.wait("tensor", s_v, s_v.v)
        A.release()

        A.mark()
        woT = A.bf16(128, 4 * D)
        woT_v = woT.rearrange("p (c o) -> p c o", c=4)
        bwo = A.bf16(128, D)
        wobs = A.f32(1, D)
        f_att = A.bf16(128, D)
        tmpb3 = A.bf16(128, D)

        for qeng in ("gpsimd", "sync"):
            P.wait(qeng, s_v, s_v.v)
            P.wait(qeng, s_t, s_t.v)
            P.wait(qeng, s_s, s_s.v)
        P.dma("gpsimd", woT[:, :], wo_d[:, :, :].transpose([1, 0, 2]), s_wb)
        wo_loaded = s_wb.v
        P.dma("sync", wobs[:, :], wob_d[:, :], sm)
        P.wait("tensor", sm, sm.v)
        for q2 in range(8):
            P.op("tensor", "matmul",
                 ppf(q2 // 2)[:, (q2 % 2) * 512:(q2 % 2 + 1) * 512],
                 onesrow[0:1, :], wobs[0:1, q2 * 512:(q2 + 1) * 512],
                 start=True, stop=True,
                 inc=(s_t if q2 % 2 == 1 else None))
            if q2 % 2 == 1:
                P.wait("vector", s_t, s_t.v)
                P.op("vector", "tensor_copy",
                     bwo[:, (q2 // 2) * 1024:(q2 // 2 + 1) * 1024],
                     ppf(q2 // 2)[:, :], inc=s_v)
        P.wait("tensor", s_v, s_v.v)
        P.wait("tensor", s_wb, wo_loaded)
        for i in range(NT):
            for ec in range(4):
                for os in range(8):
                    P.op("tensor", "matmul",
                         ppf(os // 2)[:, (os % 2) * 512:(os % 2 + 1) * 512],
                         attn_v[:, ec, i * 128:(i + 1) * 128],
                         woT_v[:, ec, os * 512:(os + 1) * 512],
                         start=(ec == 0), stop=(ec == 3),
                         inc=(s_t if ec == 3 and os == 7 else None))
            P.wait("vector", s_t, s_t.v)
            for pq in range(4):
                P.op("vector", "tensor_tensor",
                     f_att[:, pq * 1024:(pq + 1) * 1024], ppf(pq)[:, :],
                     bwo[:, pq * 1024:(pq + 1) * 1024], ALU.add,
                     inc=(s_v if pq == 3 else None))
            emit_boundary_tile(i, f_att, tmpb3, x2s_d, x3s_d, xl3, xo3,
                               x_free3, o_free3)
            P.wait("tensor", s_v, s_v.v)
        A.release()
        A.release()
        emit_barrier()
        if upto < 4:
            return finish()

        A.mark()
        qb = [A.i32(128, D), A.i32(128, D)]
        mb = [A.bf16(128, D), A.bf16(128, D)]
        deq = A.bf16(128, D)
        wd = A.bf16(128, D)
        stg = [A.bf16(128, 2048), A.bf16(128, 2048)]
        scz = A.f32(128, 65)
        a1 = A.f32(128, 32)
        a2n = A.f32(128, 32)

        qfree = [None, None]
        mfree = [None, None]
        stgfree = [None, None]
        a1_free = None
        scz_free = None
        wd_free = None
        deq_free = None
        for kc in range(32):
            par = kc % 2
            if qfree[par] is not None:
                P.wait("sync", qfree[par][0], qfree[par][1])
            P.dma("sync", qb[par][:, :], wq1_d[kc, :, :], sx[par])
            qld = sx[par].v
            if scz_free is not None:
                P.wait("sync", scz_free[0], scz_free[1])
            P.dma("sync", scz[:, 0:32], scales_d[kc, :, :], sm)
            P.dma("sync", scz[:, 32:64], zeros_d[kc, :, :], sm)
            P.dma("sync", scz[:, 64:65], scale2_d[kc, :, :], sm)
            sld = sm.v
            if mfree[par] is not None:
                P.wait("gpsimd", mfree[par][0], mfree[par][1])
            P.dma("gpsimd", mb[par][:, :], mask1_d[kc, :, :], sw[par])
            mld = sw[par].v

            P.wait("vector", sm, sld)
            if a1_free is not None:
                P.wait("vector", a1_free[0], a1_free[1])
            P.op("vector", "tensor_scalar_mul", a1[:, :], scz[:, 0:32],
                 scz[:, 64:65])
            P.chain("vector", s_v)
            P.op("vector", "scalar_tensor_tensor", a2n[:, :], scz[:, 32:64],
                 -1.0, a1[:, :], ALU.mult, ALU.mult, inc=s_v)
            scz_free = (s_v, s_v.v)
            a1_done = s_v.v

            P.wait("scalar", s_v, a1_done)
            P.wait("scalar", sx[par], qld)
            if deq_free is not None:
                P.wait("scalar", deq_free[0], deq_free[1])
            for g in range(32):
                P.op("scalar", "activation", deq[:, g * 128:(g + 1) * 128],
                     qb[par][:, g * 128:(g + 1) * 128], AF.Identity,
                     scale=a1[:, g:g + 1], bias=a2n[:, g:g + 1],
                     inc=(s_s if g == 31 else None))
            qfree[par] = (s_s, s_s.v)
            a1_free = (s_s, s_s.v)

            P.wait("vector", s_s, s_s.v)
            P.wait("vector", sw[par], mld)
            if wd_free is not None:
                P.wait("vector", wd_free[0], wd_free[1])
            P.op("vector", "tensor_tensor", wd[:, :], deq[:, :],
                 mb[par][:, :], ALU.mult, inc=s_v)
            deq_free = (s_v, s_v.v)
            mfree[par] = (s_v, s_v.v)

            P.wait("tensor", s_v, s_v.v)
            for half in range(2):
                for j in range(16):
                    ncc = half * 16 + j
                    P.op("tensor", "transpose",
                         ppb(half)[:, j * 128:(j + 1) * 128],
                         wd[:, ncc * 128:(ncc + 1) * 128], identb[:, :],
                         inc=(s_t if j == 15 else None))
                P.wait("vector", s_t, s_t.v)
                if stgfree[half] is not None:
                    P.wait("vector", stgfree[half][0], stgfree[half][1])
                P.op("vector", "tensor_copy", stg[half][:, :],
                     ppb(half)[:, :], inc=s_v)
                P.wait("scalar", s_v, s_v.v)
                P.dma("scalar",
                      wdeqT_d[kc, :, half * 16:(half + 1) * 16, :],
                      stg[half][:, :].rearrange("p (a b) -> p a b", a=16),
                      sst[half])
                stgfree[half] = (sst[half], sst[half].v)
            wd_free = (s_t, s_t.v)

        A.release()
        emit_barrier()
        if upto < 5:
            return finish()

        A.mark()
        t3T = A.bf16(128, 32 * 1024)
        t3T_v = t3T.rearrange("p (d t) -> p d t", d=32)
        f1 = A.bf16(128, 32 * 1024)
        f1_v = f1.rearrange("p (d t) -> p d t", d=32)
        wt5 = [A.bf16(128, 4096), A.bf16(128, 4096)]
        shared5 = A.bf16(128, 4096)
        xl5 = [A.bf16(128, D), A.bf16(128, D)]
        xo5 = [A.f32(128, D), A.f32(128, D)]

        x_free5 = [None, None]
        o_free5 = [None, None]
        wfree5 = [None, None]
        for h in range(2):
            emit_prep_half(x3s_d, h, xl5, shared5, t3T_v, x_free5)
            P.wait("tensor", s_v, s_v.v)
            P.wait("tensor", s_s, s_s.v)
            mm_done = {}
            for kc in range(32):
                wpar = kc % 2
                if wfree5[wpar] is not None:
                    P.wait("sync", wfree5[wpar][0], wfree5[wpar][1])
                P.dma("sync", wt5[wpar][:, :], wdeqT_d[kc, :, :, :], sy[wpar])
                wload = sy[wpar].v
                pr = 2 + kc % 2
                P.wait("tensor", sy[wpar], wload)
                if kc >= 2:
                    P.wait("tensor", mm_done[kc - 2][0], mm_done[kc - 2][1])
                for ncc in range(32):
                    last = (ncc == 31)
                    for q2 in range(2):
                        P.op("tensor", "matmul",
                             ppf(pr)[:, q2 * 512:(q2 + 1) * 512],
                             wt5[wpar][:, ncc * 128:(ncc + 1) * 128],
                             t3T_v[:, ncc, q2 * 512:(q2 + 1) * 512],
                             start=(ncc == 0), stop=last,
                             inc=(s_t if last and q2 == 1 else None))
                wfree5[wpar] = (s_t, s_t.v)
                P.wait("scalar", s_t, s_t.v)
                P.op("scalar", "activation", f1_v[:, kc, :], ppf(pr)[:, :],
                     AF.Identity, bias=b1fm[:, kc:kc + 1], scale=1.0,
                     inc=s_s)
                mm_done[kc] = (s_s, s_s.v)
            last_evac = mm_done[31]
            f_tm5 = shared5
            tmpb5 = wt5[0]
            for k in range(8):
                i = h * 8 + k
                emit_fm_to_tm(f1_v, k, f_tm5, last_evac[0], last_evac[1])
                par = i % 2
                if x_free5[par] is not None:
                    P.wait("sync", x_free5[par][0], x_free5[par][1])
                P.dma("sync", xl5[par][:, :],
                      x3s_d[i * 128:(i + 1) * 128, :], sx[par])
                xload = sx[par].v
                emit_ln_stats(f_tm5)
                P.op("vector", "tensor_scalar", tmpb5[:, :], f_tm5[:, :],
                     stats[:, MEAN:MEAN + 1], stats[:, RSTD:RSTD + 1],
                     ALU.subtract, ALU.mult)
                P.chain("vector", s_v)
                P.wait("vector", sx[par], xload)
                if o_free5[par] is not None:
                    P.wait("vector", o_free5[par][0], o_free5[par][1])
                P.op("vector", "tensor_add", xo5[par][:, :], xl5[par][:, :],
                     tmpb5[:, :], inc=s_v)
                x_free5[par] = (s_v, s_v.v)
                P.wait("scalar", s_v, s_v.v)
                P.dma("scalar", out_d[i * 128:(i + 1) * 128, :],
                      xo5[par][:, :], sst[par])
                o_free5[par] = (sst[par], sst[par].v)
            wfree5[0] = (s_v, s_v.v)

        A.release()
        return finish()


def _emit_block(nc, P):
    with nc.Block() as block:
        @block.sync
        def _(eng):
            for fn in P.q["sync"]:
                fn(eng)

        @block.scalar
        def _(eng):
            for fn in P.q["scalar"]:
                fn(eng)

        @block.vector
        def _(eng):
            for fn in P.q["vector"]:
                fn(eng)

        @block.tensor
        def _(eng):
            for fn in P.q["tensor"]:
                fn(eng)

        @block.gpsimd
        def _(eng):
            for fn in P.q["gpsimd"]:
                fn(eng)



def _prep_shared(inputs):
    C = _host_consts()
    g = lambda k: np.asarray(inputs[k])
    ac = np.ascontiguousarray

    vc = g("L0_attn_Vc_ri")
    wbig = np.concatenate([g("L0_ffn_Wq"), g("L0_ffn_Wk"),
                           g("L0_ffn_A").reshape(128, D),
                           g("L0_ffn_W")], axis=0)
    wqk = g("L1_attn_Wqk_w").reshape(8, 64, 2, D)
    wqkv = np.concatenate([ac(wqk[:, :, 0, :]).reshape(512, D),
                           ac(wqk[:, :, 1, :]).reshape(512, D),
                           g("L1_attn_Wv_w")], axis=0)
    bqk = g("L1_attn_Wqk_b").reshape(8, 64, 2)
    bqkv = np.concatenate([ac(bqk[:, :, 0]).reshape(512),
                           ac(bqk[:, :, 1]).reshape(512),
                           g("L1_attn_Wv_b")])
    shared = dict(
        lam=ac(g("L0_attn_Lambda_ri")),
        pri=ac(g("L0_attn_p_ri")),
        qri=ac(g("L0_attn_q_ri")),
        ctri=ac(g("L0_attn_Ct_ri").reshape(64, 2)),
        bp=ac(g("L0_attn_Bp")),
        dv=ac(g("L0_attn_D")),
        ls=ac(g("L0_attn_log_step")),
        vcre=ac(vc[:, :, 0]),
        vcim=ac(vc[:, :, 1]),
        wbig0T=ac(wbig.T).reshape(32, 128, 4736),
        b0fm=ac(g("L0_ffn_b").reshape(32, 128).T),
        bmflat=ac(g("L0_ffn_Bm").transpose(0, 2, 1).reshape(128, D)),
        wqkvT=ac(wqkv.T).reshape(32, 128, 1536),
        bqkv=ac(bqkv.reshape(12, 128).T),
        wo1T=ac(g("L1_attn_Wo_w").T).reshape(4, 128, D),
        wob=ac(g("L1_attn_Wo_b").reshape(1, D)),
        wq1=ac(g("L1_ffn_W_q")).reshape(32, 128, D),
        mask1=ac(g("L1_ffn_mask")).reshape(32, 128, D),
        scales1=ac(g("L1_ffn_scales")[:, :, 0]).reshape(32, 128, 32),
        zeros1=ac(g("L1_ffn_zeros")[:, :, 0]).reshape(32, 128, 32),
        scale2=ac(g("L1_ffn_scale2")).reshape(32, 128, 1),
        b1fm=ac(g("L1_ffn_bias").reshape(32, 128).T),
    )
    shared.update(C)
    return shared


def kernel(**inputs):
    shared = _prep_shared(inputs)
    x0 = np.asarray(inputs["L0_attn_u"])
    nc = build(upto=5, dev=False)
    in_maps = []
    for i in range(8):
        m = dict(shared)
        m["x0"] = np.ascontiguousarray(x0[i])
        in_maps.append(m)
    res = run_bass_kernel_spmd(nc, in_maps, core_ids=list(range(8)))
    return np.stack([res.results[i]["out"] for i in range(8)], axis=0)


# version 06
# speedup vs baseline: 1.0405x; 1.0405x over previous
import math
from contextlib import ExitStack

import numpy as np
import ml_dtypes

import concourse.bass as bass
import concourse.mybir as mybir
from concourse.bass_utils import run_bass_kernel_spmd

F32 = mybir.dt.float32
BF16 = mybir.dt.bfloat16
I32 = mybir.dt.int32
AF = mybir.ActivationFunctionType
ALU = mybir.AluOpType
AX = mybir.AxisListType

D = 4096
SEQ = 2048
NT = 16
EPS = 1e-5
NWORDS = 53200
P1_TILES = NT

ENGINES = ("sync", "scalar", "vector", "tensor", "gpsimd")



_CONST_CACHE = {}


def _host_consts():
    if _CONST_CACHE:
        return _CONST_CACHE
    L = D
    lv = np.arange(L, dtype=np.float64)
    om = np.exp(-2j * np.pi * (lv / L))
    g0 = (1.0 - om) / (1.0 + om)
    cc = 2.0 / (1.0 + om)
    G0b_re = np.broadcast_to(g0.real.astype(np.float32), (64, L)).copy()
    G0b_im = np.broadcast_to(g0.imag.astype(np.float32), (64, L)).copy()
    ccons = np.concatenate([
        cc.real.astype(np.float32).reshape(32, 128).T,
        cc.imag.astype(np.float32).reshape(32, 128).T], axis=1).copy()
    cidx = np.arange(32)
    F1 = np.exp(2j * np.pi * np.outer(cidx, cidx) / 32.0) / L
    fcons = np.ascontiguousarray(
        np.concatenate([F1.real, F1.imag, -F1.imag], axis=1).astype(np.float32))
    p_ = np.arange(128)
    TW = np.exp(2j * np.pi * np.outer(p_, cidx) / L)
    twcons = np.ascontiguousarray(
        np.concatenate([TW.real, TW.imag], axis=1).astype(np.float32))
    G2 = np.exp(2j * np.pi * np.outer(p_, p_) / 128.0)
    g2cons = np.ascontiguousarray(
        np.concatenate([G2.real, -G2.imag], axis=1).astype(np.float32))

    identb = np.eye(128, dtype=ml_dtypes.bfloat16)
    identf = np.eye(128, dtype=np.float32)
    onesrow = np.ones((1, 128), dtype=np.float32)
    e8 = np.zeros((128, 16), dtype=np.float32)
    for p in range(128):
        e8[p, p // 32] = 1.0 / math.sqrt(32.0)
        e8[p, 8 + 4 + p // 32] = 1.0 / math.sqrt(32.0)
    e8 = e8.astype(ml_dtypes.bfloat16)
    e16f = np.zeros((8, 128), dtype=np.float32)
    for er in range(128):
        e16f[er // 16, er] = 1.0

    _CONST_CACHE.update(dict(
        g0bre=G0b_re, g0bim=G0b_im, ccons=ccons, fcons=fcons, twcons=twcons,
        g2cons=g2cons, identb=identb, identf=identf, onesrow=onesrow,
        e8=e8, e16f=e16f,
    ))
    return _CONST_CACHE



class Sem:
    def __init__(self, ctx, nc, name):
        self.h = ctx.enter_context(nc.semaphore(name))
        self.v = 0


class Prog:

    def __init__(self, nc, ctx):
        self.nc = nc
        self.ctx = ctx
        self.q = {k: [] for k in ENGINES}
        self.last = {k: None for k in ENGINES}
        self.auto = {}
        self.waited = {k: {} for k in ENGINES}

    def sem(self, name):
        return Sem(self.ctx, self.nc, name)

    def wait(self, eng, sem, val):
        if val <= 0:
            return
        w = self.waited[eng]
        if w.get(id(sem), 0) >= val:
            return
        w[id(sem)] = val

        def run(e, s=sem.h, v=val):
            e.wait_ge(s, v)
        self.q[eng].append(run)
        self.last[eng] = None

    def op(self, eng, meth, *args, inc=None, incn=1, **kw):
        aser = self.auto.get(eng)
        if aser is not None:
            self.wait(eng, aser, aser.v)
            if meth not in ("dma_start", "sem_inc"):
                if inc is None:
                    inc = aser
                assert inc is aser, \
                    f"op({eng}) inc must be the engine self-sem"
        cell = {"t": None}
        if inc is not None:
            inc.v += incn
            cell["t"] = (inc.h, incn)

        def run(e, m=meth, a=args, k=kw, c=cell):
            ins = getattr(e, m)(*a, **k)
            if c["t"] is not None:
                ins.then_inc(*c["t"])
        self.q[eng].append(run)
        self.last[eng] = (cell, inc)

    def dma(self, eng, out, in_, inc, incn=16, **kw):
        self.op(eng, "dma_start", out=out, in_=in_, inc=inc, incn=incn, **kw)

    def chain(self, eng, sem):
        last = self.last[eng]
        assert last is not None, f"chain({eng}) with no preceding op"
        cell, inc = last
        if cell["t"] is None:
            sem.v += 1
            cell["t"] = (sem.h, 1)
        else:
            assert inc is sem, f"chain({eng}) after op with different inc"
        self.wait(eng, sem, sem.v)


class Arena:

    def __init__(self, ap, nwords):
        self.ap = ap
        self.n = nwords
        self.ptr = 0
        self.marks = []
        self.hi = 0

    def f32(self, parts, words):
        off = self.ptr
        self.ptr += words
        self.hi = max(self.hi, self.ptr)
        assert self.ptr <= self.n, f"arena overflow {self.ptr} > {self.n}"
        return self.ap[0:parts, off:off + words]

    def bf16(self, parts, n):
        w = (n + 1) // 2
        return self.f32(128, w).bitcast(BF16)[0:parts, 0:n]

    def i32(self, parts, n):
        return self.f32(128, n).bitcast(I32)[0:parts, 0:n]

    def mark(self):
        self.marks.append(self.ptr)

    def release(self):
        self.ptr = self.marks.pop()



def build(upto=5, dev=False):
    nc = bass.Bass("TRN2", target_bir_lowering=False, debug=False)

    def din(name, shape, dtype=F32):
        return nc.dram_tensor(name, list(shape), dtype,
                              kind="ExternalInput").ap()

    x0_d = din("x0", (SEQ, D))
    lam_d = din("lam", (64, 2))
    pri_d = din("pri", (64, 2))
    qri_d = din("qri", (64, 2))
    ctri_d = din("ctri", (64, 2))
    bp_d = din("bp", (64, 1))
    dv_d = din("dv", (1,))
    ls_d = din("ls", (1,))
    vcre_d = din("vcre", (64, 64))
    vcim_d = din("vcim", (64, 64))
    wbig_d = din("wbig0T", (32, 128, 4736))
    b0fm_d = din("b0fm", (128, 32))
    bmflat_d = din("bmflat", (128, D))
    wqkv_d = din("wqkvT", (32, 128, 1536))
    bqkv_d = din("bqkv", (128, 12))
    wo_d = din("wo1T", (4, 128, D))
    wob_d = din("wob", (1, D))
    wq1_d = din("wq1", (32, 128, D), I32)
    mask1_d = din("mask1", (32, 128, D), I32)
    scales_d = din("scales1", (32, 128, 32))
    zeros_d = din("zeros1", (32, 128, 32))
    scale2_d = din("scale2", (32, 128, 1))
    b1fm_d = din("b1fm", (128, 32))
    g0bre_d = din("g0bre", (64, D))
    g0bim_d = din("g0bim", (64, D))
    ccons_d = din("ccons", (128, 64))
    fcons_d = din("fcons", (32, 96))
    twcons_d = din("twcons", (128, 64))
    g2cons_d = din("g2cons", (128, 256))
    identb_d = din("identb", (128, 128), BF16)
    identf_d = din("identf", (128, 128))
    ones_d = din("onesrow", (1, 128))
    e8_d = din("e8", (128, 16), BF16)
    e16_d = din("e16f", (8, 128))

    out_d = nc.dram_tensor("out", [SEQ, D], F32, kind="ExternalOutput").ap()

    skind = "ExternalOutput" if dev else "Internal"
    x1s_d = nc.dram_tensor("x1s", [SEQ, D], BF16, kind=skind).ap()
    x2s_d = nc.dram_tensor("x2s", [SEQ, D], BF16, kind=skind).ap()
    x3s_d = nc.dram_tensor("x3s", [SEQ, D], BF16, kind=skind).ap()
    kpad_d = nc.dram_tensor("kpad", [4992], BF16, kind=skind).ap()
    wdeqT_d = nc.dram_tensor("wdeqT", [32, 128, 32, 128], BF16,
                             kind=skind).ap()

    ctx = ExitStack()
    with ctx:
        e_ = ctx.enter_context
        P = Prog(nc, ctx)

        arena_t = e_(nc.sbuf_tensor("arena", [128, NWORDS], F32))
        A = Arena(arena_t[:, :], NWORDS)

        pp = [e_(nc.psum_tensor("pp%d" % i, [128, 1024], F32))
              for i in range(4)]

        def ppf(i):
            return pp[i][:, :]

        def ppb(i):
            return pp[i][:].bitcast(BF16)

        sx = [P.sem("sx0"), P.sem("sx1")]
        sy = [P.sem("sy0"), P.sem("sy1")]
        sm = P.sem("sm")
        sw = [P.sem("sw0"), P.sem("sw1")]
        sv2 = [P.sem("sv2a"), P.sem("sv2b")]
        s_wb = P.sem("s_wb")
        sst = [P.sem("sst0"), P.sem("sst1")]
        s_t = P.sem("s_t")
        s_v = P.sem("s_v")
        s_s = P.sem("s_s")
        s_g = P.sem("s_g")
        sb = P.sem("sb")
        P.auto = {"vector": s_v, "scalar": s_s}

        identb = A.bf16(128, 128)
        identf = A.f32(128, 128)
        onesrow = A.f32(1, 128)
        e8 = A.bf16(128, 16)
        e16f = A.f32(8, 128)
        stats = A.f32(128, 16)
        bnbuf = A.f32(128, 48)
        zcol = A.f32(128, 1)
        dcol = A.f32(128, 1)
        b0fm = A.f32(128, 32)
        b1fm = A.f32(128, 32)
        bqkv = A.f32(128, 12)

        P.dma("sync", identb[:, :], identb_d[:, :], sm)
        P.dma("sync", identf[:, :], identf_d[:, :], sm)
        P.dma("sync", onesrow[:, :], ones_d[:, :], sm)
        P.dma("sync", e8[:, :], e8_d[:, :], sm)
        P.dma("sync", e16f[:, :], e16_d[:, :], sm)
        P.dma("sync", b0fm[:, :], b0fm_d[:, :], sm)
        P.dma("sync", b1fm[:, :], b1fm_d[:, :], sm)
        P.dma("sync", bqkv[:, :], bqkv_d[:, :], sm)
        P.op("vector", "memset", zcol[:, :], 0.0, inc=s_v)

        MEAN, VAR, VAREP, SQ, RSTD, NEGMX, SSUM, RSUM = range(8)

        def emit_ln_stats(x_ap):
            for c in range(8):
                P.op("vector", "bn_stats", bnbuf[:, 6 * c:6 * c + 6],
                     x_ap[:, c * 512:(c + 1) * 512])
            P.chain("vector", s_v)
            P.op("vector", "bn_aggr", stats[:, MEAN:MEAN + 2], bnbuf[:, 0:48])
            P.chain("vector", s_v)
            P.op("vector", "tensor_scalar_add", stats[:, VAREP:VAREP + 1],
                 stats[:, VAR:VAR + 1], EPS, inc=s_v)
            P.wait("scalar", s_v, s_v.v)
            P.op("scalar", "activation", stats[:, SQ:SQ + 1],
                 stats[:, VAREP:VAREP + 1], AF.Sqrt, bias=zcol[:, 0:1],
                 inc=s_s)
            P.wait("vector", s_s, s_s.v)
            P.op("vector", "reciprocal", stats[:, RSTD:RSTD + 1],
                 stats[:, SQ:SQ + 1])
            P.chain("vector", s_v)

        def emit_barrier():
            P.wait("sync", sx[0], sx[0].v)
            P.wait("sync", sx[1], sx[1].v)
            P.wait("sync", sm, sm.v)
            P.wait("vector", sv2[0], sv2[0].v)
            P.wait("vector", sv2[1], sv2[1].v)
            P.wait("scalar", sst[0], sst[0].v)
            P.wait("scalar", sst[1], sst[1].v)
            P.wait("gpsimd", sw[0], sw[0].v)
            P.wait("gpsimd", sw[1], sw[1].v)
            P.wait("gpsimd", s_wb, s_wb.v)
            P.wait("sync", sy[0], sy[0].v)
            P.wait("sync", sy[1], sy[1].v)
            P.wait("tensor", s_t, s_t.v)
            for eng in ("tensor", "vector", "scalar", "gpsimd"):
                P.op(eng, "sem_inc", sb.h, 1)
                sb.v += 1
            for eng in ENGINES:
                P.wait(eng, sb, sb.v)

        def finish():
            P.wait("scalar", sst[0], sst[0].v)
            P.wait("scalar", sst[1], sst[1].v)
            P.wait("vector", sv2[0], sv2[0].v)
            P.wait("vector", sv2[1], sv2[1].v)
            P.wait("gpsimd", sw[0], sw[0].v)
            P.wait("gpsimd", sw[1], sw[1].v)
            P.wait("gpsimd", s_wb, s_wb.v)
            P.wait("sync", sx[0], sx[0].v)
            P.wait("sync", sx[1], sx[1].v)
            P.wait("sync", sy[0], sy[0].v)
            P.wait("sync", sy[1], sy[1].v)
            P.wait("sync", sm, sm.v)
            _emit_block(nc, P)
            return nc

        A.mark()
        w3 = A.bf16(128, 4480)

        A.mark()
        g0re = A.f32(64, D)
        g0im = A.f32(64, D)
        dre = A.f32(64, D)
        dim = A.f32(64, D)
        m1 = A.f32(64, D)
        m2 = A.f32(64, D)
        c2 = A.f32(64, D)
        rcp = A.f32(64, D)
        rec_re = A.f32(64, D)
        rec_im = A.f32(64, D)
        smallp = A.f32(64, 16)
        vc = A.f32(64, 128)
        vcT = A.f32(64, 128)
        v8 = A.f32(64, 12)
        ccons = A.f32(128, 64)
        fcons = A.f32(32, 96)
        twcons = A.f32(128, 64)
        g2cons = A.f32(128, 256)
        kt = A.f32(128, 256)
        at2 = A.f32(128, 64)
        atT = A.f32(32, 256)
        b2 = A.f32(128, 64)
        ksb = A.f32(32, 128)
        kfinb = A.bf16(128, 32)
        zrow = A.bf16(1, 512)
        s4t = A.f32(64, 8)
        s4u = A.f32(128, 32 * 10)

        def U(j):
            return s4u[:, j * 32:(j + 1) * 32]

        P.dma("sync", g0re[:, :], g0bre_d[:, :], sm)
        P.dma("sync", g0im[:, :], g0bim_d[:, :], sm)
        P.dma("sync", smallp[:, 0:2], lam_d[:, :], sm)
        P.dma("sync", smallp[:, 2:4], pri_d[:, :], sm)
        P.dma("sync", smallp[:, 4:6], qri_d[:, :], sm)
        P.dma("sync", smallp[:, 6:8], ctri_d[:, :], sm)
        P.dma("sync", smallp[:, 8:9], bp_d[:, :], sm)
        P.dma("sync", smallp[0:1, 9:10], ls_d.unsqueeze(0), sm)
        P.dma("sync", smallp[0:1, 10:11], dv_d.unsqueeze(0), sm)
        P.dma("sync", vc[:, 0:64], vcre_d[:, :], sm)
        P.dma("sync", vc[:, 64:128], vcim_d[:, :], sm)
        P.dma("sync", ccons[:, :], ccons_d[:, :], sm)
        P.dma("sync", fcons[:, :], fcons_d[:, :], sm)
        P.dma("sync", twcons[:, :], twcons_d[:, :], sm)
        P.dma("sync", g2cons[:, :], g2cons_d[:, :], sm)
        P.op("vector", "memset", zrow[:, :], 0.0, inc=s_v)
        zmemset_done = s_v.v

        ld0 = sm.v
        P.wait("tensor", sm, ld0)
        P.wait("vector", sm, ld0)
        P.wait("scalar", sm, ld0)

        P.op("tensor", "matmul", ppf(0)[0:64, 0:1], onesrow[0:1, 0:64],
             smallp[0:1, 9:10], start=True, stop=True)
        P.op("tensor", "matmul", ppf(0)[0:128, 512:513], onesrow[0:1, 0:128],
             smallp[0:1, 10:11], start=True, stop=True, inc=s_t)
        P.wait("scalar", s_t, s_t.v)
        P.wait("scalar", s_v, zmemset_done)
        P.op("scalar", "activation", smallp[:, 11:12], ppf(0)[0:64, 0:1],
             AF.Exp, bias=zcol[0:64, 0:1], inc=s_s)
        P.wait("vector", s_s, s_s.v)
        P.wait("vector", s_t, s_t.v)
        P.op("vector", "tensor_copy", dcol[:, 0:1], ppf(0)[0:128, 512:513])
        P.op("vector", "reciprocal", smallp[:, 12:13], smallp[:, 11:12])
        P.chain("vector", s_v)
        P.op("vector", "tensor_scalar_mul", smallp[:, 13:14],
             smallp[:, 12:13], 2.0, inc=s_v)
        tds_done = s_v.v

        P.op("tensor", "transpose", ppf(1)[0:64, 0:64], vc[0:64, 0:64],
             identf[0:64, 0:64])
        P.op("tensor", "transpose", ppf(1)[0:64, 512:576], vc[0:64, 64:128],
             identf[0:64, 0:64], inc=s_t)
        P.wait("vector", s_t, s_t.v)
        P.op("vector", "tensor_copy", vcT[:, 0:64], ppf(1)[0:64, 0:64])
        P.op("vector", "tensor_copy", vcT[:, 64:128], ppf(1)[0:64, 512:576],
             inc=s_v)
        P.wait("tensor", s_v, s_v.v)
        P.op("tensor", "matmul", ppf(2)[0:64, 0:1], vcT[0:64, 0:64],
             smallp[:, 8:9], start=True, stop=True)
        P.op("tensor", "matmul", ppf(2)[0:64, 512:513], vcT[0:64, 64:128],
             smallp[:, 8:9], start=True, stop=True, inc=s_t)
        P.wait("vector", s_t, s_t.v)
        P.op("vector", "tensor_copy", smallp[:, 14:15], ppf(2)[0:64, 0:1])
        P.op("vector", "tensor_copy", smallp[:, 15:16],
             ppf(2)[0:64, 512:513], inc=s_v)

        prs = [(6, 7, 14, 15), (6, 7, 2, 3), (4, 5, 14, 15), (4, 5, 2, 3)]
        P.chain("vector", s_v)
        for i, (ar, ai, br, bi) in enumerate(prs):
            P.op("vector", "tensor_tensor", s4t[:, i:i + 1],
                 smallp[:, ai:ai + 1], smallp[:, bi:bi + 1], ALU.mult)
            P.op("vector", "tensor_tensor", s4t[:, 4 + i:5 + i],
                 smallp[:, ai:ai + 1], smallp[:, br:br + 1], ALU.mult)
        P.chain("vector", s_v)
        for i, (ar, ai, br, bi) in enumerate(prs):
            P.op("vector", "scalar_tensor_tensor", v8[:, i:i + 1],
                 smallp[:, ar:ar + 1], smallp[:, br:br + 1],
                 s4t[:, i:i + 1], ALU.mult, ALU.add)
            P.op("vector", "scalar_tensor_tensor", v8[:, 4 + i:5 + i],
                 smallp[:, ar:ar + 1], smallp[:, bi:bi + 1],
                 s4t[:, 4 + i:5 + i], ALU.mult, ALU.subtract)
        P.chain("vector", s_v)
        P.op("vector", "tensor_scalar_mul", v8[:, 8:12], v8[:, 4:8], -1.0)

        P.wait("vector", s_v, tds_done)
        P.op("vector", "tensor_scalar", dre[:, :], g0re[:, :],
             smallp[:, 13:14], smallp[:, 0:1], ALU.mult, ALU.subtract)
        P.op("vector", "tensor_scalar", dim[:, :], g0im[:, :],
             smallp[:, 13:14], smallp[:, 1:2], ALU.mult, ALU.subtract)
        P.op("vector", "tensor_tensor", m1[:, :], dre[:, :], dre[:, :],
             ALU.mult)
        P.op("vector", "tensor_tensor", m2[:, :], dim[:, :], dim[:, :],
             ALU.mult)
        P.chain("vector", s_v)
        P.op("vector", "tensor_tensor", c2[:, :], m1[:, :], m2[:, :], ALU.add)
        P.chain("vector", s_v)
        P.op("vector", "reciprocal", rcp[:, :], c2[:, :])
        P.chain("vector", s_v)
        P.op("vector", "tensor_tensor", rec_re[:, :], dre[:, :], rcp[:, :],
             ALU.mult)
        P.op("vector", "scalar_tensor_tensor", rec_im[:, :], dim[:, :], -1.0,
             rcp[:, :], ALU.mult, ALU.mult, inc=s_v)
        rec_done = s_v.v

        P.wait("tensor", s_v, rec_done)
        evac_v = {}
        for c in range(32):
            bank = ppf(1 + (c % 2))
            if c >= 2:
                P.wait("tensor", s_v, evac_v[c - 2])
            P.op("tensor", "matmul", bank[0:128, 0:4],
                 rec_re[:, c * 128:(c + 1) * 128], v8[:, 0:4],
                 start=True, stop=False)
            P.op("tensor", "matmul", bank[0:128, 0:4],
                 rec_im[:, c * 128:(c + 1) * 128], v8[:, 8:12],
                 start=False, stop=True)
            P.op("tensor", "matmul", bank[0:128, 4:8],
                 rec_im[:, c * 128:(c + 1) * 128], v8[:, 0:4],
                 start=True, stop=False)
            P.op("tensor", "matmul", bank[0:128, 4:8],
                 rec_re[:, c * 128:(c + 1) * 128], v8[:, 4:8],
                 start=False, stop=True, inc=s_t)
            P.wait("vector", s_t, s_t.v)
            P.op("vector", "tensor_copy", kt[:, c::32],
                 bank[0:128, 0:8], inc=s_v)
            evac_v[c] = s_v.v

        k00re, k01re, k10re, k11re = (kt[:, 0:32], kt[:, 32:64],
                                      kt[:, 64:96], kt[:, 96:128])
        k00im, k01im, k10im, k11im = (kt[:, 128:160], kt[:, 160:192],
                                      kt[:, 192:224], kt[:, 224:256])
        P.chain("vector", s_v)
        P.op("vector", "tensor_scalar_add", U(9), k11re, 1.0)
        P.op("vector", "tensor_tensor", U(0), k01re, k10re, ALU.mult)
        P.op("vector", "tensor_tensor", U(1), k01im, k10im, ALU.mult)
        P.op("vector", "tensor_tensor", U(2), k01re, k10im, ALU.mult)
        P.op("vector", "tensor_tensor", U(3), k01im, k10re, ALU.mult)
        P.chain("vector", s_v)
        P.op("vector", "tensor_tensor", U(4), U(0), U(1), ALU.subtract)
        P.op("vector", "tensor_tensor", U(5), U(2), U(3), ALU.add)
        P.op("vector", "tensor_tensor", U(0), U(9), U(9), ALU.mult)
        P.op("vector", "tensor_tensor", U(1), k11im, k11im, ALU.mult)
        P.chain("vector", s_v)
        P.op("vector", "tensor_tensor", U(2), U(0), U(1), ALU.add)
        P.chain("vector", s_v)
        P.op("vector", "reciprocal", U(3), U(2))
        P.op("vector", "tensor_tensor", U(6), U(4), U(9), ALU.mult)
        P.op("vector", "tensor_tensor", U(7), U(5), k11im, ALU.mult)
        P.op("vector", "tensor_tensor", U(8), U(5), U(9), ALU.mult)
        P.op("vector", "tensor_tensor", U(9), U(4), k11im, ALU.mult)
        P.chain("vector", s_v)
        P.op("vector", "tensor_tensor", U(6), U(6), U(7), ALU.add)
        P.op("vector", "tensor_tensor", U(8), U(8), U(9), ALU.subtract)
        P.chain("vector", s_v)
        P.op("vector", "tensor_tensor", U(6), U(6), U(3), ALU.mult)
        P.op("vector", "tensor_tensor", U(8), U(8), U(3), ALU.mult)
        P.chain("vector", s_v)
        P.op("vector", "tensor_tensor", U(0), k00re, U(6), ALU.subtract)
        P.op("vector", "tensor_tensor", U(1), k00im, U(8), ALU.subtract)
        P.chain("vector", s_v)
        P.op("vector", "tensor_tensor", U(2), ccons[:, 0:32], U(0), ALU.mult)
        P.op("vector", "tensor_tensor", U(3), ccons[:, 32:64], U(1), ALU.mult)
        P.op("vector", "tensor_tensor", U(4), ccons[:, 0:32], U(1), ALU.mult)
        P.op("vector", "tensor_tensor", U(5), ccons[:, 32:64], U(0), ALU.mult)
        P.chain("vector", s_v)
        P.op("vector", "tensor_tensor", at2[:, 0:32], U(2), U(3),
             ALU.subtract)
        P.op("vector", "tensor_tensor", at2[:, 32:64], U(4), U(5), ALU.add,
             inc=s_v)

        P.wait("tensor", s_v, s_v.v)
        P.op("tensor", "transpose", ppf(0)[0:32, 0:128], at2[:, 0:32],
             identf[:, :])
        P.op("tensor", "transpose", ppf(0)[0:32, 512:640], at2[:, 32:64],
             identf[:, :], inc=s_t)
        P.wait("vector", s_t, s_t.v)
        P.op("vector", "tensor_copy", atT[:, 0:128], ppf(0)[0:32, 0:128])
        P.op("vector", "tensor_copy", atT[:, 128:256], ppf(0)[0:32, 512:640],
             inc=s_v)
        P.wait("tensor", s_v, s_v.v)
        P.op("tensor", "matmul", ppf(1)[0:128, 0:32], atT[0:32, 0:128],
             fcons[:, 0:32], start=True, stop=False)
        P.op("tensor", "matmul", ppf(1)[0:128, 0:32], atT[0:32, 128:256],
             fcons[:, 64:96], start=False, stop=True)
        P.op("tensor", "matmul", ppf(1)[0:128, 32:64], atT[0:32, 0:128],
             fcons[:, 32:64], start=True, stop=False)
        P.op("tensor", "matmul", ppf(1)[0:128, 32:64], atT[0:32, 128:256],
             fcons[:, 0:32], start=False, stop=True, inc=s_t)
        P.wait("vector", s_t, s_t.v)
        P.op("vector", "tensor_tensor", U(0), ppf(1)[0:128, 0:32],
             twcons[:, 0:32], ALU.mult)
        P.op("vector", "tensor_tensor", U(1), ppf(1)[0:128, 32:64],
             twcons[:, 32:64], ALU.mult)
        P.op("vector", "tensor_tensor", U(2), ppf(1)[0:128, 0:32],
             twcons[:, 32:64], ALU.mult)
        P.op("vector", "tensor_tensor", U(3), ppf(1)[0:128, 32:64],
             twcons[:, 0:32], ALU.mult)
        P.chain("vector", s_v)
        P.op("vector", "tensor_tensor", b2[:, 0:32], U(0), U(1), ALU.subtract)
        P.op("vector", "tensor_tensor", b2[:, 32:64], U(2), U(3), ALU.add,
             inc=s_v)
        P.wait("tensor", s_v, s_v.v)
        P.op("tensor", "matmul", ppf(2)[0:32, 0:128], b2[:, 0:32],
             g2cons[:, 0:128], start=True, stop=False)
        P.op("tensor", "matmul", ppf(2)[0:32, 0:128], b2[:, 32:64],
             g2cons[:, 128:256], start=False, stop=True, inc=s_t)
        P.wait("vector", s_t, s_t.v)
        P.op("vector", "tensor_copy", ksb[:, :], ppf(2)[0:32, 0:128],
             inc=s_v)
        P.wait("tensor", s_v, s_v.v)
        P.op("tensor", "transpose", ppf(3)[0:128, 0:32], ksb[0:32, :],
             identf[0:32, 0:32], inc=s_t)
        P.wait("vector", s_t, s_t.v)
        P.op("vector", "tensor_copy", kfinb[:, :], ppf(3)[0:128, 0:32],
             inc=s_v)

        if dev:
            dbg_d = nc.dram_tensor("dbg", [128, 3072], F32,
                                   kind="ExternalOutput").ap()
            P.wait("scalar", s_v, s_v.v)
            P.dma("scalar", dbg_d[:, 0:256], kt[:, :], sst[1])
            P.dma("scalar", dbg_d[:, 256:320], at2[:, :], sst[1])
            P.dma("scalar", dbg_d[0:64, 320:332], v8[:, :], sst[1])
            P.dma("scalar", dbg_d[0:64, 332:348], smallp[:, :], sst[1])
            P.dma("scalar", dbg_d[0:64, 348:476], vcT[:, :], sst[1])
            P.dma("scalar", dbg_d[0:32, 512:768], atT[:, :], sst[1])
            P.dma("scalar", dbg_d[0:32, 768:896], ksb[:, :], sst[1])
            P.dma("scalar", dbg_d[:, 896:960], b2[:, :], sst[1])
            P.dma("scalar", dbg_d[0:64, 1024:2048], rec_re[:, 0:1024],
                  sst[1])
            P.dma("scalar", dbg_d[0:64, 2048:3072], rec_im[:, 0:1024],
                  sst[1])

        P.wait("scalar", s_v, s_v.v)
        P.dma("scalar", kpad_d[0:511].unsqueeze(0), zrow[0:1, 0:511], sst[0])
        P.dma("scalar", kpad_d[4607:4992].unsqueeze(0), zrow[0:1, 0:385],
              sst[0])
        P.dma("scalar", kpad_d[511:4607], kfinb[:, :], sst[0])
        P.wait("sync", sst[0], sst[0].v)
        for r in range(128):
            P.dma("sync", w3[r:r + 1, :],
                  kpad_d[511 - r:511 - r + 4480].unsqueeze(0), sm)
        w3_loaded = sm.v

        A.release()
        emit_barrier()
        if upto < 1:
            return finish()

        A.mark()
        xl1 = [A.f32(128, D), A.f32(128, D)]
        xo1 = [A.bf16(128, D), A.bf16(128, D)]
        u_bf = A.bf16(128, D)
        uT = A.bf16(128, D)
        z_bf = A.bf16(128, D)
        tmp1 = A.bf16(128, D)

        P.wait("tensor", sm, w3_loaded)
        xfree = [None, None]
        ofree = [None, None]
        for i in range(P1_TILES):
            par = i % 2
            if xfree[par] is not None:
                P.wait("sync", xfree[par][0], xfree[par][1])
            P.dma("sync", xl1[par][:, :], x0_d[i * 128:(i + 1) * 128, :],
                  sx[par])
            xload = sx[par].v
            P.wait("vector", sx[par], xload)
            emit_ln_stats(xl1[par])
            P.op("vector", "tensor_scalar", u_bf[:, :], xl1[par][:, :],
                 stats[:, MEAN:MEAN + 1], stats[:, RSTD:RSTD + 1],
                 ALU.subtract, ALU.mult, inc=s_v)
            u_done = s_v.v
            P.wait("tensor", s_v, u_done)
            for half in range(2):
                for j in range(16):
                    dd = half * 16 + j
                    P.op("tensor", "transpose",
                         ppb(half)[:, j * 128:(j + 1) * 128],
                         u_bf[:, dd * 128:(dd + 1) * 128], identb[:, :],
                         inc=(s_t if j == 15 else None))
                P.wait("vector", s_t, s_t.v)
                P.op("vector", "tensor_copy",
                     uT[:, half * 2048:(half + 1) * 2048], ppb(half)[:, :],
                     inc=s_v)
            P.wait("tensor", s_v, s_v.v)
            stop_cnt = {}
            for j0 in range(32):
                for c in range(j0 // 4, 8):
                    b = (c + 2) % 8
                    bank = ppf(b // 2)[:, (b % 2) * 512:(b % 2 + 1) * 512]
                    last = (j0 == 4 * c + 3)
                    off = 384 + 512 * c - 128 * j0
                    P.op("tensor", "matmul", bank,
                         uT[:, j0 * 128:(j0 + 1) * 128],
                         w3[:, off:off + 512],
                         start=(j0 == 0), stop=last,
                         inc=(s_t if last else None))
                    if last:
                        stop_cnt[c] = s_t.v
            for c in range(8):
                b = (c + 2) % 8
                bank = ppf(b // 2)[:, (b % 2) * 512:(b % 2 + 1) * 512]
                P.wait("vector", s_t, stop_cnt[c])
                P.op("vector", "scalar_tensor_tensor",
                     z_bf[:, c * 512:(c + 1) * 512],
                     u_bf[:, c * 512:(c + 1) * 512], dcol[:, 0:1], bank,
                     ALU.mult, ALU.add, inc=(s_v if c == 7 else None))
            epi_done = s_v.v
            P.chain("vector", s_v)
            emit_ln_stats(z_bf)
            P.op("vector", "tensor_scalar", tmp1[:, :], z_bf[:, :],
                 stats[:, MEAN:MEAN + 1], stats[:, RSTD:RSTD + 1],
                 ALU.subtract, ALU.mult)
            P.chain("vector", s_v)
            if ofree[par] is not None:
                P.wait("vector", ofree[par][0], ofree[par][1])
            P.op("vector", "tensor_add", xo1[par][:, :], xl1[par][:, :],
                 tmp1[:, :], inc=s_v)
            xfree[par] = (s_v, s_v.v)
            P.wait("scalar", s_v, s_v.v)
            P.dma("scalar", x1s_d[i * 128:(i + 1) * 128, :], xo1[par][:, :],
                  sst[par])
            ofree[par] = (sst[par], sst[par].v)
            P.wait("tensor", s_v, epi_done)

        if dev:
            dbg1_d = nc.dram_tensor("dbg1", [128, 3 * D], BF16,
                                    kind="ExternalOutput").ap()
            P.wait("scalar", s_v, s_v.v)
            P.dma("scalar", dbg1_d[:, 0:D], u_bf[:, :], sst[1])
            P.dma("scalar", dbg1_d[:, D:2 * D], uT[:, :], sst[1])
            P.dma("scalar", dbg1_d[:, 2 * D:3 * D], z_bf[:, :], sst[1])

        A.release()
        A.release()
        emit_barrier()
        if upto < 2:
            return finish()

        def emit_prep_half(xsrc_d, h, xl, tbuf, tT_v, x_free):
            P.wait("vector", s_t, s_t.v)
            for k in range(8):
                i = h * 8 + k
                par = k % 2
                if x_free[par] is not None:
                    P.wait("sync", x_free[par][0], x_free[par][1])
                P.dma("sync", xl[par][:, :],
                      xsrc_d[i * 128:(i + 1) * 128, :], sx[par])
                xload = sx[par].v
                P.wait("vector", sx[par], xload)
                emit_ln_stats(xl[par])
                if k > 0:
                    P.wait("vector", s_t, tp_done)
                P.op("vector", "tensor_scalar", tbuf[:, :], xl[par][:, :],
                     stats[:, MEAN:MEAN + 1], stats[:, RSTD:RSTD + 1],
                     ALU.subtract, ALU.mult, inc=s_v)
                x_free[par] = (s_v, s_v.v)
                P.wait("tensor", s_v, s_v.v)
                for half in range(2):
                    for j in range(16):
                        dd = half * 16 + j
                        P.op("tensor", "transpose",
                             ppb(half)[:, j * 128:(j + 1) * 128],
                             tbuf[:, dd * 128:(dd + 1) * 128], identb[:, :],
                             inc=(s_t if j == 15 else None))
                    tp_done = s_t.v
                    P.wait("vector", s_t, tp_done)
                    P.op("vector", "tensor_copy",
                         tT_v[:, half * 16:(half + 1) * 16,
                              k * 128:(k + 1) * 128],
                         ppb(half)[:, :].rearrange("p (a b) -> p a b", a=16),
                         inc=s_v)

        def emit_fm_to_tm(fv, k, f_tm, src_ready_sem, src_ready_val):
            P.wait("tensor", src_ready_sem, src_ready_val)
            P.wait("tensor", s_v, s_v.v)
            for half in range(2):
                for j in range(16):
                    oc = half * 16 + j
                    P.op("tensor", "transpose",
                         ppb(half)[:, j * 128:(j + 1) * 128],
                         fv[:, oc, k * 128:(k + 1) * 128], identb[:, :],
                         inc=(s_t if j == 15 else None))
                P.wait("vector", s_t, s_t.v)
                P.op("vector", "tensor_copy",
                     f_tm[:, half * 2048:(half + 1) * 2048], ppb(half)[:, :],
                     inc=s_v)

        def emit_boundary_tile(i, f_tm, tmpb, xsrc_d, xdst_d, xl, xo,
                               x_free, o_free):
            par = i % 2
            if x_free[par] is not None:
                P.wait("sync", x_free[par][0], x_free[par][1])
            P.dma("sync", xl[par][:, :], xsrc_d[i * 128:(i + 1) * 128, :],
                  sx[par])
            xload = sx[par].v
            emit_ln_stats(f_tm)
            P.op("vector", "tensor_scalar", tmpb[:, :], f_tm[:, :],
                 stats[:, MEAN:MEAN + 1], stats[:, RSTD:RSTD + 1],
                 ALU.subtract, ALU.mult)
            P.chain("vector", s_v)
            P.wait("vector", sx[par], xload)
            if o_free[par] is not None:
                P.wait("vector", o_free[par][0], o_free[par][1])
            P.op("vector", "tensor_add", xo[par][:, :], xl[par][:, :],
                 tmpb[:, :], inc=s_v)
            x_free[par] = (s_v, s_v.v)
            P.wait("scalar", s_v, s_v.v)
            P.dma("scalar", xdst_d[i * 128:(i + 1) * 128, :], xo[par][:, :],
                  sst[par])
            o_free[par] = (sst[par], sst[par].v)

        A.mark()
        t1T = A.bf16(128, 32 * 1024)
        t1T_v = t1T.rearrange("p (d t) -> p d t", d=32)
        fbuf = A.bf16(128, 32 * 1024)
        fbuf_v = fbuf.rearrange("p (d t) -> p d t", d=32)
        wbuf = [A.bf16(128, 4096), A.bf16(128, 4096)]
        shared = A.bf16(128, 5120)
        hprime = A.bf16(128, 1024)
        qk = A.bf16(128, 1024)
        s8 = A.f32(8, 1024)
        wrow = s8
        stw = A.f32(128, 192)
        bmflat = A.bf16(128, D)
        xl2 = [A.bf16(128, D), A.bf16(128, D)]
        xo2 = [A.bf16(128, D), A.bf16(128, D)]
        qkh = shared

        P.dma("gpsimd", bmflat[:, :], bmflat_d[:, :], s_wb)
        bm_loaded = s_wb.v
        x_free2 = [None, None]
        o_free2 = [None, None]
        wfree = [None, None]

        def p2_scores():
            stm = stw[:, 0:64]
            wtm = stw[:, 64:128]
            wsc = stw[:, 128:192]
            for hq in range(2):
                if hq == 1:
                    P.wait("vector", s_t, s_t.v)
                P.op("vector", "tensor_tensor", qk[:, :],
                     qkh[:, hq * 1024:(hq + 1) * 1024],
                     qkh[:, (2 + hq) * 1024:(3 + hq) * 1024], ALU.mult,
                     inc=s_v)
                P.wait("tensor", s_v, s_v.v)
                for q2 in range(2):
                    P.op("tensor", "matmul",
                         ppf(0)[0:8, q2 * 512:(q2 + 1) * 512],
                         e8[:, hq * 8:(hq + 1) * 8],
                         qk[:, q2 * 512:(q2 + 1) * 512],
                         start=(hq == 0), stop=(hq == 1),
                         inc=(s_t if q2 == 1 else None))
            P.wait("vector", s_t, s_t.v)
            P.op("vector", "tensor_copy", s8[:, :], ppf(0)[0:8, :], inc=s_v)
            P.wait("tensor", s_v, s_v.v)
            for k in range(8):
                P.op("tensor", "transpose", ppf(1)[0:128, k * 8:(k + 1) * 8],
                     s8[0:8, k * 128:(k + 1) * 128], identf[0:8, 0:8],
                     inc=(s_t if k == 7 else None))
            P.wait("vector", s_t, s_t.v)
            P.op("vector", "tensor_copy", stm[:, :], ppf(1)[0:128, 0:64],
                 inc=s_v)
            for k in range(8):
                P.chain("vector", s_v)
                P.op("vector", "tensor_reduce", stats[:, NEGMX:NEGMX + 1],
                     stm[:, k * 8:(k + 1) * 8], AX.X, ALU.max, negate=True,
                     inc=s_v)
                P.wait("scalar", s_v, s_v.v)
                P.op("scalar", "activation", wtm[:, k * 8:(k + 1) * 8],
                     stm[:, k * 8:(k + 1) * 8], AF.Exp,
                     bias=stats[:, NEGMX:NEGMX + 1],
                     accum_out=stats[:, SSUM:SSUM + 1], inc=s_s)
                P.wait("vector", s_s, s_s.v)
                P.op("vector", "reciprocal", stats[:, RSUM:RSUM + 1],
                     stats[:, SSUM:SSUM + 1])
                P.chain("vector", s_v)
                P.op("vector", "tensor_scalar_mul", wsc[:, k * 8:(k + 1) * 8],
                     wtm[:, k * 8:(k + 1) * 8], stats[:, RSUM:RSUM + 1],
                     inc=s_v)
            P.wait("tensor", s_v, s_v.v)
            for k in range(8):
                P.op("tensor", "transpose",
                     ppf(0)[0:8, k * 128:(k + 1) * 128],
                     wsc[:, k * 8:(k + 1) * 8], identf[:, :],
                     inc=(s_t if k == 7 else None))
            P.wait("vector", s_t, s_t.v)
            P.op("vector", "tensor_copy", wrow[:, :], ppf(0)[0:8, :],
                 inc=s_v)
            P.wait("tensor", s_v, s_v.v)
            for q2 in range(2):
                P.op("tensor", "matmul", ppf(1)[:, q2 * 512:(q2 + 1) * 512],
                     e16f[0:8, :], wrow[0:8, q2 * 512:(q2 + 1) * 512],
                     start=True, stop=True, inc=(s_t if q2 == 1 else None))
            P.wait("vector", s_t, s_t.v)
            P.op("vector", "tensor_tensor", hprime[:, :],
                 qkh[:, 4 * 1024:5 * 1024], ppf(1)[:, :], ALU.mult, inc=s_v)
            return s_v.v

        for h in range(2):
            t1b = shared[:, 0:4096]
            emit_prep_half(x1s_d, h, xl2, t1b, t1T_v, x_free2)
            P.wait("tensor", s_v, s_v.v)
            P.wait("tensor", s_s, s_s.v)
            hprime_done = None
            mm_done = {}
            for oc in range(37):
                wpar = oc % 2
                if wfree[wpar] is not None:
                    P.wait("gpsimd", wfree[wpar][0], wfree[wpar][1])
                P.dma("gpsimd", wbuf[wpar][:, :],
                      wbig_d[:, :, oc * 128:(oc + 1) * 128].transpose(
                          [1, 0, 2]), sw[wpar])
                wload = sw[wpar].v
                pr = 2 + oc % 2
                P.wait("tensor", sw[wpar], wload)
                if oc >= 2:
                    P.wait("tensor", mm_done[oc - 2][0], mm_done[oc - 2][1])
                for d in range(32):
                    laststop = (d == 31 and oc < 5)
                    for q2 in range(2):
                        P.op("tensor", "matmul",
                             ppf(pr)[:, q2 * 512:(q2 + 1) * 512],
                             wbuf[wpar][:, d * 128:(d + 1) * 128],
                             t1T_v[:, d, q2 * 512:(q2 + 1) * 512],
                             start=(d == 0), stop=laststop,
                             inc=(s_t if laststop and q2 == 1 else None))
                if oc >= 5:
                    if oc == 5:
                        P.wait("tensor", s_v, hprime_done)
                        P.wait("tensor", s_wb, bm_loaded)
                    for q2 in range(2):
                        P.op("tensor", "matmul",
                             ppf(pr)[:, q2 * 512:(q2 + 1) * 512],
                             bmflat[:, (oc - 5) * 128:(oc - 4) * 128],
                             hprime[:, q2 * 512:(q2 + 1) * 512],
                             start=False, stop=True,
                             inc=(s_t if q2 == 1 else None))
                wfree[wpar] = (s_t, s_t.v)
                mm_t = s_t.v
                if oc < 5:
                    P.wait("vector", s_t, mm_t)
                    P.op("vector", "tensor_copy",
                         qkh[:, oc * 1024:(oc + 1) * 1024], ppf(pr)[:, :],
                         inc=s_v)
                    mm_done[oc] = (s_v, s_v.v)
                else:
                    P.wait("scalar", s_t, mm_t)
                    P.op("scalar", "activation", fbuf_v[:, oc - 5, :],
                         ppf(pr)[:, :], AF.Identity,
                         bias=b0fm[:, oc - 5:oc - 4], scale=1.0, inc=s_s)
                    mm_done[oc] = (s_s, s_s.v)
                if oc == 4:
                    P.wait("vector", mm_done[4][0], mm_done[4][1])
                    hprime_done = p2_scores()
            last_evac = mm_done[36]
            f_tm = shared[:, 0:4096]
            tmpb2 = wbuf[0]
            for k in range(8):
                emit_fm_to_tm(fbuf_v, k, f_tm, last_evac[0], last_evac[1])
                emit_boundary_tile(h * 8 + k, f_tm, tmpb2, x1s_d, x2s_d,
                                   xl2, xo2, x_free2, o_free2)
            wfree[0] = (s_v, s_v.v)

        A.release()
        emit_barrier()
        if upto < 3:
            return finish()

        A.mark()
        qkv = A.bf16(128, 12 * 2048)
        qkv_v = qkv.rearrange("p (c t) -> p c t", c=12)
        attn_cat = A.bf16(128, 4 * 2048)
        attn_v = attn_cat.rearrange("p (c t) -> p c t", c=4)
        vtm = A.bf16(128, 8 * 16 * 65)
        vtm_v = vtm.rearrange("p (h k c) -> p h k c", h=8, k=16)
        xl3 = [A.bf16(128, D), A.bf16(128, D)]
        xo3 = [A.bf16(128, D), A.bf16(128, D)]

        x_free3 = [None, None]
        o_free3 = [None, None]

        A.mark()
        t2T = A.bf16(128, 32 * 1024)
        t2T_v = t2T.rearrange("p (d t) -> p d t", d=32)
        wbuf3 = [A.bf16(128, 4096), A.bf16(128, 4096)]
        t2b = A.bf16(128, 4096)
        qkv_done = None
        for h in range(2):
            emit_prep_half(x2s_d, h, xl3, t2b, t2T_v, x_free3)
            P.wait("tensor", s_v, s_v.v)
            P.wait("tensor", s_s, s_s.v)
            mm_done = {}
            for oc in range(12):
                wpar = oc % 2
                if wfree[wpar] is not None:
                    P.wait("gpsimd", wfree[wpar][0], wfree[wpar][1])
                P.dma("gpsimd", wbuf3[wpar][:, :],
                      wqkv_d[:, :, oc * 128:(oc + 1) * 128].transpose(
                          [1, 0, 2]), sw[wpar])
                wload = sw[wpar].v
                pr = 2 + oc % 2
                P.wait("tensor", sw[wpar], wload)
                if oc >= 2:
                    P.wait("tensor", mm_done[oc - 2][0], mm_done[oc - 2][1])
                for d in range(32):
                    last = (d == 31)
                    for q2 in range(2):
                        P.op("tensor", "matmul",
                             ppf(pr)[:, q2 * 512:(q2 + 1) * 512],
                             wbuf3[wpar][:, d * 128:(d + 1) * 128],
                             t2T_v[:, d, q2 * 512:(q2 + 1) * 512],
                             start=(d == 0), stop=last,
                             inc=(s_t if last and q2 == 1 else None))
                wfree[wpar] = (s_t, s_t.v)
                P.wait("scalar", s_t, s_t.v)
                P.op("scalar", "activation",
                     qkv_v[:, oc, h * 1024:(h + 1) * 1024], ppf(pr)[:, :],
                     AF.Identity, bias=bqkv[:, oc:oc + 1], scale=1.0,
                     inc=s_s)
                mm_done[oc] = (s_s, s_s.v)
        qkv_done = s_s.v
        A.release()

        A.mark()
        PT = A.bf16(128, 16 * 2048)
        kaug = A.bf16(128, 2048)
        qaug = A.bf16(128, 2048)
        rowbuf = A.f32(1, 2048)
        rowbuf2 = A.f32(1, 2048)
        rb_s = A.bf16(64, 2048)
        mq16 = A.f32(128, 16)
        m4 = A.f32(128, 4)
        m16n = A.f32(128, 16)
        mqTb = A.bf16(16, 128)

        P.wait("vector", s_s, qkv_done)
        P.op("vector", "memset", vtm[:, :], 1.0, inc=s_v)
        P.op("vector", "memset", kaug[64:65, :], 1.0, inc=s_v)
        for h8 in range(8):
            vch = 8 + h8 // 2
            rb = (h8 % 2) * 64
            P.wait("tensor", s_v, s_v.v)
            for kt2 in range(16):
                P.op("tensor", "transpose",
                     ppb(0)[:, kt2 * 64:(kt2 + 1) * 64],
                     qkv_v[rb:rb + 64, vch, kt2 * 128:(kt2 + 1) * 128],
                     identb[rb:rb + 64, rb:rb + 64],
                     inc=(s_t if kt2 == 15 else None))
            P.wait("vector", s_t, s_t.v)
            P.op("vector", "tensor_copy", vtm_v[:, h8, :, 0:64],
                 ppb(0)[:, 0:1024].rearrange("p (k c) -> p k c", k=16),
                 inc=s_v)
        P.wait("tensor", s_v, s_v.v)

        for h8 in range(8):
            qch = h8 // 2
            kch = 4 + h8 // 2
            rb = (h8 % 2) * 64
            P.wait("vector", s_t, s_t.v)
            P.op("vector", "tensor_copy", kaug[0:64, :],
                 qkv_v[rb:rb + 64, kch, :])
            P.chain("vector", s_v)
            P.op("vector", "tensor_copy", qaug[0:64, :],
                 qkv_v[rb:rb + 64, qch, :], inc=s_v)
            kq_staged = s_v.v
            P.wait("tensor", s_v, kq_staged)
            qtred = {}
            for qt in range(16):
                pr0 = (qt % 2) * 2
                if qt >= 2:
                    P.wait("tensor", qtred[qt - 2][0], qtred[qt - 2][1])
                for c in range(4):
                    P.op("tensor", "matmul",
                         ppf(pr0 + c // 2)[:, (c % 2) * 512:(c % 2 + 1) * 512],
                         qaug[0:64, qt * 128:(qt + 1) * 128],
                         kaug[0:64, c * 512:(c + 1) * 512],
                         start=True, stop=True,
                         inc=(s_t if c == 3 else None))
                P.wait("vector", s_t, s_t.v)
                for c in range(4):
                    P.op("vector", "tensor_reduce", m4[:, c:c + 1],
                         ppf(pr0 + c // 2)[:, (c % 2) * 512:(c % 2 + 1) * 512],
                         AX.X, ALU.max)
                P.chain("vector", s_v)
                P.op("vector", "tensor_reduce", mq16[:, qt:qt + 1], m4[:, :],
                     AX.X, ALU.max, inc=s_v)
                qtred[qt] = (s_v, s_v.v)
            P.op("vector", "tensor_scalar_mul", m16n[:, :], mq16[:, :], -1.0,
                 inc=s_v)
            P.wait("tensor", s_v, s_v.v)
            P.op("tensor", "transpose", ppf(0)[0:16, 0:128], m16n[:, :],
                 identf[:, :], inc=s_t)
            P.wait("vector", s_t, s_t.v)
            P.op("vector", "tensor_copy", mqTb[:, :], ppf(0)[0:16, 0:128],
                 inc=s_v)
            P.wait("sync", s_v, s_v.v)
            P.dma("sync", qaug[64:65, :], mqTb[0:16, :], sm)
            negm_done = sm.v
            P.wait("tensor", sm, negm_done)
            exp_done = {}
            for kt2 in range(16):
                for reg in range(2):
                    ri = kt2 * 2 + reg
                    pr = ri % 2
                    if ri >= 2:
                        P.wait("tensor", exp_done[ri - 2][0],
                               exp_done[ri - 2][1])
                    for q2 in range(2):
                        P.op("tensor", "matmul",
                             ppf(pr)[:, q2 * 512:(q2 + 1) * 512],
                             kaug[0:65, kt2 * 128:(kt2 + 1) * 128],
                             qaug[0:65, reg * 1024 + q2 * 512:
                                  reg * 1024 + (q2 + 1) * 512],
                             start=True, stop=True,
                             inc=(s_t if q2 == 1 else None))
                    P.wait("scalar", s_t, s_t.v)
                    P.op("scalar", "activation",
                         PT[:, kt2 * 2048 + reg * 1024:
                            kt2 * 2048 + (reg + 1) * 1024],
                         ppf(pr)[:, :], AF.Exp,
                         bias=zcol[:, 0:1], inc=s_s)
                    exp_done[ri] = (s_s, s_s.v)
                P.wait("tensor", exp_done[kt2 * 2 + 1][0],
                       exp_done[kt2 * 2 + 1][1])
                for qs in range(4):
                    P.op("tensor", "matmul",
                         ppf(2 + qs // 2)[0:65, (qs % 2) * 512:
                                          (qs % 2 + 1) * 512],
                         vtm_v[:, h8, kt2, :],
                         PT[:, kt2 * 2048 + qs * 512:
                            kt2 * 2048 + (qs + 1) * 512],
                         start=(kt2 == 0), stop=(kt2 == 15),
                         inc=(s_t if kt2 == 15 and qs == 3 else None))
            P.wait("vector", s_t, s_t.v)
            P.op("vector", "reciprocal", rowbuf[0:1, 0:1024],
                 ppf(2)[64:65, :])
            P.op("vector", "reciprocal", rowbuf[0:1, 1024:2048],
                 ppf(3)[64:65, :])
            P.chain("vector", s_v)
            P.op("vector", "tensor_scalar_mul", rowbuf2[0:1, :],
                 rowbuf[0:1, :], 1.0 / math.sqrt(512.0), inc=s_v)
            P.wait("tensor", s_v, s_v.v)
            for q2 in range(4):
                P.op("tensor", "matmul",
                     ppf(q2 // 2)[0:64, (q2 % 2) * 512:(q2 % 2 + 1) * 512],
                     onesrow[0:1, 0:64],
                     rowbuf2[0:1, q2 * 512:(q2 + 1) * 512],
                     start=True, stop=True,
                     inc=(s_t if q2 == 3 else None))
            P.wait("vector", s_t, s_t.v)
            P.op("vector", "tensor_copy", rb_s[:, 0:1024], ppf(0)[0:64, :])
            P.op("vector", "tensor_copy", rb_s[:, 1024:2048],
                 ppf(1)[0:64, :])
            P.chain("vector", s_v)
            P.op("vector", "tensor_tensor", attn_v[rb:rb + 64, h8 // 2,
                                                   0:1024],
                 ppf(2)[0:64, :], rb_s[0:64, 0:1024], ALU.mult)
            P.op("vector", "tensor_tensor", attn_v[rb:rb + 64, h8 // 2,
                                                   1024:2048],
                 ppf(3)[0:64, :], rb_s[0:64, 1024:2048], ALU.mult, inc=s_v)
            P.wait("tensor", s_v, s_v.v)
        A.release()

        A.mark()
        woT = A.bf16(128, 4 * D)
        woT_v = woT.rearrange("p (c o) -> p c o", c=4)
        bwo = A.bf16(128, D)
        wobs = A.f32(1, D)
        f_att = A.bf16(128, D)
        tmpb3 = A.bf16(128, D)

        for qeng in ("gpsimd", "sync"):
            P.wait(qeng, s_v, s_v.v)
            P.wait(qeng, s_t, s_t.v)
            P.wait(qeng, s_s, s_s.v)
        P.dma("gpsimd", woT[:, :], wo_d[:, :, :].transpose([1, 0, 2]), s_wb)
        wo_loaded = s_wb.v
        P.dma("sync", wobs[:, :], wob_d[:, :], sm)
        P.wait("tensor", sm, sm.v)
        for q2 in range(8):
            P.op("tensor", "matmul",
                 ppf(q2 // 2)[:, (q2 % 2) * 512:(q2 % 2 + 1) * 512],
                 onesrow[0:1, :], wobs[0:1, q2 * 512:(q2 + 1) * 512],
                 start=True, stop=True,
                 inc=(s_t if q2 % 2 == 1 else None))
            if q2 % 2 == 1:
                P.wait("vector", s_t, s_t.v)
                P.op("vector", "tensor_copy",
                     bwo[:, (q2 // 2) * 1024:(q2 // 2 + 1) * 1024],
                     ppf(q2 // 2)[:, :], inc=s_v)
        P.wait("tensor", s_v, s_v.v)
        P.wait("tensor", s_wb, wo_loaded)
        for i in range(NT):
            for ec in range(4):
                for os in range(8):
                    P.op("tensor", "matmul",
                         ppf(os // 2)[:, (os % 2) * 512:(os % 2 + 1) * 512],
                         attn_v[:, ec, i * 128:(i + 1) * 128],
                         woT_v[:, ec, os * 512:(os + 1) * 512],
                         start=(ec == 0), stop=(ec == 3),
                         inc=(s_t if ec == 3 and os == 7 else None))
            P.wait("vector", s_t, s_t.v)
            for pq in range(4):
                P.op("vector", "tensor_tensor",
                     f_att[:, pq * 1024:(pq + 1) * 1024], ppf(pq)[:, :],
                     bwo[:, pq * 1024:(pq + 1) * 1024], ALU.add,
                     inc=(s_v if pq == 3 else None))
            emit_boundary_tile(i, f_att, tmpb3, x2s_d, x3s_d, xl3, xo3,
                               x_free3, o_free3)
            P.wait("tensor", s_v, s_v.v)
        A.release()
        A.release()
        emit_barrier()
        if upto < 4:
            return finish()

        A.mark()
        qb = [A.i32(128, D), A.i32(128, D)]
        qf = A.bf16(128, 2048)
        mb = [A.bf16(128, D), A.bf16(128, D)]
        deq = [A.bf16(128, D), A.bf16(128, D)]
        wd = [A.bf16(128, D), A.bf16(128, D)]
        stg = [A.bf16(128, 2048), A.bf16(128, 2048)]
        scz = [A.f32(128, 65), A.f32(128, 65)]
        a1 = [A.f32(128, 32), A.f32(128, 32)]
        a2n = [A.f32(128, 32), A.f32(128, 32)]

        qfree_s = [None, None]
        qfree_v = [None, None]
        mfree = [None, None]
        sczfree = [None, None]
        a1free = [None, None]
        dfree = [None, None]
        wfree4 = [None, None]
        stgfree = [None, None]
        p4ld = {}

        def p4_load(kc):
            par = kc % 2
            if qfree_s[par] is not None:
                P.wait("sync", qfree_s[par][0], qfree_s[par][1])
            if qfree_v[par] is not None:
                P.wait("sync", qfree_v[par][0], qfree_v[par][1])
            P.dma("sync", qb[par][:, :], wq1_d[kc, :, :], sx[par])
            if sczfree[par] is not None:
                P.wait("sync", sczfree[par][0], sczfree[par][1])
            P.dma("sync", scz[par][:, 0:32], scales_d[kc, :, :], sm)
            P.dma("sync", scz[par][:, 32:64], zeros_d[kc, :, :], sm)
            P.dma("sync", scz[par][:, 64:65], scale2_d[kc, :, :], sm)
            if mfree[par] is not None:
                P.wait("gpsimd", mfree[par][0], mfree[par][1])
            P.dma("gpsimd", mb[par][:, :], mask1_d[kc, :, :], sw[par])
            p4ld[kc] = (sx[par].v, sm.v, sw[par].v)

        p4_load(0)
        p4_load(1)
        for kc in range(32):
            par = kc % 2
            qv, szv, swv = p4ld[kc]
            P.wait("vector", sm, szv)
            if a1free[par] is not None:
                P.wait("vector", a1free[par][0], a1free[par][1])
            P.op("vector", "tensor_scalar_mul", a1[par][:, :],
                 scz[par][:, 0:32], scz[par][:, 64:65])
            P.chain("vector", s_v)
            P.op("vector", "scalar_tensor_tensor", a2n[par][:, :],
                 scz[par][:, 32:64], -1.0, a1[par][:, :], ALU.mult, ALU.mult,
                 inc=s_v)
            a1_done = s_v.v
            sczfree[par] = (s_v, a1_done)
            P.wait("scalar", s_v, a1_done)
            P.wait("scalar", sx[par], qv)
            if dfree[par] is not None:
                P.wait("scalar", dfree[par][0], dfree[par][1])
            for g in range(16):
                P.op("scalar", "activation",
                     deq[par][:, g * 128:(g + 1) * 128],
                     qb[par][:, g * 128:(g + 1) * 128], AF.Identity,
                     scale=a1[par][:, g:g + 1], bias=a2n[par][:, g:g + 1],
                     inc=(s_s if g == 15 else None))
            sdeq_done = s_s.v
            qfree_s[par] = (s_s, sdeq_done)
            a1free[par] = (s_s, sdeq_done)
            P.wait("vector", sx[par], qv)
            P.op("vector", "tensor_copy", qf[:, :], qb[par][:, 2048:4096],
                 inc=s_v)
            qfree_v[par] = (s_v, s_v.v)
            for g in range(16):
                P.op("vector", "tensor_scalar",
                     deq[par][:, 2048 + g * 128:2048 + (g + 1) * 128],
                     qf[:, g * 128:(g + 1) * 128],
                     a1[par][:, 16 + g:17 + g], a2n[par][:, 16 + g:17 + g],
                     ALU.mult, ALU.add,
                     inc=(s_v if g == 15 else None))
            P.wait("vector", s_s, sdeq_done)
            P.wait("vector", sw[par], swv)
            if wfree4[par] is not None:
                P.wait("vector", wfree4[par][0], wfree4[par][1])
            P.op("vector", "tensor_tensor", wd[par][:, :], deq[par][:, :],
                 mb[par][:, :], ALU.mult, inc=s_v)
            wd_done = s_v.v
            dfree[par] = (s_v, wd_done)
            mfree[par] = (s_v, wd_done)
            if kc + 2 < 32:
                p4_load(kc + 2)

            P.wait("tensor", s_v, wd_done)
            for half in range(2):
                for j in range(16):
                    ncc = half * 16 + j
                    P.op("tensor", "transpose",
                         ppb(half)[:, j * 128:(j + 1) * 128],
                         wd[par][:, ncc * 128:(ncc + 1) * 128], identb[:, :],
                         inc=(s_t if j == 15 else None))
                P.wait("vector", s_t, s_t.v)
                if stgfree[half] is not None:
                    P.wait("vector", stgfree[half][0], stgfree[half][1])
                P.op("vector", "tensor_copy", stg[half][:, :],
                     ppb(half)[:, :], inc=s_v)
                P.wait("scalar", s_v, s_v.v)
                P.dma("scalar",
                      wdeqT_d[kc, :, half * 16:(half + 1) * 16, :],
                      stg[half][:, :].rearrange("p (a b) -> p a b", a=16),
                      sst[half])
                stgfree[half] = (sst[half], sst[half].v)
            wfree4[par] = (s_t, s_t.v)

        A.release()
        emit_barrier()
        if upto < 5:
            return finish()

        A.mark()
        t3T = A.bf16(128, 32 * 1024)
        t3T_v = t3T.rearrange("p (d t) -> p d t", d=32)
        f1 = A.bf16(128, 32 * 1024)
        f1_v = f1.rearrange("p (d t) -> p d t", d=32)
        wt5 = [A.bf16(128, 4096), A.bf16(128, 4096)]
        shared5 = A.bf16(128, 4096)
        xl5 = [A.bf16(128, D), A.bf16(128, D)]
        xo5 = [A.f32(128, D), A.f32(128, D)]

        x_free5 = [None, None]
        o_free5 = [None, None]
        wfree5 = [None, None]
        for h in range(2):
            emit_prep_half(x3s_d, h, xl5, shared5, t3T_v, x_free5)
            P.wait("tensor", s_v, s_v.v)
            P.wait("tensor", s_s, s_s.v)
            mm_done = {}
            for kc in range(32):
                wpar = kc % 2
                if wfree5[wpar] is not None:
                    P.wait("sync", wfree5[wpar][0], wfree5[wpar][1])
                P.dma("sync", wt5[wpar][:, :], wdeqT_d[kc, :, :, :], sy[wpar])
                wload = sy[wpar].v
                pr = 2 + kc % 2
                P.wait("tensor", sy[wpar], wload)
                if kc >= 2:
                    P.wait("tensor", mm_done[kc - 2][0], mm_done[kc - 2][1])
                for ncc in range(32):
                    last = (ncc == 31)
                    for q2 in range(2):
                        P.op("tensor", "matmul",
                             ppf(pr)[:, q2 * 512:(q2 + 1) * 512],
                             wt5[wpar][:, ncc * 128:(ncc + 1) * 128],
                             t3T_v[:, ncc, q2 * 512:(q2 + 1) * 512],
                             start=(ncc == 0), stop=last,
                             inc=(s_t if last and q2 == 1 else None))
                wfree5[wpar] = (s_t, s_t.v)
                P.wait("scalar", s_t, s_t.v)
                P.op("scalar", "activation", f1_v[:, kc, :], ppf(pr)[:, :],
                     AF.Identity, bias=b1fm[:, kc:kc + 1], scale=1.0,
                     inc=s_s)
                mm_done[kc] = (s_s, s_s.v)
            last_evac = mm_done[31]
            f_tm5 = shared5
            tmpb5 = wt5[0]
            for k in range(8):
                i = h * 8 + k
                emit_fm_to_tm(f1_v, k, f_tm5, last_evac[0], last_evac[1])
                par = i % 2
                if x_free5[par] is not None:
                    P.wait("sync", x_free5[par][0], x_free5[par][1])
                P.dma("sync", xl5[par][:, :],
                      x3s_d[i * 128:(i + 1) * 128, :], sx[par])
                xload = sx[par].v
                emit_ln_stats(f_tm5)
                P.op("vector", "tensor_scalar", tmpb5[:, :], f_tm5[:, :],
                     stats[:, MEAN:MEAN + 1], stats[:, RSTD:RSTD + 1],
                     ALU.subtract, ALU.mult)
                P.chain("vector", s_v)
                P.wait("vector", sx[par], xload)
                if o_free5[par] is not None:
                    P.wait("vector", o_free5[par][0], o_free5[par][1])
                P.op("vector", "tensor_add", xo5[par][:, :], xl5[par][:, :],
                     tmpb5[:, :], inc=s_v)
                x_free5[par] = (s_v, s_v.v)
                P.wait("scalar", s_v, s_v.v)
                P.dma("scalar", out_d[i * 128:(i + 1) * 128, :],
                      xo5[par][:, :], sst[par])
                o_free5[par] = (sst[par], sst[par].v)
            wfree5[0] = (s_v, s_v.v)

        A.release()
        return finish()


def _emit_block(nc, P):
    with nc.Block() as block:
        @block.sync
        def _(eng):
            for fn in P.q["sync"]:
                fn(eng)

        @block.scalar
        def _(eng):
            for fn in P.q["scalar"]:
                fn(eng)

        @block.vector
        def _(eng):
            for fn in P.q["vector"]:
                fn(eng)

        @block.tensor
        def _(eng):
            for fn in P.q["tensor"]:
                fn(eng)

        @block.gpsimd
        def _(eng):
            for fn in P.q["gpsimd"]:
                fn(eng)



def _prep_shared(inputs):
    C = _host_consts()
    g = lambda k: np.asarray(inputs[k])
    ac = np.ascontiguousarray

    vc = g("L0_attn_Vc_ri")
    wbig = np.concatenate([g("L0_ffn_Wq"), g("L0_ffn_Wk"),
                           g("L0_ffn_A").reshape(128, D),
                           g("L0_ffn_W")], axis=0)
    wqk = g("L1_attn_Wqk_w").reshape(8, 64, 2, D)
    wqkv = np.concatenate([ac(wqk[:, :, 0, :]).reshape(512, D),
                           ac(wqk[:, :, 1, :]).reshape(512, D),
                           g("L1_attn_Wv_w")], axis=0)
    bqk = g("L1_attn_Wqk_b").reshape(8, 64, 2)
    bqkv = np.concatenate([ac(bqk[:, :, 0]).reshape(512),
                           ac(bqk[:, :, 1]).reshape(512),
                           g("L1_attn_Wv_b")])
    shared = dict(
        lam=ac(g("L0_attn_Lambda_ri")),
        pri=ac(g("L0_attn_p_ri")),
        qri=ac(g("L0_attn_q_ri")),
        ctri=ac(g("L0_attn_Ct_ri").reshape(64, 2)),
        bp=ac(g("L0_attn_Bp")),
        dv=ac(g("L0_attn_D")),
        ls=ac(g("L0_attn_log_step")),
        vcre=ac(vc[:, :, 0]),
        vcim=ac(vc[:, :, 1]),
        wbig0T=ac(wbig.T).reshape(32, 128, 4736),
        b0fm=ac(g("L0_ffn_b").reshape(32, 128).T),
        bmflat=ac(g("L0_ffn_Bm").transpose(0, 2, 1).reshape(128, D)),
        wqkvT=ac(wqkv.T).reshape(32, 128, 1536),
        bqkv=ac(bqkv.reshape(12, 128).T),
        wo1T=ac(g("L1_attn_Wo_w").T).reshape(4, 128, D),
        wob=ac(g("L1_attn_Wo_b").reshape(1, D)),
        wq1=ac(g("L1_ffn_W_q")).reshape(32, 128, D),
        mask1=ac(g("L1_ffn_mask")).reshape(32, 128, D),
        scales1=ac(g("L1_ffn_scales")[:, :, 0]).reshape(32, 128, 32),
        zeros1=ac(g("L1_ffn_zeros")[:, :, 0]).reshape(32, 128, 32),
        scale2=ac(g("L1_ffn_scale2")).reshape(32, 128, 1),
        b1fm=ac(g("L1_ffn_bias").reshape(32, 128).T),
    )
    shared.update(C)
    return shared


def kernel(**inputs):
    shared = _prep_shared(inputs)
    x0 = np.asarray(inputs["L0_attn_u"])
    nc = build(upto=5, dev=False)
    in_maps = []
    for i in range(8):
        m = dict(shared)
        m["x0"] = np.ascontiguousarray(x0[i])
        in_maps.append(m)
    res = run_bass_kernel_spmd(nc, in_maps, core_ids=list(range(8)))
    return np.stack([res.results[i]["out"] for i in range(8)], axis=0)


# version 08
# speedup vs baseline: 1.0490x; 1.0081x over previous
import math
from contextlib import ExitStack

import numpy as np
import ml_dtypes

import concourse.bass as bass
import concourse.mybir as mybir
from concourse.bass_utils import run_bass_kernel_spmd

F32 = mybir.dt.float32
BF16 = mybir.dt.bfloat16
I32 = mybir.dt.int32
AF = mybir.ActivationFunctionType
ALU = mybir.AluOpType
AX = mybir.AxisListType

D = 4096
SEQ = 2048
NT = 16
EPS = 1e-5
NWORDS = 53200
P1_TILES = NT

ENGINES = ("sync", "scalar", "vector", "tensor", "gpsimd")



_CONST_CACHE = {}


def _host_consts():
    if _CONST_CACHE:
        return _CONST_CACHE
    L = D
    lv = np.arange(L, dtype=np.float64)
    om = np.exp(-2j * np.pi * (lv / L))
    g0 = (1.0 - om) / (1.0 + om)
    cc = 2.0 / (1.0 + om)
    G0b_re = np.broadcast_to(g0.real.astype(np.float32), (64, L)).copy()
    G0b_im = np.broadcast_to(g0.imag.astype(np.float32), (64, L)).copy()
    ccons = np.concatenate([
        cc.real.astype(np.float32).reshape(32, 128).T,
        cc.imag.astype(np.float32).reshape(32, 128).T], axis=1).copy()
    cidx = np.arange(32)
    F1 = np.exp(2j * np.pi * np.outer(cidx, cidx) / 32.0) / L
    fcons = np.ascontiguousarray(
        np.concatenate([F1.real, F1.imag, -F1.imag], axis=1).astype(np.float32))
    p_ = np.arange(128)
    TW = np.exp(2j * np.pi * np.outer(p_, cidx) / L)
    twcons = np.ascontiguousarray(
        np.concatenate([TW.real, TW.imag], axis=1).astype(np.float32))
    G2 = np.exp(2j * np.pi * np.outer(p_, p_) / 128.0)
    g2cons = np.ascontiguousarray(
        np.concatenate([G2.real, -G2.imag], axis=1).astype(np.float32))

    identb = np.eye(128, dtype=ml_dtypes.bfloat16)
    identf = np.eye(128, dtype=np.float32)
    onesrow = np.ones((1, 128), dtype=np.float32)
    e8 = np.zeros((128, 16), dtype=np.float32)
    for p in range(128):
        e8[p, p // 32] = 1.0 / math.sqrt(32.0)
        e8[p, 8 + 4 + p // 32] = 1.0 / math.sqrt(32.0)
    e8 = e8.astype(ml_dtypes.bfloat16)
    e16f = np.zeros((8, 128), dtype=np.float32)
    for er in range(128):
        e16f[er // 16, er] = 1.0

    _CONST_CACHE.update(dict(
        g0bre=G0b_re, g0bim=G0b_im, ccons=ccons, fcons=fcons, twcons=twcons,
        g2cons=g2cons, identb=identb, identf=identf, onesrow=onesrow,
        e8=e8, e16f=e16f,
    ))
    return _CONST_CACHE



class Sem:
    def __init__(self, ctx, nc, name):
        self.h = ctx.enter_context(nc.semaphore(name))
        self.v = 0


class Prog:

    def __init__(self, nc, ctx):
        self.nc = nc
        self.ctx = ctx
        self.q = {k: [] for k in ENGINES}
        self.last = {k: None for k in ENGINES}
        self.auto = {}
        self.waited = {k: {} for k in ENGINES}

    def sem(self, name):
        return Sem(self.ctx, self.nc, name)

    def wait(self, eng, sem, val):
        if val <= 0:
            return
        w = self.waited[eng]
        if w.get(id(sem), 0) >= val:
            return
        w[id(sem)] = val

        def run(e, s=sem.h, v=val):
            e.wait_ge(s, v)
        self.q[eng].append(run)
        self.last[eng] = None

    def op(self, eng, meth, *args, inc=None, incn=1, **kw):
        aser = self.auto.get(eng)
        if aser is not None:
            self.wait(eng, aser, aser.v)
            if meth not in ("dma_start", "sem_inc"):
                if inc is None:
                    inc = aser
                assert inc is aser, \
                    f"op({eng}) inc must be the engine self-sem"
        cell = {"t": None}
        if inc is not None:
            inc.v += incn
            cell["t"] = (inc.h, incn)

        def run(e, m=meth, a=args, k=kw, c=cell):
            ins = getattr(e, m)(*a, **k)
            if c["t"] is not None:
                ins.then_inc(*c["t"])
        self.q[eng].append(run)
        self.last[eng] = (cell, inc)

    def dma(self, eng, out, in_, inc, incn=16, **kw):
        self.op(eng, "dma_start", out=out, in_=in_, inc=inc, incn=incn, **kw)

    def chain(self, eng, sem):
        last = self.last[eng]
        assert last is not None, f"chain({eng}) with no preceding op"
        cell, inc = last
        if cell["t"] is None:
            sem.v += 1
            cell["t"] = (sem.h, 1)
        else:
            assert inc is sem, f"chain({eng}) after op with different inc"
        self.wait(eng, sem, sem.v)


class Arena:

    def __init__(self, ap, nwords):
        self.ap = ap
        self.n = nwords
        self.ptr = 0
        self.marks = []
        self.hi = 0

    def f32(self, parts, words):
        off = self.ptr
        self.ptr += words
        self.hi = max(self.hi, self.ptr)
        assert self.ptr <= self.n, f"arena overflow {self.ptr} > {self.n}"
        return self.ap[0:parts, off:off + words]

    def bf16(self, parts, n):
        w = (n + 1) // 2
        return self.f32(128, w).bitcast(BF16)[0:parts, 0:n]

    def i32(self, parts, n):
        return self.f32(128, n).bitcast(I32)[0:parts, 0:n]

    def mark(self):
        self.marks.append(self.ptr)

    def release(self):
        self.ptr = self.marks.pop()



def build(upto=5, dev=False):
    nc = bass.Bass("TRN2", target_bir_lowering=False, debug=False)

    def din(name, shape, dtype=F32):
        return nc.dram_tensor(name, list(shape), dtype,
                              kind="ExternalInput").ap()

    x0_d = din("x0", (SEQ, D))
    lam_d = din("lam", (64, 2))
    pri_d = din("pri", (64, 2))
    qri_d = din("qri", (64, 2))
    ctri_d = din("ctri", (64, 2))
    bp_d = din("bp", (64, 1))
    dv_d = din("dv", (1,))
    ls_d = din("ls", (1,))
    vcre_d = din("vcre", (64, 64))
    vcim_d = din("vcim", (64, 64))
    wbig_d = din("wbig0T", (32, 128, 4736))
    b0fm_d = din("b0fm", (128, 32))
    bmflat_d = din("bmflat", (128, D))
    wqkv_d = din("wqkvT", (32, 128, 1536))
    bqkv_d = din("bqkv", (128, 12))
    wo_d = din("wo1T", (4, 128, D))
    wob_d = din("wob", (1, D))
    wq1_d = din("wq1", (32, 128, D), I32)
    mask1_d = din("mask1", (32, 128, D), I32)
    scales_d = din("scales1", (32, 128, 32))
    zeros_d = din("zeros1", (32, 128, 32))
    scale2_d = din("scale2", (32, 128, 1))
    b1fm_d = din("b1fm", (128, 32))
    g0bre_d = din("g0bre", (64, D))
    g0bim_d = din("g0bim", (64, D))
    ccons_d = din("ccons", (128, 64))
    fcons_d = din("fcons", (32, 96))
    twcons_d = din("twcons", (128, 64))
    g2cons_d = din("g2cons", (128, 256))
    identb_d = din("identb", (128, 128), BF16)
    identf_d = din("identf", (128, 128))
    ones_d = din("onesrow", (1, 128))
    e8_d = din("e8", (128, 16), BF16)
    e16_d = din("e16f", (8, 128))

    out_d = nc.dram_tensor("out", [SEQ, D], F32, kind="ExternalOutput").ap()

    skind = "ExternalOutput" if dev else "Internal"
    x1s_d = nc.dram_tensor("x1s", [SEQ, D], BF16, kind=skind).ap()
    x2s_d = nc.dram_tensor("x2s", [SEQ, D], BF16, kind=skind).ap()
    x3s_d = nc.dram_tensor("x3s", [SEQ, D], BF16, kind=skind).ap()
    kpad_d = nc.dram_tensor("kpad", [4992], BF16, kind=skind).ap()
    wdeqT_d = nc.dram_tensor("wdeqT", [32, 128, 32, 128], BF16,
                             kind=skind).ap()

    ctx = ExitStack()
    with ctx:
        e_ = ctx.enter_context
        P = Prog(nc, ctx)

        arena_t = e_(nc.sbuf_tensor("arena", [128, NWORDS], F32))
        A = Arena(arena_t[:, :], NWORDS)

        pp = [e_(nc.psum_tensor("pp%d" % i, [128, 1024], F32))
              for i in range(4)]

        def ppf(i):
            return pp[i][:, :]

        def ppb(i):
            return pp[i][:].bitcast(BF16)

        sx = [P.sem("sx0"), P.sem("sx1")]
        sy = [P.sem("sy0"), P.sem("sy1")]
        sm = P.sem("sm")
        sw = [P.sem("sw0"), P.sem("sw1")]
        sv2 = [P.sem("sv2a"), P.sem("sv2b")]
        s_wb = P.sem("s_wb")
        sst = [P.sem("sst0"), P.sem("sst1")]
        s_t = P.sem("s_t")
        s_v = P.sem("s_v")
        s_s = P.sem("s_s")
        s_g = P.sem("s_g")
        sb = P.sem("sb")
        P.auto = {"vector": s_v, "scalar": s_s}

        identb = A.bf16(128, 128)
        identf = A.f32(128, 128)
        onesrow = A.f32(1, 128)
        e8 = A.bf16(128, 16)
        e16f = A.f32(8, 128)
        stats = A.f32(128, 16)
        bnbuf = A.f32(128, 48)
        zcol = A.f32(128, 1)
        dcol = A.f32(128, 1)
        b0fm = A.f32(128, 32)
        b1fm = A.f32(128, 32)
        bqkv = A.f32(128, 12)

        P.dma("sync", identb[:, :], identb_d[:, :], sm)
        P.dma("sync", identf[:, :], identf_d[:, :], sm)
        P.dma("sync", onesrow[:, :], ones_d[:, :], sm)
        P.dma("sync", e8[:, :], e8_d[:, :], sm)
        P.dma("sync", e16f[:, :], e16_d[:, :], sm)
        P.dma("sync", b0fm[:, :], b0fm_d[:, :], sm)
        P.dma("sync", b1fm[:, :], b1fm_d[:, :], sm)
        P.dma("sync", bqkv[:, :], bqkv_d[:, :], sm)
        P.op("vector", "memset", zcol[:, :], 0.0, inc=s_v)

        MEAN, VAR, VAREP, SQ, RSTD, NEGMX, SSUM, RSUM = range(8)

        def emit_ln_stats(x_ap):
            for c in range(8):
                P.op("vector", "bn_stats", bnbuf[:, 6 * c:6 * c + 6],
                     x_ap[:, c * 512:(c + 1) * 512])
            P.chain("vector", s_v)
            P.op("vector", "bn_aggr", stats[:, MEAN:MEAN + 2], bnbuf[:, 0:48])
            P.chain("vector", s_v)
            P.op("vector", "tensor_scalar_add", stats[:, VAREP:VAREP + 1],
                 stats[:, VAR:VAR + 1], EPS, inc=s_v)
            P.wait("scalar", s_v, s_v.v)
            P.op("scalar", "activation", stats[:, SQ:SQ + 1],
                 stats[:, VAREP:VAREP + 1], AF.Sqrt, bias=zcol[:, 0:1],
                 inc=s_s)
            P.wait("vector", s_s, s_s.v)
            P.op("vector", "reciprocal", stats[:, RSTD:RSTD + 1],
                 stats[:, SQ:SQ + 1])
            P.chain("vector", s_v)

        def emit_barrier():
            P.wait("sync", sx[0], sx[0].v)
            P.wait("sync", sx[1], sx[1].v)
            P.wait("sync", sm, sm.v)
            P.wait("vector", sv2[0], sv2[0].v)
            P.wait("vector", sv2[1], sv2[1].v)
            P.wait("scalar", sst[0], sst[0].v)
            P.wait("scalar", sst[1], sst[1].v)
            P.wait("gpsimd", sw[0], sw[0].v)
            P.wait("gpsimd", sw[1], sw[1].v)
            P.wait("gpsimd", s_wb, s_wb.v)
            P.wait("sync", sy[0], sy[0].v)
            P.wait("sync", sy[1], sy[1].v)
            P.wait("tensor", s_t, s_t.v)
            for eng in ("tensor", "vector", "scalar", "gpsimd"):
                P.op(eng, "sem_inc", sb.h, 1)
                sb.v += 1
            for eng in ENGINES:
                P.wait(eng, sb, sb.v)

        def finish():
            P.wait("scalar", sst[0], sst[0].v)
            P.wait("scalar", sst[1], sst[1].v)
            P.wait("vector", sv2[0], sv2[0].v)
            P.wait("vector", sv2[1], sv2[1].v)
            P.wait("gpsimd", sw[0], sw[0].v)
            P.wait("gpsimd", sw[1], sw[1].v)
            P.wait("gpsimd", s_wb, s_wb.v)
            P.wait("sync", sx[0], sx[0].v)
            P.wait("sync", sx[1], sx[1].v)
            P.wait("sync", sy[0], sy[0].v)
            P.wait("sync", sy[1], sy[1].v)
            P.wait("sync", sm, sm.v)
            _emit_block(nc, P)
            return nc

        A.mark()
        w3 = A.bf16(128, 4480)

        A.mark()
        g0re = A.f32(64, D)
        g0im = A.f32(64, D)
        dre = A.f32(64, D)
        dim = A.f32(64, D)
        m1 = A.f32(64, D)
        m2 = A.f32(64, D)
        c2 = A.f32(64, D)
        rcp = A.f32(64, D)
        rec_re = A.f32(64, D)
        rec_im = A.f32(64, D)
        smallp = A.f32(64, 16)
        vc = A.f32(64, 128)
        vcT = A.f32(64, 128)
        v8 = A.f32(64, 12)
        ccons = A.f32(128, 64)
        fcons = A.f32(32, 96)
        twcons = A.f32(128, 64)
        g2cons = A.f32(128, 256)
        kt = A.f32(128, 256)
        at2 = A.f32(128, 64)
        atT = A.f32(32, 256)
        b2 = A.f32(128, 64)
        ksb = A.f32(32, 128)
        kfinb = A.bf16(128, 32)
        zrow = A.bf16(1, 512)
        s4t = A.f32(64, 8)
        s4u = A.f32(128, 32 * 10)

        def U(j):
            return s4u[:, j * 32:(j + 1) * 32]

        P.dma("sync", g0re[:, :], g0bre_d[:, :], sm)
        P.dma("sync", g0im[:, :], g0bim_d[:, :], sm)
        P.dma("sync", smallp[:, 0:2], lam_d[:, :], sm)
        P.dma("sync", smallp[:, 2:4], pri_d[:, :], sm)
        P.dma("sync", smallp[:, 4:6], qri_d[:, :], sm)
        P.dma("sync", smallp[:, 6:8], ctri_d[:, :], sm)
        P.dma("sync", smallp[:, 8:9], bp_d[:, :], sm)
        P.dma("sync", smallp[0:1, 9:10], ls_d.unsqueeze(0), sm)
        P.dma("sync", smallp[0:1, 10:11], dv_d.unsqueeze(0), sm)
        P.dma("sync", vc[:, 0:64], vcre_d[:, :], sm)
        P.dma("sync", vc[:, 64:128], vcim_d[:, :], sm)
        P.dma("sync", ccons[:, :], ccons_d[:, :], sm)
        P.dma("sync", fcons[:, :], fcons_d[:, :], sm)
        P.dma("sync", twcons[:, :], twcons_d[:, :], sm)
        P.dma("sync", g2cons[:, :], g2cons_d[:, :], sm)
        P.op("vector", "memset", zrow[:, :], 0.0, inc=s_v)
        zmemset_done = s_v.v

        ld0 = sm.v
        P.wait("tensor", sm, ld0)
        P.wait("vector", sm, ld0)
        P.wait("scalar", sm, ld0)

        P.op("tensor", "matmul", ppf(0)[0:64, 0:1], onesrow[0:1, 0:64],
             smallp[0:1, 9:10], start=True, stop=True)
        P.op("tensor", "matmul", ppf(0)[0:128, 512:513], onesrow[0:1, 0:128],
             smallp[0:1, 10:11], start=True, stop=True, inc=s_t)
        P.wait("scalar", s_t, s_t.v)
        P.wait("scalar", s_v, zmemset_done)
        P.op("scalar", "activation", smallp[:, 11:12], ppf(0)[0:64, 0:1],
             AF.Exp, bias=zcol[0:64, 0:1], inc=s_s)
        P.wait("vector", s_s, s_s.v)
        P.wait("vector", s_t, s_t.v)
        P.op("vector", "tensor_copy", dcol[:, 0:1], ppf(0)[0:128, 512:513])
        P.op("vector", "reciprocal", smallp[:, 12:13], smallp[:, 11:12])
        P.chain("vector", s_v)
        P.op("vector", "tensor_scalar_mul", smallp[:, 13:14],
             smallp[:, 12:13], 2.0, inc=s_v)
        tds_done = s_v.v

        P.op("tensor", "transpose", ppf(1)[0:64, 0:64], vc[0:64, 0:64],
             identf[0:64, 0:64])
        P.op("tensor", "transpose", ppf(1)[0:64, 512:576], vc[0:64, 64:128],
             identf[0:64, 0:64], inc=s_t)
        P.wait("vector", s_t, s_t.v)
        P.op("vector", "tensor_copy", vcT[:, 0:64], ppf(1)[0:64, 0:64])
        P.op("vector", "tensor_copy", vcT[:, 64:128], ppf(1)[0:64, 512:576],
             inc=s_v)
        P.wait("tensor", s_v, s_v.v)
        P.op("tensor", "matmul", ppf(2)[0:64, 0:1], vcT[0:64, 0:64],
             smallp[:, 8:9], start=True, stop=True)
        P.op("tensor", "matmul", ppf(2)[0:64, 512:513], vcT[0:64, 64:128],
             smallp[:, 8:9], start=True, stop=True, inc=s_t)
        P.wait("vector", s_t, s_t.v)
        P.op("vector", "tensor_copy", smallp[:, 14:15], ppf(2)[0:64, 0:1])
        P.op("vector", "tensor_copy", smallp[:, 15:16],
             ppf(2)[0:64, 512:513], inc=s_v)

        prs = [(6, 7, 14, 15), (6, 7, 2, 3), (4, 5, 14, 15), (4, 5, 2, 3)]
        P.chain("vector", s_v)
        for i, (ar, ai, br, bi) in enumerate(prs):
            P.op("vector", "tensor_tensor", s4t[:, i:i + 1],
                 smallp[:, ai:ai + 1], smallp[:, bi:bi + 1], ALU.mult)
            P.op("vector", "tensor_tensor", s4t[:, 4 + i:5 + i],
                 smallp[:, ai:ai + 1], smallp[:, br:br + 1], ALU.mult)
        P.chain("vector", s_v)
        for i, (ar, ai, br, bi) in enumerate(prs):
            P.op("vector", "scalar_tensor_tensor", v8[:, i:i + 1],
                 smallp[:, ar:ar + 1], smallp[:, br:br + 1],
                 s4t[:, i:i + 1], ALU.mult, ALU.add)
            P.op("vector", "scalar_tensor_tensor", v8[:, 4 + i:5 + i],
                 smallp[:, ar:ar + 1], smallp[:, bi:bi + 1],
                 s4t[:, 4 + i:5 + i], ALU.mult, ALU.subtract)
        P.chain("vector", s_v)
        P.op("vector", "tensor_scalar_mul", v8[:, 8:12], v8[:, 4:8], -1.0)

        P.wait("vector", s_v, tds_done)
        P.op("vector", "tensor_scalar", dre[:, :], g0re[:, :],
             smallp[:, 13:14], smallp[:, 0:1], ALU.mult, ALU.subtract)
        P.op("vector", "tensor_scalar", dim[:, :], g0im[:, :],
             smallp[:, 13:14], smallp[:, 1:2], ALU.mult, ALU.subtract)
        P.op("vector", "tensor_tensor", m1[:, :], dre[:, :], dre[:, :],
             ALU.mult)
        P.op("vector", "tensor_tensor", m2[:, :], dim[:, :], dim[:, :],
             ALU.mult)
        P.chain("vector", s_v)
        P.op("vector", "tensor_tensor", c2[:, :], m1[:, :], m2[:, :], ALU.add)
        P.chain("vector", s_v)
        P.op("vector", "reciprocal", rcp[:, :], c2[:, :])
        P.chain("vector", s_v)
        P.op("vector", "tensor_tensor", rec_re[:, :], dre[:, :], rcp[:, :],
             ALU.mult)
        P.op("vector", "scalar_tensor_tensor", rec_im[:, :], dim[:, :], -1.0,
             rcp[:, :], ALU.mult, ALU.mult, inc=s_v)
        rec_done = s_v.v

        P.wait("tensor", s_v, rec_done)
        evac_v = {}
        for c in range(32):
            bank = ppf(1 + (c % 2))
            if c >= 2:
                P.wait("tensor", s_v, evac_v[c - 2])
            P.op("tensor", "matmul", bank[0:128, 0:4],
                 rec_re[:, c * 128:(c + 1) * 128], v8[:, 0:4],
                 start=True, stop=False)
            P.op("tensor", "matmul", bank[0:128, 0:4],
                 rec_im[:, c * 128:(c + 1) * 128], v8[:, 8:12],
                 start=False, stop=True)
            P.op("tensor", "matmul", bank[0:128, 4:8],
                 rec_im[:, c * 128:(c + 1) * 128], v8[:, 0:4],
                 start=True, stop=False)
            P.op("tensor", "matmul", bank[0:128, 4:8],
                 rec_re[:, c * 128:(c + 1) * 128], v8[:, 4:8],
                 start=False, stop=True, inc=s_t)
            P.wait("vector", s_t, s_t.v)
            P.op("vector", "tensor_copy", kt[:, c::32],
                 bank[0:128, 0:8], inc=s_v)
            evac_v[c] = s_v.v

        k00re, k01re, k10re, k11re = (kt[:, 0:32], kt[:, 32:64],
                                      kt[:, 64:96], kt[:, 96:128])
        k00im, k01im, k10im, k11im = (kt[:, 128:160], kt[:, 160:192],
                                      kt[:, 192:224], kt[:, 224:256])
        P.chain("vector", s_v)
        P.op("vector", "tensor_scalar_add", U(9), k11re, 1.0)
        P.op("vector", "tensor_tensor", U(0), k01re, k10re, ALU.mult)
        P.op("vector", "tensor_tensor", U(1), k01im, k10im, ALU.mult)
        P.op("vector", "tensor_tensor", U(2), k01re, k10im, ALU.mult)
        P.op("vector", "tensor_tensor", U(3), k01im, k10re, ALU.mult)
        P.chain("vector", s_v)
        P.op("vector", "tensor_tensor", U(4), U(0), U(1), ALU.subtract)
        P.op("vector", "tensor_tensor", U(5), U(2), U(3), ALU.add)
        P.op("vector", "tensor_tensor", U(0), U(9), U(9), ALU.mult)
        P.op("vector", "tensor_tensor", U(1), k11im, k11im, ALU.mult)
        P.chain("vector", s_v)
        P.op("vector", "tensor_tensor", U(2), U(0), U(1), ALU.add)
        P.chain("vector", s_v)
        P.op("vector", "reciprocal", U(3), U(2))
        P.op("vector", "tensor_tensor", U(6), U(4), U(9), ALU.mult)
        P.op("vector", "tensor_tensor", U(7), U(5), k11im, ALU.mult)
        P.op("vector", "tensor_tensor", U(8), U(5), U(9), ALU.mult)
        P.op("vector", "tensor_tensor", U(9), U(4), k11im, ALU.mult)
        P.chain("vector", s_v)
        P.op("vector", "tensor_tensor", U(6), U(6), U(7), ALU.add)
        P.op("vector", "tensor_tensor", U(8), U(8), U(9), ALU.subtract)
        P.chain("vector", s_v)
        P.op("vector", "tensor_tensor", U(6), U(6), U(3), ALU.mult)
        P.op("vector", "tensor_tensor", U(8), U(8), U(3), ALU.mult)
        P.chain("vector", s_v)
        P.op("vector", "tensor_tensor", U(0), k00re, U(6), ALU.subtract)
        P.op("vector", "tensor_tensor", U(1), k00im, U(8), ALU.subtract)
        P.chain("vector", s_v)
        P.op("vector", "tensor_tensor", U(2), ccons[:, 0:32], U(0), ALU.mult)
        P.op("vector", "tensor_tensor", U(3), ccons[:, 32:64], U(1), ALU.mult)
        P.op("vector", "tensor_tensor", U(4), ccons[:, 0:32], U(1), ALU.mult)
        P.op("vector", "tensor_tensor", U(5), ccons[:, 32:64], U(0), ALU.mult)
        P.chain("vector", s_v)
        P.op("vector", "tensor_tensor", at2[:, 0:32], U(2), U(3),
             ALU.subtract)
        P.op("vector", "tensor_tensor", at2[:, 32:64], U(4), U(5), ALU.add,
             inc=s_v)

        P.wait("tensor", s_v, s_v.v)
        P.op("tensor", "transpose", ppf(0)[0:32, 0:128], at2[:, 0:32],
             identf[:, :])
        P.op("tensor", "transpose", ppf(0)[0:32, 512:640], at2[:, 32:64],
             identf[:, :], inc=s_t)
        P.wait("vector", s_t, s_t.v)
        P.op("vector", "tensor_copy", atT[:, 0:128], ppf(0)[0:32, 0:128])
        P.op("vector", "tensor_copy", atT[:, 128:256], ppf(0)[0:32, 512:640],
             inc=s_v)
        P.wait("tensor", s_v, s_v.v)
        P.op("tensor", "matmul", ppf(1)[0:128, 0:32], atT[0:32, 0:128],
             fcons[:, 0:32], start=True, stop=False)
        P.op("tensor", "matmul", ppf(1)[0:128, 0:32], atT[0:32, 128:256],
             fcons[:, 64:96], start=False, stop=True)
        P.op("tensor", "matmul", ppf(1)[0:128, 32:64], atT[0:32, 0:128],
             fcons[:, 32:64], start=True, stop=False)
        P.op("tensor", "matmul", ppf(1)[0:128, 32:64], atT[0:32, 128:256],
             fcons[:, 0:32], start=False, stop=True, inc=s_t)
        P.wait("vector", s_t, s_t.v)
        P.op("vector", "tensor_tensor", U(0), ppf(1)[0:128, 0:32],
             twcons[:, 0:32], ALU.mult)
        P.op("vector", "tensor_tensor", U(1), ppf(1)[0:128, 32:64],
             twcons[:, 32:64], ALU.mult)
        P.op("vector", "tensor_tensor", U(2), ppf(1)[0:128, 0:32],
             twcons[:, 32:64], ALU.mult)
        P.op("vector", "tensor_tensor", U(3), ppf(1)[0:128, 32:64],
             twcons[:, 0:32], ALU.mult)
        P.chain("vector", s_v)
        P.op("vector", "tensor_tensor", b2[:, 0:32], U(0), U(1), ALU.subtract)
        P.op("vector", "tensor_tensor", b2[:, 32:64], U(2), U(3), ALU.add,
             inc=s_v)
        P.wait("tensor", s_v, s_v.v)
        P.op("tensor", "matmul", ppf(2)[0:32, 0:128], b2[:, 0:32],
             g2cons[:, 0:128], start=True, stop=False)
        P.op("tensor", "matmul", ppf(2)[0:32, 0:128], b2[:, 32:64],
             g2cons[:, 128:256], start=False, stop=True, inc=s_t)
        P.wait("vector", s_t, s_t.v)
        P.op("vector", "tensor_copy", ksb[:, :], ppf(2)[0:32, 0:128],
             inc=s_v)
        P.wait("tensor", s_v, s_v.v)
        P.op("tensor", "transpose", ppf(3)[0:128, 0:32], ksb[0:32, :],
             identf[0:32, 0:32], inc=s_t)
        P.wait("vector", s_t, s_t.v)
        P.op("vector", "tensor_copy", kfinb[:, :], ppf(3)[0:128, 0:32],
             inc=s_v)

        if dev:
            dbg_d = nc.dram_tensor("dbg", [128, 3072], F32,
                                   kind="ExternalOutput").ap()
            P.wait("scalar", s_v, s_v.v)
            P.dma("scalar", dbg_d[:, 0:256], kt[:, :], sst[1])
            P.dma("scalar", dbg_d[:, 256:320], at2[:, :], sst[1])
            P.dma("scalar", dbg_d[0:64, 320:332], v8[:, :], sst[1])
            P.dma("scalar", dbg_d[0:64, 332:348], smallp[:, :], sst[1])
            P.dma("scalar", dbg_d[0:64, 348:476], vcT[:, :], sst[1])
            P.dma("scalar", dbg_d[0:32, 512:768], atT[:, :], sst[1])
            P.dma("scalar", dbg_d[0:32, 768:896], ksb[:, :], sst[1])
            P.dma("scalar", dbg_d[:, 896:960], b2[:, :], sst[1])
            P.dma("scalar", dbg_d[0:64, 1024:2048], rec_re[:, 0:1024],
                  sst[1])
            P.dma("scalar", dbg_d[0:64, 2048:3072], rec_im[:, 0:1024],
                  sst[1])

        P.wait("scalar", s_v, s_v.v)
        P.dma("scalar", kpad_d[0:511].unsqueeze(0), zrow[0:1, 0:511], sst[0])
        P.dma("scalar", kpad_d[4607:4992].unsqueeze(0), zrow[0:1, 0:385],
              sst[0])
        P.dma("scalar", kpad_d[511:4607], kfinb[:, :], sst[0])
        P.wait("sync", sst[0], sst[0].v)
        for r in range(128):
            P.dma("sync", w3[r:r + 1, :],
                  kpad_d[511 - r:511 - r + 4480].unsqueeze(0), sm)
        w3_loaded = sm.v

        A.release()
        emit_barrier()
        if upto < 1:
            return finish()

        A.mark()
        xl1 = [A.f32(128, D), A.f32(128, D)]
        xo1 = [A.bf16(128, D), A.bf16(128, D)]
        u2 = [A.bf16(128, D), A.bf16(128, D)]
        uT2 = [A.bf16(128, D), A.bf16(128, D)]
        z_bf = A.bf16(128, D)
        tmp1 = A.bf16(128, D)

        P.wait("tensor", sm, w3_loaded)
        xfree = [None, None]
        ofree = [None, None]
        ufree = [None, None]
        uT_done = [0, 0]
        u_done = [0, 0]
        tp_half = [[0, 0], [0, 0]]

        def p1_prep(i):
            par = i % 2
            if xfree[par] is not None:
                P.wait("sync", xfree[par][0], xfree[par][1])
            P.dma("sync", xl1[par][:, :], x0_d[i * 128:(i + 1) * 128, :],
                  sx[par])
            P.wait("vector", sx[par], sx[par].v)
            emit_ln_stats(xl1[par])
            if ufree[par] is not None:
                P.wait("vector", ufree[par][0], ufree[par][1])
            P.op("vector", "tensor_scalar", u2[par][:, :], xl1[par][:, :],
                 stats[:, MEAN:MEAN + 1], stats[:, RSTD:RSTD + 1],
                 ALU.subtract, ALU.mult, inc=s_v)
            u_done[par] = s_v.v

        def p1_transposes(i):
            par = i % 2
            P.wait("tensor", s_v, u_done[par])
            for half in range(2):
                for j in range(16):
                    dd = half * 16 + j
                    P.op("tensor", "transpose",
                         ppb(half)[:, j * 128:(j + 1) * 128],
                         u2[par][:, dd * 128:(dd + 1) * 128], identb[:, :],
                         inc=(s_t if j == 15 else None))
                tp_half[par][half] = s_t.v
            ufree[par] = (s_t, s_t.v)

        def p1_ut_copies(i):
            par = i % 2
            for half in range(2):
                P.wait("vector", s_t, tp_half[par][half])
                P.op("vector", "tensor_copy",
                     uT2[par][:, half * 2048:(half + 1) * 2048],
                     ppb(half)[:, :], inc=s_v)
            uT_done[par] = s_v.v

        p1_prep(0)
        p1_transposes(0)
        p1_ut_copies(0)
        for i in range(P1_TILES):
            par = i % 2
            P.wait("tensor", s_v, uT_done[par])
            stop_cnt = {}
            for j0 in range(32):
                for c in range(j0 // 4, 8):
                    bank = ppf(c // 2)[:, (c % 2) * 512:(c % 2 + 1) * 512]
                    last = (j0 == 4 * c + 3)
                    off = 384 + 512 * c - 128 * j0
                    P.op("tensor", "matmul", bank,
                         uT2[par][:, j0 * 128:(j0 + 1) * 128],
                         w3[:, off:off + 512],
                         start=(j0 == 0), stop=last,
                         inc=(s_t if last else None))
                    if last:
                        stop_cnt[c] = s_t.v

            def p1_epi_chunk(c):
                bank = ppf(c // 2)[:, (c % 2) * 512:(c % 2 + 1) * 512]
                P.wait("vector", s_t, stop_cnt[c])
                P.op("vector", "scalar_tensor_tensor",
                     z_bf[:, c * 512:(c + 1) * 512],
                     u2[par][:, c * 512:(c + 1) * 512], dcol[:, 0:1], bank,
                     ALU.mult, ALU.add, inc=s_v)

            for c in range(4):
                p1_epi_chunk(c)
            if i + 1 < P1_TILES:
                p1_prep(i + 1)
                p1_transposes(i + 1)
            for c in range(4, 8):
                p1_epi_chunk(c)
            if i + 1 < P1_TILES:
                p1_ut_copies(i + 1)
            emit_ln_stats(z_bf)
            P.op("vector", "tensor_scalar", tmp1[:, :], z_bf[:, :],
                 stats[:, MEAN:MEAN + 1], stats[:, RSTD:RSTD + 1],
                 ALU.subtract, ALU.mult)
            P.chain("vector", s_v)
            if ofree[par] is not None:
                P.wait("vector", ofree[par][0], ofree[par][1])
            P.op("vector", "tensor_add", xo1[par][:, :], xl1[par][:, :],
                 tmp1[:, :], inc=s_v)
            xfree[par] = (s_v, s_v.v)
            P.wait("scalar", s_v, s_v.v)
            P.dma("scalar", x1s_d[i * 128:(i + 1) * 128, :], xo1[par][:, :],
                  sst[par])
            ofree[par] = (sst[par], sst[par].v)

        if dev:
            dbg1_d = nc.dram_tensor("dbg1", [128, 3 * D], BF16,
                                    kind="ExternalOutput").ap()
            P.wait("scalar", s_v, s_v.v)
            P.dma("scalar", dbg1_d[:, 0:D], u2[0][:, :], sst[1])
            P.dma("scalar", dbg1_d[:, D:2 * D], uT2[0][:, :], sst[1])
            P.dma("scalar", dbg1_d[:, 2 * D:3 * D], z_bf[:, :], sst[1])

        A.release()
        A.release()
        emit_barrier()
        if upto < 2:
            return finish()

        def emit_prep_half(xsrc_d, h, xl, tbuf, tT_v, x_free):
            P.wait("vector", s_t, s_t.v)
            for k in range(8):
                i = h * 8 + k
                par = k % 2
                if x_free[par] is not None:
                    P.wait("sync", x_free[par][0], x_free[par][1])
                P.dma("sync", xl[par][:, :],
                      xsrc_d[i * 128:(i + 1) * 128, :], sx[par])
                xload = sx[par].v
                P.wait("vector", sx[par], xload)
                emit_ln_stats(xl[par])
                if k > 0:
                    P.wait("vector", s_t, tp_done)
                P.op("vector", "tensor_scalar", tbuf[:, :], xl[par][:, :],
                     stats[:, MEAN:MEAN + 1], stats[:, RSTD:RSTD + 1],
                     ALU.subtract, ALU.mult, inc=s_v)
                x_free[par] = (s_v, s_v.v)
                P.wait("tensor", s_v, s_v.v)
                P.wait("tensor", s_s, s_s.v)
                for half in range(2):
                    for j in range(16):
                        dd = half * 16 + j
                        P.op("tensor", "transpose",
                             ppb(half)[:, j * 128:(j + 1) * 128],
                             tbuf[:, dd * 128:(dd + 1) * 128], identb[:, :],
                             inc=(s_t if j == 15 else None))
                    tp_done = s_t.v
                    ceng = "scalar" if half == 0 else "vector"
                    P.wait(ceng, s_t, tp_done)
                    if half == 0:
                        P.op("scalar", "activation",
                             tT_v[:, 0:16, k * 128:(k + 1) * 128],
                             ppb(0)[:, :].rearrange("p (a b) -> p a b", a=16),
                             AF.Identity, bias=zcol[:, 0:1], scale=1.0,
                             inc=s_s)
                    else:
                        P.op("vector", "tensor_copy",
                             tT_v[:, 16:32, k * 128:(k + 1) * 128],
                             ppb(1)[:, :].rearrange("p (a b) -> p a b", a=16),
                             inc=s_v)

        def emit_fm_to_tm(fv, k, f_tm, src_ready_sem, src_ready_val):
            P.wait("tensor", src_ready_sem, src_ready_val)
            P.wait("tensor", s_v, s_v.v)
            P.wait("tensor", s_s, s_s.v)
            for half in range(2):
                for j in range(16):
                    oc = half * 16 + j
                    P.op("tensor", "transpose",
                         ppb(half)[:, j * 128:(j + 1) * 128],
                         fv[:, oc, k * 128:(k + 1) * 128], identb[:, :],
                         inc=(s_t if j == 15 else None))
                if half == 0:
                    P.wait("scalar", s_t, s_t.v)
                    P.op("scalar", "activation", f_tm[:, 0:2048],
                         ppb(0)[:, :], AF.Identity, bias=zcol[:, 0:1],
                         scale=1.0, inc=s_s)
                else:
                    P.wait("vector", s_t, s_t.v)
                    P.op("vector", "tensor_copy", f_tm[:, 2048:4096],
                         ppb(1)[:, :], inc=s_v)

        def emit_boundary_tile(i, f_tm, tmpb, xsrc_d, xdst_d, xl, xo,
                               x_free, o_free):
            par = i % 2
            if x_free[par] is not None:
                P.wait("sync", x_free[par][0], x_free[par][1])
            P.dma("sync", xl[par][:, :], xsrc_d[i * 128:(i + 1) * 128, :],
                  sx[par])
            xload = sx[par].v
            P.wait("vector", s_s, s_s.v)
            emit_ln_stats(f_tm)
            P.op("vector", "tensor_scalar", tmpb[:, :], f_tm[:, :],
                 stats[:, MEAN:MEAN + 1], stats[:, RSTD:RSTD + 1],
                 ALU.subtract, ALU.mult)
            P.chain("vector", s_v)
            P.wait("vector", sx[par], xload)
            if o_free[par] is not None:
                P.wait("vector", o_free[par][0], o_free[par][1])
            P.op("vector", "tensor_add", xo[par][:, :], xl[par][:, :],
                 tmpb[:, :], inc=s_v)
            x_free[par] = (s_v, s_v.v)
            P.wait("scalar", s_v, s_v.v)
            P.dma("scalar", xdst_d[i * 128:(i + 1) * 128, :], xo[par][:, :],
                  sst[par])
            o_free[par] = (sst[par], sst[par].v)

        A.mark()
        t1T = A.bf16(128, 32 * 1024)
        t1T_v = t1T.rearrange("p (d t) -> p d t", d=32)
        fbuf = A.bf16(128, 32 * 1024)
        fbuf_v = fbuf.rearrange("p (d t) -> p d t", d=32)
        wbuf = [A.bf16(128, 4096), A.bf16(128, 4096)]
        shared = A.bf16(128, 5120)
        hprime = A.bf16(128, 1024)
        qk = A.bf16(128, 1024)
        s8 = A.f32(8, 1024)
        wrow = s8
        stw = A.f32(128, 192)
        bmflat = A.bf16(128, D)
        xl2 = [A.bf16(128, D), A.bf16(128, D)]
        xo2 = [A.bf16(128, D), A.bf16(128, D)]
        qkh = shared

        P.dma("gpsimd", bmflat[:, :], bmflat_d[:, :], s_wb)
        bm_loaded = s_wb.v
        x_free2 = [None, None]
        o_free2 = [None, None]
        wfree = [None, None]

        def p2_scores():
            stm = stw[:, 0:64]
            wtm = stw[:, 64:128]
            wsc = stw[:, 128:192]
            for hq in range(2):
                if hq == 1:
                    P.wait("vector", s_t, s_t.v)
                P.op("vector", "tensor_tensor", qk[:, :],
                     qkh[:, hq * 1024:(hq + 1) * 1024],
                     qkh[:, (2 + hq) * 1024:(3 + hq) * 1024], ALU.mult,
                     inc=s_v)
                P.wait("tensor", s_v, s_v.v)
                for q2 in range(2):
                    P.op("tensor", "matmul",
                         ppf(0)[0:8, q2 * 512:(q2 + 1) * 512],
                         e8[:, hq * 8:(hq + 1) * 8],
                         qk[:, q2 * 512:(q2 + 1) * 512],
                         start=(hq == 0), stop=(hq == 1),
                         inc=(s_t if q2 == 1 else None))
            P.wait("vector", s_t, s_t.v)
            P.op("vector", "tensor_copy", s8[:, :], ppf(0)[0:8, :], inc=s_v)
            P.wait("tensor", s_v, s_v.v)
            for k in range(8):
                P.op("tensor", "transpose", ppf(1)[0:128, k * 8:(k + 1) * 8],
                     s8[0:8, k * 128:(k + 1) * 128], identf[0:8, 0:8],
                     inc=(s_t if k == 7 else None))
            P.wait("vector", s_t, s_t.v)
            P.op("vector", "tensor_copy", stm[:, :], ppf(1)[0:128, 0:64],
                 inc=s_v)
            for k in range(8):
                P.chain("vector", s_v)
                P.op("vector", "tensor_reduce", stats[:, NEGMX:NEGMX + 1],
                     stm[:, k * 8:(k + 1) * 8], AX.X, ALU.max, negate=True,
                     inc=s_v)
                P.wait("scalar", s_v, s_v.v)
                P.op("scalar", "activation", wtm[:, k * 8:(k + 1) * 8],
                     stm[:, k * 8:(k + 1) * 8], AF.Exp,
                     bias=stats[:, NEGMX:NEGMX + 1],
                     accum_out=stats[:, SSUM:SSUM + 1], inc=s_s)
                P.wait("vector", s_s, s_s.v)
                P.op("vector", "reciprocal", stats[:, RSUM:RSUM + 1],
                     stats[:, SSUM:SSUM + 1])
                P.chain("vector", s_v)
                P.op("vector", "tensor_scalar_mul", wsc[:, k * 8:(k + 1) * 8],
                     wtm[:, k * 8:(k + 1) * 8], stats[:, RSUM:RSUM + 1],
                     inc=s_v)
            P.wait("tensor", s_v, s_v.v)
            for k in range(8):
                P.op("tensor", "transpose",
                     ppf(0)[0:8, k * 128:(k + 1) * 128],
                     wsc[:, k * 8:(k + 1) * 8], identf[:, :],
                     inc=(s_t if k == 7 else None))
            P.wait("vector", s_t, s_t.v)
            P.op("vector", "tensor_copy", wrow[:, :], ppf(0)[0:8, :],
                 inc=s_v)
            P.wait("tensor", s_v, s_v.v)
            for q2 in range(2):
                P.op("tensor", "matmul", ppf(1)[:, q2 * 512:(q2 + 1) * 512],
                     e16f[0:8, :], wrow[0:8, q2 * 512:(q2 + 1) * 512],
                     start=True, stop=True, inc=(s_t if q2 == 1 else None))
            P.wait("vector", s_t, s_t.v)
            P.op("vector", "tensor_tensor", hprime[:, :],
                 qkh[:, 4 * 1024:5 * 1024], ppf(1)[:, :], ALU.mult, inc=s_v)
            return s_v.v

        for h in range(2):
            t1b = shared[:, 0:4096]
            emit_prep_half(x1s_d, h, xl2, t1b, t1T_v, x_free2)
            P.wait("tensor", s_v, s_v.v)
            P.wait("tensor", s_s, s_s.v)
            hprime_done = None
            mm_done = {}
            for oc in range(37):
                wpar = oc % 2
                if wfree[wpar] is not None:
                    P.wait("gpsimd", wfree[wpar][0], wfree[wpar][1])
                P.dma("gpsimd", wbuf[wpar][:, :],
                      wbig_d[:, :, oc * 128:(oc + 1) * 128].transpose(
                          [1, 0, 2]), sw[wpar])
                wload = sw[wpar].v
                pr = 2 + oc % 2
                P.wait("tensor", sw[wpar], wload)
                if oc >= 2:
                    P.wait("tensor", mm_done[oc - 2][0], mm_done[oc - 2][1])
                for d in range(32):
                    laststop = (d == 31 and oc < 5)
                    for q2 in range(2):
                        P.op("tensor", "matmul",
                             ppf(pr)[:, q2 * 512:(q2 + 1) * 512],
                             wbuf[wpar][:, d * 128:(d + 1) * 128],
                             t1T_v[:, d, q2 * 512:(q2 + 1) * 512],
                             start=(d == 0), stop=laststop,
                             inc=(s_t if laststop and q2 == 1 else None))
                if oc >= 5:
                    if oc == 5:
                        P.wait("tensor", s_v, hprime_done)
                        P.wait("tensor", s_wb, bm_loaded)
                    for q2 in range(2):
                        P.op("tensor", "matmul",
                             ppf(pr)[:, q2 * 512:(q2 + 1) * 512],
                             bmflat[:, (oc - 5) * 128:(oc - 4) * 128],
                             hprime[:, q2 * 512:(q2 + 1) * 512],
                             start=False, stop=True,
                             inc=(s_t if q2 == 1 else None))
                wfree[wpar] = (s_t, s_t.v)
                mm_t = s_t.v
                if oc < 5:
                    P.wait("vector", s_t, mm_t)
                    P.op("vector", "tensor_copy",
                         qkh[:, oc * 1024:(oc + 1) * 1024], ppf(pr)[:, :],
                         inc=s_v)
                    mm_done[oc] = (s_v, s_v.v)
                else:
                    P.wait("scalar", s_t, mm_t)
                    P.op("scalar", "activation", fbuf_v[:, oc - 5, :],
                         ppf(pr)[:, :], AF.Identity,
                         bias=b0fm[:, oc - 5:oc - 4], scale=1.0, inc=s_s)
                    mm_done[oc] = (s_s, s_s.v)
                if oc == 4:
                    P.wait("vector", mm_done[4][0], mm_done[4][1])
                    hprime_done = p2_scores()
            last_evac = mm_done[36]
            f_tm = shared[:, 0:4096]
            tmpb2 = wbuf[0]
            for k in range(8):
                emit_fm_to_tm(fbuf_v, k, f_tm, last_evac[0], last_evac[1])
                emit_boundary_tile(h * 8 + k, f_tm, tmpb2, x1s_d, x2s_d,
                                   xl2, xo2, x_free2, o_free2)
            wfree[0] = (s_v, s_v.v)

        A.release()
        emit_barrier()
        if upto < 3:
            return finish()

        A.mark()
        qkv = A.bf16(128, 12 * 2048)
        qkv_v = qkv.rearrange("p (c t) -> p c t", c=12)
        attn_cat = A.bf16(128, 4 * 2048)
        attn_v = attn_cat.rearrange("p (c t) -> p c t", c=4)
        vtm = A.bf16(128, 8 * 16 * 65)
        vtm_v = vtm.rearrange("p (h k c) -> p h k c", h=8, k=16)
        xl3 = [A.bf16(128, D), A.bf16(128, D)]
        xo3 = [A.bf16(128, D), A.bf16(128, D)]

        x_free3 = [None, None]
        o_free3 = [None, None]

        A.mark()
        t2T = A.bf16(128, 32 * 1024)
        t2T_v = t2T.rearrange("p (d t) -> p d t", d=32)
        wbuf3 = [A.bf16(128, 4096), A.bf16(128, 4096)]
        t2b = A.bf16(128, 4096)
        qkv_done = None
        for h in range(2):
            emit_prep_half(x2s_d, h, xl3, t2b, t2T_v, x_free3)
            P.wait("tensor", s_v, s_v.v)
            P.wait("tensor", s_s, s_s.v)
            mm_done = {}
            for oc in range(12):
                wpar = oc % 2
                if wfree[wpar] is not None:
                    P.wait("gpsimd", wfree[wpar][0], wfree[wpar][1])
                P.dma("gpsimd", wbuf3[wpar][:, :],
                      wqkv_d[:, :, oc * 128:(oc + 1) * 128].transpose(
                          [1, 0, 2]), sw[wpar])
                wload = sw[wpar].v
                pr = 2 + oc % 2
                P.wait("tensor", sw[wpar], wload)
                if oc >= 2:
                    P.wait("tensor", mm_done[oc - 2][0], mm_done[oc - 2][1])
                for d in range(32):
                    last = (d == 31)
                    for q2 in range(2):
                        P.op("tensor", "matmul",
                             ppf(pr)[:, q2 * 512:(q2 + 1) * 512],
                             wbuf3[wpar][:, d * 128:(d + 1) * 128],
                             t2T_v[:, d, q2 * 512:(q2 + 1) * 512],
                             start=(d == 0), stop=last,
                             inc=(s_t if last and q2 == 1 else None))
                wfree[wpar] = (s_t, s_t.v)
                P.wait("scalar", s_t, s_t.v)
                P.op("scalar", "activation",
                     qkv_v[:, oc, h * 1024:(h + 1) * 1024], ppf(pr)[:, :],
                     AF.Identity, bias=bqkv[:, oc:oc + 1], scale=1.0,
                     inc=s_s)
                mm_done[oc] = (s_s, s_s.v)
        qkv_done = s_s.v
        A.release()

        A.mark()
        PT = A.bf16(128, 16 * 2048)
        kaug = A.bf16(128, 2048)
        qaug = A.bf16(128, 2048)
        rowbuf = A.f32(1, 2048)
        rowbuf2 = A.f32(1, 2048)
        rb_s = A.bf16(64, 2048)
        mq16 = A.f32(128, 16)
        m4 = A.f32(128, 4)
        m16n = A.f32(128, 16)
        mqTb = A.bf16(16, 128)

        P.wait("vector", s_s, qkv_done)
        P.op("vector", "memset", vtm[:, :], 1.0, inc=s_v)
        P.op("vector", "memset", kaug[64:65, :], 1.0, inc=s_v)
        for h8 in range(8):
            vch = 8 + h8 // 2
            rb = (h8 % 2) * 64
            P.wait("tensor", s_v, s_v.v)
            for kt2 in range(16):
                P.op("tensor", "transpose",
                     ppb(0)[:, kt2 * 64:(kt2 + 1) * 64],
                     qkv_v[rb:rb + 64, vch, kt2 * 128:(kt2 + 1) * 128],
                     identb[rb:rb + 64, rb:rb + 64],
                     inc=(s_t if kt2 == 15 else None))
            P.wait("vector", s_t, s_t.v)
            P.op("vector", "tensor_copy", vtm_v[:, h8, :, 0:64],
                 ppb(0)[:, 0:1024].rearrange("p (k c) -> p k c", k=16),
                 inc=s_v)
        P.wait("tensor", s_v, s_v.v)

        for h8 in range(8):
            qch = h8 // 2
            kch = 4 + h8 // 2
            rb = (h8 % 2) * 64
            P.wait("vector", s_t, s_t.v)
            P.op("vector", "tensor_copy", kaug[0:64, :],
                 qkv_v[rb:rb + 64, kch, :])
            P.chain("vector", s_v)
            P.op("vector", "tensor_copy", qaug[0:64, :],
                 qkv_v[rb:rb + 64, qch, :], inc=s_v)
            kq_staged = s_v.v
            P.wait("tensor", s_v, kq_staged)
            qtred = {}
            for qt in range(16):
                pr0 = (qt % 2) * 2
                if qt >= 2:
                    P.wait("tensor", qtred[qt - 2][0], qtred[qt - 2][1])
                for c in range(4):
                    P.op("tensor", "matmul",
                         ppf(pr0 + c // 2)[:, (c % 2) * 512:(c % 2 + 1) * 512],
                         qaug[0:64, qt * 128:(qt + 1) * 128],
                         kaug[0:64, c * 512:(c + 1) * 512],
                         start=True, stop=True,
                         inc=(s_t if c == 3 else None))
                P.wait("vector", s_t, s_t.v)
                for c in range(4):
                    P.op("vector", "tensor_reduce", m4[:, c:c + 1],
                         ppf(pr0 + c // 2)[:, (c % 2) * 512:(c % 2 + 1) * 512],
                         AX.X, ALU.max)
                P.chain("vector", s_v)
                P.op("vector", "tensor_reduce", mq16[:, qt:qt + 1], m4[:, :],
                     AX.X, ALU.max, inc=s_v)
                qtred[qt] = (s_v, s_v.v)
            P.op("vector", "tensor_scalar_mul", m16n[:, :], mq16[:, :], -1.0,
                 inc=s_v)
            P.wait("tensor", s_v, s_v.v)
            P.op("tensor", "transpose", ppf(0)[0:16, 0:128], m16n[:, :],
                 identf[:, :], inc=s_t)
            P.wait("vector", s_t, s_t.v)
            P.op("vector", "tensor_copy", mqTb[:, :], ppf(0)[0:16, 0:128],
                 inc=s_v)
            P.wait("sync", s_v, s_v.v)
            P.dma("sync", qaug[64:65, :], mqTb[0:16, :], sm)
            negm_done = sm.v
            P.wait("tensor", sm, negm_done)
            exp_done = {}
            for kt2 in range(16):
                for reg in range(2):
                    ri = kt2 * 2 + reg
                    pr = ri % 2
                    if ri >= 2:
                        P.wait("tensor", exp_done[ri - 2][0],
                               exp_done[ri - 2][1])
                    for q2 in range(2):
                        P.op("tensor", "matmul",
                             ppf(pr)[:, q2 * 512:(q2 + 1) * 512],
                             kaug[0:65, kt2 * 128:(kt2 + 1) * 128],
                             qaug[0:65, reg * 1024 + q2 * 512:
                                  reg * 1024 + (q2 + 1) * 512],
                             start=True, stop=True,
                             inc=(s_t if q2 == 1 else None))
                    P.wait("scalar", s_t, s_t.v)
                    P.op("scalar", "activation",
                         PT[:, kt2 * 2048 + reg * 1024:
                            kt2 * 2048 + (reg + 1) * 1024],
                         ppf(pr)[:, :], AF.Exp,
                         bias=zcol[:, 0:1], inc=s_s)
                    exp_done[ri] = (s_s, s_s.v)
                P.wait("tensor", exp_done[kt2 * 2 + 1][0],
                       exp_done[kt2 * 2 + 1][1])
                for qs in range(4):
                    P.op("tensor", "matmul",
                         ppf(2 + qs // 2)[0:65, (qs % 2) * 512:
                                          (qs % 2 + 1) * 512],
                         vtm_v[:, h8, kt2, :],
                         PT[:, kt2 * 2048 + qs * 512:
                            kt2 * 2048 + (qs + 1) * 512],
                         start=(kt2 == 0), stop=(kt2 == 15),
                         inc=(s_t if kt2 == 15 and qs == 3 else None))
            P.wait("vector", s_t, s_t.v)
            P.op("vector", "reciprocal", rowbuf[0:1, 0:1024],
                 ppf(2)[64:65, :])
            P.op("vector", "reciprocal", rowbuf[0:1, 1024:2048],
                 ppf(3)[64:65, :])
            P.chain("vector", s_v)
            P.op("vector", "tensor_scalar_mul", rowbuf2[0:1, :],
                 rowbuf[0:1, :], 1.0 / math.sqrt(512.0), inc=s_v)
            P.wait("tensor", s_v, s_v.v)
            for q2 in range(4):
                P.op("tensor", "matmul",
                     ppf(q2 // 2)[0:64, (q2 % 2) * 512:(q2 % 2 + 1) * 512],
                     onesrow[0:1, 0:64],
                     rowbuf2[0:1, q2 * 512:(q2 + 1) * 512],
                     start=True, stop=True,
                     inc=(s_t if q2 == 3 else None))
            P.wait("vector", s_t, s_t.v)
            P.op("vector", "tensor_copy", rb_s[:, 0:1024], ppf(0)[0:64, :])
            P.op("vector", "tensor_copy", rb_s[:, 1024:2048],
                 ppf(1)[0:64, :])
            P.chain("vector", s_v)
            P.op("vector", "tensor_tensor", attn_v[rb:rb + 64, h8 // 2,
                                                   0:1024],
                 ppf(2)[0:64, :], rb_s[0:64, 0:1024], ALU.mult)
            P.op("vector", "tensor_tensor", attn_v[rb:rb + 64, h8 // 2,
                                                   1024:2048],
                 ppf(3)[0:64, :], rb_s[0:64, 1024:2048], ALU.mult, inc=s_v)
            P.wait("tensor", s_v, s_v.v)
        A.release()

        A.mark()
        woT = A.bf16(128, 4 * D)
        woT_v = woT.rearrange("p (c o) -> p c o", c=4)
        wobs = A.f32(1, D)
        f_att = A.bf16(128, D)
        tmpb3 = A.bf16(128, D)

        for qeng in ("gpsimd", "sync"):
            P.wait(qeng, s_v, s_v.v)
            P.wait(qeng, s_t, s_t.v)
            P.wait(qeng, s_s, s_s.v)
        P.dma("gpsimd", woT[:, :], wo_d[:, :, :].transpose([1, 0, 2]), s_wb)
        wo_loaded = s_wb.v
        P.dma("sync", wobs[:, :], wob_d[:, :], sm)
        P.wait("tensor", sm, sm.v)
        P.wait("tensor", s_v, s_v.v)
        P.wait("tensor", s_wb, wo_loaded)
        for i in range(NT):
            for os in range(8):
                P.op("tensor", "matmul",
                     ppf(os // 2)[:, (os % 2) * 512:(os % 2 + 1) * 512],
                     onesrow[0:1, 0:128], wobs[0:1, os * 512:(os + 1) * 512],
                     start=True, stop=False)
            for ec in range(4):
                for os in range(8):
                    P.op("tensor", "matmul",
                         ppf(os // 2)[:, (os % 2) * 512:(os % 2 + 1) * 512],
                         attn_v[:, ec, i * 128:(i + 1) * 128],
                         woT_v[:, ec, os * 512:(os + 1) * 512],
                         start=False, stop=(ec == 3),
                         inc=(s_t if ec == 3 and os == 7 else None))
            P.wait("scalar", s_t, s_t.v)
            P.wait("scalar", s_v, s_v.v)
            for pq in range(4):
                P.op("scalar", "activation",
                     f_att[:, pq * 1024:(pq + 1) * 1024], ppf(pq)[:, :],
                     AF.Identity, bias=zcol[:, 0:1], scale=1.0, inc=s_s)
            emit_boundary_tile(i, f_att, tmpb3, x2s_d, x3s_d, xl3, xo3,
                               x_free3, o_free3)
            P.wait("tensor", s_s, s_s.v)
        A.release()
        A.release()
        emit_barrier()
        if upto < 4:
            return finish()

        A.mark()
        qb = [A.i32(128, D), A.i32(128, D)]
        qf = A.bf16(128, 2048)
        mb = [A.bf16(128, D), A.bf16(128, D)]
        deq = [A.bf16(128, D), A.bf16(128, D)]
        wd = [A.bf16(128, D), A.bf16(128, D)]
        stg = [A.bf16(128, 2048), A.bf16(128, 2048)]
        scz = [A.f32(128, 65), A.f32(128, 65)]
        a1 = [A.f32(128, 32), A.f32(128, 32)]
        a2n = [A.f32(128, 32), A.f32(128, 32)]

        qfree_s = [None, None]
        qfree_v = [None, None]
        mfree = [None, None]
        sczfree = [None, None]
        a1free = [None, None]
        dfree = [None, None]
        wfree4 = [None, None]
        stgfree = [None, None]
        p4ld = {}

        def p4_load(kc):
            par = kc % 2
            if qfree_s[par] is not None:
                P.wait("sync", qfree_s[par][0], qfree_s[par][1])
            if qfree_v[par] is not None:
                P.wait("sync", qfree_v[par][0], qfree_v[par][1])
            P.dma("sync", qb[par][:, :], wq1_d[kc, :, :], sx[par])
            if sczfree[par] is not None:
                P.wait("sync", sczfree[par][0], sczfree[par][1])
            P.dma("sync", scz[par][:, 0:32], scales_d[kc, :, :], sm)
            P.dma("sync", scz[par][:, 32:64], zeros_d[kc, :, :], sm)
            P.dma("sync", scz[par][:, 64:65], scale2_d[kc, :, :], sm)
            if mfree[par] is not None:
                P.wait("gpsimd", mfree[par][0], mfree[par][1])
            P.dma("gpsimd", mb[par][:, :], mask1_d[kc, :, :], sw[par])
            p4ld[kc] = (sx[par].v, sm.v, sw[par].v)

        p4_load(0)
        p4_load(1)
        for kc in range(32):
            par = kc % 2
            qv, szv, swv = p4ld[kc]
            P.wait("vector", sm, szv)
            if a1free[par] is not None:
                P.wait("vector", a1free[par][0], a1free[par][1])
            P.op("vector", "tensor_scalar_mul", a1[par][:, :],
                 scz[par][:, 0:32], scz[par][:, 64:65])
            P.chain("vector", s_v)
            P.op("vector", "scalar_tensor_tensor", a2n[par][:, :],
                 scz[par][:, 32:64], -1.0, a1[par][:, :], ALU.mult, ALU.mult,
                 inc=s_v)
            a1_done = s_v.v
            sczfree[par] = (s_v, a1_done)
            P.wait("scalar", s_v, a1_done)
            P.wait("scalar", sx[par], qv)
            if dfree[par] is not None:
                P.wait("scalar", dfree[par][0], dfree[par][1])
            for g in range(16):
                P.op("scalar", "activation",
                     deq[par][:, g * 128:(g + 1) * 128],
                     qb[par][:, g * 128:(g + 1) * 128], AF.Identity,
                     scale=a1[par][:, g:g + 1], bias=a2n[par][:, g:g + 1],
                     inc=(s_s if g == 15 else None))
            sdeq_done = s_s.v
            qfree_s[par] = (s_s, sdeq_done)
            a1free[par] = (s_s, sdeq_done)
            P.wait("vector", sx[par], qv)
            P.op("vector", "tensor_copy", qf[:, :], qb[par][:, 2048:4096],
                 inc=s_v)
            qfree_v[par] = (s_v, s_v.v)
            for g in range(16):
                P.op("vector", "tensor_scalar",
                     deq[par][:, 2048 + g * 128:2048 + (g + 1) * 128],
                     qf[:, g * 128:(g + 1) * 128],
                     a1[par][:, 16 + g:17 + g], a2n[par][:, 16 + g:17 + g],
                     ALU.mult, ALU.add,
                     inc=(s_v if g == 15 else None))
            P.wait("vector", s_s, sdeq_done)
            P.wait("vector", sw[par], swv)
            if wfree4[par] is not None:
                P.wait("vector", wfree4[par][0], wfree4[par][1])
            P.op("vector", "tensor_tensor", wd[par][:, :], deq[par][:, :],
                 mb[par][:, :], ALU.mult, inc=s_v)
            wd_done = s_v.v
            dfree[par] = (s_v, wd_done)
            mfree[par] = (s_v, wd_done)
            if kc + 2 < 32:
                p4_load(kc + 2)

            P.wait("tensor", s_v, wd_done)
            for half in range(2):
                for j in range(16):
                    ncc = half * 16 + j
                    P.op("tensor", "transpose",
                         ppb(half)[:, j * 128:(j + 1) * 128],
                         wd[par][:, ncc * 128:(ncc + 1) * 128], identb[:, :],
                         inc=(s_t if j == 15 else None))
                P.wait("vector", s_t, s_t.v)
                if stgfree[half] is not None:
                    P.wait("vector", stgfree[half][0], stgfree[half][1])
                P.op("vector", "tensor_copy", stg[half][:, :],
                     ppb(half)[:, :], inc=s_v)
                P.wait("scalar", s_v, s_v.v)
                P.dma("scalar",
                      wdeqT_d[kc, :, half * 16:(half + 1) * 16, :],
                      stg[half][:, :].rearrange("p (a b) -> p a b", a=16),
                      sst[half])
                stgfree[half] = (sst[half], sst[half].v)
            wfree4[par] = (s_t, s_t.v)

        A.release()
        emit_barrier()
        if upto < 5:
            return finish()

        A.mark()
        t3T = A.bf16(128, 32 * 1024)
        t3T_v = t3T.rearrange("p (d t) -> p d t", d=32)
        f1 = A.bf16(128, 32 * 1024)
        f1_v = f1.rearrange("p (d t) -> p d t", d=32)
        wt5 = [A.bf16(128, 4096), A.bf16(128, 4096)]
        shared5 = A.bf16(128, 4096)
        xl5 = [A.bf16(128, D), A.bf16(128, D)]
        xo5 = [A.f32(128, D), A.f32(128, D)]

        x_free5 = [None, None]
        o_free5 = [None, None]
        wfree5 = [None, None]
        for h in range(2):
            emit_prep_half(x3s_d, h, xl5, shared5, t3T_v, x_free5)
            P.wait("tensor", s_v, s_v.v)
            P.wait("tensor", s_s, s_s.v)
            mm_done = {}
            for kc in range(32):
                wpar = kc % 2
                if wfree5[wpar] is not None:
                    P.wait("sync", wfree5[wpar][0], wfree5[wpar][1])
                P.dma("sync", wt5[wpar][:, :], wdeqT_d[kc, :, :, :], sy[wpar])
                wload = sy[wpar].v
                pr = 2 + kc % 2
                P.wait("tensor", sy[wpar], wload)
                if kc >= 2:
                    P.wait("tensor", mm_done[kc - 2][0], mm_done[kc - 2][1])
                for ncc in range(32):
                    last = (ncc == 31)
                    for q2 in range(2):
                        P.op("tensor", "matmul",
                             ppf(pr)[:, q2 * 512:(q2 + 1) * 512],
                             wt5[wpar][:, ncc * 128:(ncc + 1) * 128],
                             t3T_v[:, ncc, q2 * 512:(q2 + 1) * 512],
                             start=(ncc == 0), stop=last,
                             inc=(s_t if last and q2 == 1 else None))
                wfree5[wpar] = (s_t, s_t.v)
                P.wait("scalar", s_t, s_t.v)
                P.op("scalar", "activation", f1_v[:, kc, :], ppf(pr)[:, :],
                     AF.Identity, bias=b1fm[:, kc:kc + 1], scale=1.0,
                     inc=s_s)
                mm_done[kc] = (s_s, s_s.v)
            last_evac = mm_done[31]
            f_tm5 = shared5
            tmpb5 = wt5[0]
            for k in range(8):
                i = h * 8 + k
                emit_fm_to_tm(f1_v, k, f_tm5, last_evac[0], last_evac[1])
                par = i % 2
                if x_free5[par] is not None:
                    P.wait("sync", x_free5[par][0], x_free5[par][1])
                P.dma("sync", xl5[par][:, :],
                      x3s_d[i * 128:(i + 1) * 128, :], sx[par])
                xload = sx[par].v
                P.wait("vector", s_s, s_s.v)
                emit_ln_stats(f_tm5)
                P.op("vector", "tensor_scalar", tmpb5[:, :], f_tm5[:, :],
                     stats[:, MEAN:MEAN + 1], stats[:, RSTD:RSTD + 1],
                     ALU.subtract, ALU.mult)
                P.chain("vector", s_v)
                P.wait("vector", sx[par], xload)
                if o_free5[par] is not None:
                    P.wait("vector", o_free5[par][0], o_free5[par][1])
                P.op("vector", "tensor_add", xo5[par][:, :], xl5[par][:, :],
                     tmpb5[:, :], inc=s_v)
                x_free5[par] = (s_v, s_v.v)
                P.wait("scalar", s_v, s_v.v)
                P.dma("scalar", out_d[i * 128:(i + 1) * 128, :],
                      xo5[par][:, :], sst[par])
                o_free5[par] = (sst[par], sst[par].v)
            wfree5[0] = (s_v, s_v.v)

        A.release()
        return finish()


def _emit_block(nc, P):
    with nc.Block() as block:
        @block.sync
        def _(eng):
            for fn in P.q["sync"]:
                fn(eng)

        @block.scalar
        def _(eng):
            for fn in P.q["scalar"]:
                fn(eng)

        @block.vector
        def _(eng):
            for fn in P.q["vector"]:
                fn(eng)

        @block.tensor
        def _(eng):
            for fn in P.q["tensor"]:
                fn(eng)

        @block.gpsimd
        def _(eng):
            for fn in P.q["gpsimd"]:
                fn(eng)



def _prep_shared(inputs):
    C = _host_consts()
    g = lambda k: np.asarray(inputs[k])
    ac = np.ascontiguousarray

    vc = g("L0_attn_Vc_ri")
    wbig = np.concatenate([g("L0_ffn_Wq"), g("L0_ffn_Wk"),
                           g("L0_ffn_A").reshape(128, D),
                           g("L0_ffn_W")], axis=0)
    wqk = g("L1_attn_Wqk_w").reshape(8, 64, 2, D)
    wqkv = np.concatenate([ac(wqk[:, :, 0, :]).reshape(512, D),
                           ac(wqk[:, :, 1, :]).reshape(512, D),
                           g("L1_attn_Wv_w")], axis=0)
    bqk = g("L1_attn_Wqk_b").reshape(8, 64, 2)
    bqkv = np.concatenate([ac(bqk[:, :, 0]).reshape(512),
                           ac(bqk[:, :, 1]).reshape(512),
                           g("L1_attn_Wv_b")])
    shared = dict(
        lam=ac(g("L0_attn_Lambda_ri")),
        pri=ac(g("L0_attn_p_ri")),
        qri=ac(g("L0_attn_q_ri")),
        ctri=ac(g("L0_attn_Ct_ri").reshape(64, 2)),
        bp=ac(g("L0_attn_Bp")),
        dv=ac(g("L0_attn_D")),
        ls=ac(g("L0_attn_log_step")),
        vcre=ac(vc[:, :, 0]),
        vcim=ac(vc[:, :, 1]),
        wbig0T=ac(wbig.T).reshape(32, 128, 4736),
        b0fm=ac(g("L0_ffn_b").reshape(32, 128).T),
        bmflat=ac(g("L0_ffn_Bm").transpose(0, 2, 1).reshape(128, D)),
        wqkvT=ac(wqkv.T).reshape(32, 128, 1536),
        bqkv=ac(bqkv.reshape(12, 128).T),
        wo1T=ac(g("L1_attn_Wo_w").T).reshape(4, 128, D),
        wob=ac(g("L1_attn_Wo_b").reshape(1, D)),
        wq1=ac(g("L1_ffn_W_q")).reshape(32, 128, D),
        mask1=ac(g("L1_ffn_mask")).reshape(32, 128, D),
        scales1=ac(g("L1_ffn_scales")[:, :, 0]).reshape(32, 128, 32),
        zeros1=ac(g("L1_ffn_zeros")[:, :, 0]).reshape(32, 128, 32),
        scale2=ac(g("L1_ffn_scale2")).reshape(32, 128, 1),
        b1fm=ac(g("L1_ffn_bias").reshape(32, 128).T),
    )
    shared.update(C)
    return shared


def kernel(**inputs):
    shared = _prep_shared(inputs)
    x0 = np.asarray(inputs["L0_attn_u"])
    nc = build(upto=5, dev=False)
    in_maps = []
    for i in range(8):
        m = dict(shared)
        m["x0"] = np.ascontiguousarray(x0[i])
        in_maps.append(m)
    res = run_bass_kernel_spmd(nc, in_maps, core_ids=list(range(8)))
    return np.stack([res.results[i]["out"] for i in range(8)], axis=0)
